# version 09
# baseline (speedup 1.0000x reference)
import math

import numpy as np
import ml_dtypes

import concourse.bass as bass
import concourse.mybir as mybir
from concourse.bass_utils import run_bass_kernel_spmd

F32 = mybir.dt.float32
BF16 = mybir.dt.bfloat16
AF = mybir.ActivationFunctionType
ALU = mybir.AluOpType
AX = mybir.AxisListType.X
BFD = ml_dtypes.bfloat16

B, T_FULL, D = 2, 8192, 2048
H, HD, QC = 16, 128, 512
DFF = 5632
RT, HALO, RND = 2048, 512, 1024
NT, NSRC, NKC, NCH, NW = 8, 12, 16, 5, 44
SG = 6
LNEPS, RMSEPS, QEPS = 1e-5, 1e-6, 1e-8
SCALE = 1.0 / math.sqrt(HD)
NEG = -1e30
N_CORE = 8

ENGINES = ("sync", "scalar", "vector", "tensor", "gpsimd")
EPROG = {"sync": "pS", "scalar": "pA", "vector": "pV", "tensor": "pP",
         "gpsimd": "pG"}

ACOL = 89088
O_GACC, O_GSTG = 0, 1024
O_LN1T = 0
O_HT, O_QEV, O_QRF, O_QT3 = 24576, 24576, 28672, 32768
O_QNB, O_Q1T, O_WQC, O_KSC, O_LNB = 36864, 38912, 43008, 44032, 46080
O_COS, O_SIN = 50176, 50944
O_WQU, O_WK, O_WV, O_KT, O_VN, O_QTN = 57344, 65536, 67584, 69632, 51712, 72704
VST = 132
O_ANAT, O_ATT = 0, 4096
O_MKS, O_PRB, O_CMT, O_CRW = 24576, 25856, 27136, 27264
O_GST2 = 53296
O_X = 24576
O_Y1, O_W1G = 57344, 73728
O_W2QS = (0, 20480, 81920, 4096)
O_LN2T, O_LNB2 = 57344, 73728
O_W3A = 73728
O_W1T, O_W2T, O_FST, O_SGM, O_S1, O_QD, O_HID = (
    0, 4096, 8192, 12288, 14336, 16384, 18432)
O_S1B = 15360
O_S1C = 88064
NQD = 2


class Builder:
    def __init__(self):
        self.nc = bass.Bass("TRN2", target_bir_lowering=False)
        self.ops = {e: [] for e in ENGINES}
        self.semh = {}
        self.semc = {}
        self.ctx = None

    def s(self, name):
        if name not in self.semh:
            self.semh[name] = self.nc.alloc_semaphore(name)
            self.semc[name] = 0
        return name

    def tok(self, eng):
        n = EPROG[eng]
        return (n, self.semc.get(n, 0))

    def op(self, eng, fn, waits=(), dma=None, inc=True):
        wl = []
        for w in waits:
            if w is None:
                continue
            n, v = w
            if v > 0:
                self.s(n)
                wl.append((n, v))
        incs = []
        if dma is not None:
            self.s(dma)
            self.semc[dma] += 16
            incs.append((dma, 16))
            ret = (dma, self.semc[dma])
        elif inc:
            n = self.s(EPROG[eng])
            self.semc[n] += 1
            incs.append((n, 1))
            ret = (n, self.semc[n])
        else:
            ret = None
        self.ops[eng].append((wl, fn, incs))
        return ret

    def emit(self):
        with self.nc.Block() as block:
            for ename in ENGINES:
                def mk(ename):
                    def body(eng):
                        hi = {}
                        for waits, fn, incs in self.ops[ename]:
                            for (n, v) in waits:
                                if hi.get(n, 0) >= v:
                                    continue
                                eng.wait_ge(self.semh[n], v)
                                hi[n] = v
                            ins = fn(eng)
                            for (n, k) in incs:
                                ins.then_inc(self.semh[n], k)
                    return body
                getattr(block, ename)(mk(ename))

    def build(self):
        nc = self.nc
        import contextlib
        self.ctx = contextlib.ExitStack()
        E = self.ctx.enter_context

        self.h_in = nc.dram_tensor("h", [RT + HALO, D], F32,
                                   kind="ExternalInput").ap()
        self.wqc_in = nc.dram_tensor("wq_comp", [D, QC], F32,
                                     kind="ExternalInput").ap()
        self.wqu_in = nc.dram_tensor("wq_up", [QC, D], F32,
                                     kind="ExternalInput").ap()
        self.wk_in = nc.dram_tensor("wk", [D, HD], F32,
                                    kind="ExternalInput").ap()
        self.wv_in = nc.dram_tensor("wv", [D, HD], F32,
                                    kind="ExternalInput").ap()
        self.qnw_ap = nc.dram_tensor("q_norm_w", [HD], F32,
                                     kind="ExternalInput").ap()
        self.knw_ap = nc.dram_tensor("k_norm_w", [HD], F32,
                                     kind="ExternalInput").ap()
        self.sink_ap = nc.dram_tensor("sink", [H], F32,
                                      kind="ExternalInput").ap()
        self.ow1_in = nc.dram_tensor("out_w1", [4, QC, 2048], F32,
                                     kind="ExternalInput").ap()
        self.ow2_in = nc.dram_tensor("out_w2", [4 * 2048, D], F32,
                                     kind="ExternalInput").ap()
        self.fw1_in = nc.dram_tensor("ffn_w1", [DFF, D], F32,
                                     kind="ExternalInput").ap()
        self.fw2_in = nc.dram_tensor("ffn_w2", [DFF, D], F32,
                                     kind="ExternalInput").ap()
        self.fw3_in = nc.dram_tensor("ffn_w3", [D, DFF], F32,
                                     kind="ExternalInput").ap()
        self.cos_in = nc.dram_tensor("costab", [2, NSRC, 128, 32], F32,
                                     kind="ExternalInput").ap()
        self.sin_in = nc.dram_tensor("sintab", [2, NSRC, 128, 32], F32,
                                     kind="ExternalInput").ap()
        self.iden_ap = nc.dram_tensor("iden", [128, 128], BF16,
                                      kind="ExternalInput").ap()
        self.mask_in = nc.dram_tensor("masks", [2, NT, NCH, 128, 128], BF16,
                                      kind="ExternalInput").ap()
        self.out_hbm = nc.dram_tensor("out", [RT, D], F32,
                                      kind="ExternalOutput").ap()
        import os
        self.dbg = None
        if os.environ.get("KDBG"):
            self.dbg = nc.dram_tensor("dbg", [RT, D], F32,
                                      kind="ExternalOutput").ap()
            self.dbg2 = nc.dram_tensor("dbg2", [2, 128, H * RND], BF16,
                                       kind="ExternalOutput").ap()
            self.dbg3 = nc.dram_tensor("dbg3", [2, NT, 128, D], BF16,
                                       kind="ExternalOutput").ap()
            self.dbg4 = nc.dram_tensor("dbg4", [2, 128, NSRC * 128], BF16,
                                       kind="ExternalOutput").ap()
            self.dbg5 = nc.dram_tensor("dbg5", [2, 128, NSRC * 129], BF16,
                                       kind="ExternalOutput").ap()

        self.iden = E(nc.sbuf_tensor("idn", [128, 128], BF16))
        self.ones_r = E(nc.sbuf_tensor("ones_r", [1, 128], BF16))
        self.ones_c = E(nc.sbuf_tensor("ones_c", [128, 1], BF16))
        self.ones_rf = E(nc.sbuf_tensor("ones_rf", [1, 128], F32))
        self.ones_cf = E(nc.sbuf_tensor("ones_cf", [128, 1], F32))
        self.b_m05 = E(nc.sbuf_tensor("b_m05", [128, 1], F32))
        self.b_eps = E(nc.sbuf_tensor("b_eps", [128, 1], F32))
        self.b_p05 = E(nc.sbuf_tensor("b_p05", [128, 1], F32))
        self.sinkb = E(nc.sbuf_tensor("sinkb", [128, H], F32))
        self.nsinkb = E(nc.sbuf_tensor("nsinkb", [128, H], F32))
        self.qnwb = E(nc.sbuf_tensor("qnwb", [128, HD], F32))
        self.knwb = E(nc.sbuf_tensor("knwb", [128, HD], F32))
        self.gacc = E(nc.sbuf_tensor("gacc", [128, 112], F32))
        self.gtot = E(nc.sbuf_tensor("gtot", [1, 4], F32))
        self.invg = E(nc.sbuf_tensor("invg", [128, 4], F32))
        self.cmaxb = E(nc.sbuf_tensor("cmaxb", [128, 2 * H], BF16))
        self.cmaxf = E(nc.sbuf_tensor("cmaxf", [128, 2 * H], F32))
        self.red1 = E(nc.sbuf_tensor("red1", [128, 4], F32))
        self.gred = E(nc.sbuf_tensor("gred", [128, 1], F32))
        self.dstat = E(nc.sbuf_tensor("dstat", [128, 4], F32))
        self.lns = E(nc.sbuf_tensor("lns", [128, 8], F32))
        self.rmsS = E(nc.sbuf_tensor("rmsS", [128, 2 * H], F32))

        self.AR = E(nc.sbuf_tensor("arena", [128, ACOL], BF16))
        self.ARF = self.AR[:].bitcast(F32)
        self.P0 = E(nc.psum_tensor("P0", [128, 2048], F32))
        self.P1 = E(nc.psum_tensor("P1", [128, 1536], F32))
        self.PT = E(nc.psum_tensor("PT", [128, 1024], BF16))

        self.st_final = []
        self.prev = {"d_done": None, "c_done": None, "e_done": None,
                     "ln1_free": None, "store": [None] * NT,
                     "cos_free": None, "wqu_free": None, "wkv_free": None,
                     "qtn_free": None, "att_free": None, "x_free": None,
                     "mask_free": [None, None]}
        self.pt_free = None

        self.consts()
        self.gamma_gen = self.gamma_steps()
        self._gamma_last = None
        self.T_GAMMA = None
        self.gamma_finalize()
        for r in range(2):
            self.round(r)
        self.op("sync", lambda e: e.nop(), waits=self.st_final, inc=False)
        self.emit()
        self.ctx.close()
        return nc

    def bf(self, off, n):
        return self.AR[:, off:off + n]

    def f32(self, off, n):
        assert off % 2 == 0
        return self.ARF[:, off // 2: off // 2 + n]

    def consts(self):
        P = self
        self.t_iden = P.op("sync", lambda e: e.dma_start(
            out=P.iden[:], in_=P.iden_ap), dma="dIDN")
        self.t_sink = P.op("gpsimd", lambda e: e.dma_start(
            out=P.sinkb[:], in_=P.sink_ap
            .unsqueeze(0).partition_broadcast(128).squeeze(1)), dma="dBCS")
        self.t_qnb = P.op("gpsimd", lambda e: e.dma_start(
            out=P.qnwb[:], in_=P.qnw_ap
            .unsqueeze(0).partition_broadcast(128).squeeze(1)), dma="dBCQ")
        t_bc = P.op("gpsimd", lambda e: e.dma_start(
            out=P.knwb[:], in_=P.knw_ap
            .unsqueeze(0).partition_broadcast(128).squeeze(1)), dma="dBCK")
        self.t_bc = t_bc
        P.op("vector", lambda e: e.memset(P.ones_r[:], 1.0))
        P.op("vector", lambda e: e.memset(P.ones_c[:], 1.0))
        P.op("vector", lambda e: e.memset(P.ones_rf[:], 1.0))
        P.op("vector", lambda e: e.memset(P.ones_cf[:], 1.0))
        P.op("vector", lambda e: e.memset(P.b_m05[:], -0.5))
        P.op("vector", lambda e: e.memset(P.b_eps[:], LNEPS))
        t_b = P.op("vector", lambda e: e.memset(P.b_p05[:], 0.5))
        self.t_nsk = P.op("vector", lambda e: e.tensor_scalar_mul(
            out=P.nsinkb[:], in0=P.sinkb[:], scalar1=-1.0),
            waits=[self.t_sink, t_bc, t_b])
        self.t_consts = self.t_nsk

    def pump(self, n):
        for _ in range(n):
            if self.gamma_gen is not None:
                try:
                    next(self.gamma_gen)
                except StopIteration:
                    self.gamma_gen = None

    def gamma_steps(self):
        P = self
        pieces = []
        for wsel, src, ncols in ([(0, P.fw1_in, D)] * 0 +
                                 [(0, P.fw1_in, D)] +
                                 []):
            pass
        chunks = ([(P.fw1_in, i, D, i) for i in range(NW)] +
                  [(P.fw2_in, i, D, NW + i) for i in range(NW)] +
                  [(P.fw3_in, i, DFF, 2 * NW + i) for i in range(NKC)])
        last = [None, None]
        prevr = None
        n = 0
        for (src, i, ncols, col) in chunks:
            npc = (ncols + 2047) // 2048
            for pc in range(npc):
                cw = min(2048, ncols - pc * 2048)
                sl = n % 2
                n += 1
                stg = P.f32(O_GSTG + sl * 4096, 2048)
                t_ld = P.op("sync", lambda e, s2=stg[:, 0:cw],
                            a=src[i * 128:(i + 1) * 128,
                                  pc * 2048:pc * 2048 + cw]:
                            e.dma_start(out=s2, in_=a), dma=f"dG{sl}",
                            waits=[last[sl]])
                dst = (P.gacc[:, col:col + 1] if pc == 0 else P.gred[:])
                t_rd = P.op("vector", lambda e, s2=stg[:, 0:cw], d=dst:
                            e.tensor_reduce(out=d, in_=s2, axis=AX,
                                            op=ALU.add,
                                            apply_absolute_value=True),
                            waits=[t_ld, prevr])
                prevr = t_rd
                last[sl] = t_rd
                if pc > 0:
                    prevr = P.op("vector", lambda e, c=col: e.tensor_tensor(
                        out=P.gacc[:, c:c + 1], in0=P.gacc[:, c:c + 1],
                        in1=P.gred[:], op=ALU.add), waits=[prevr])
                    last[sl] = prevr
                self._gamma_last = prevr
                yield

    def gamma_finalize(self):
        P = self
        while self.gamma_gen is not None:
            self.pump(8)
        t_g = self._gamma_last
        t_pe_prev = None
        for wi, (c0, cn, nel) in enumerate([(0, NW, DFF * D),
                                            (NW, NW, DFF * D),
                                            (2 * NW, NKC, D * DFF)]):
            t1 = P.op("vector", lambda e, c0=c0, cn=cn: e.reduce_sum(
                out=P.red1[:, 1:2], in_=P.gacc[:, c0:c0 + cn], axis=AX),
                waits=[t_g])
            t2 = P.op("tensor", lambda e: e.matmul(
                P.P1[0:1, 0:1], P.red1[:, 1:2], P.ones_cf[:],
                start=True, stop=True), waits=[t1, self.t_consts, t_pe_prev])
            t3 = P.op("vector", lambda e, wi=wi: e.tensor_copy(
                out=P.gtot[0:1, wi:wi + 1], in_=P.P1[0:1, 0:1]),
                waits=[t2])
            t4 = P.op("tensor", lambda e, wi=wi: e.matmul(
                P.P1[:, 0:1], P.ones_rf[:], P.gtot[0:1, wi:wi + 1],
                start=True, stop=True), waits=[t3])
            t_pe_prev = t4
            t5 = P.op("vector", lambda e, wi=wi, nel=nel: e.tensor_scalar(
                out=P.invg[:, wi:wi + 1], in0=P.P1[:, 0:1],
                scalar1=1.0 / nel, scalar2=QEPS, op0=ALU.mult, op1=ALU.add),
                waits=[t4])
            t_g = P.op("vector", lambda e, wi=wi: e.reciprocal(
                out=P.invg[:, wi:wi + 1], in_=P.invg[:, wi:wi + 1]),
                waits=[t5])
        t_g = P.op("vector", lambda e: e.tensor_scalar_mul(
            out=P.invg[:, 3:4], in0=P.invg[:, 1:2], scalar1=-1.0),
            waits=[t_g])
        self.T_GAMMA = t_g
        self.t_gamma_pe = t_pe_prev

    def ln_stats(self, xv, t_in, sq_off, extra_act_w=()):
        P = self
        t_s1 = P.op("vector", lambda e: e.reduce_sum(
            out=P.lns[:, 0:1], in_=xv, axis=AX), waits=[t_in])
        t_s2 = P.op("scalar", lambda e: e.activation(
            out=P.f32(sq_off, 1024), in_=xv[:, 0:1024], func=AF.Square,
            accum_out=P.lns[:, 1:2]), waits=[t_in] + list(extra_act_w))
        t_s2b = P.op("scalar", lambda e: e.activation(
            out=P.f32(sq_off, 1024), in_=xv[:, 1024:2048], func=AF.Square,
            accum_out=P.lns[:, 7:8]), waits=[t_s2])
        t_s3 = P.op("vector", lambda e: e.tensor_scalar_mul(
            out=P.lns[:, 2:3], in0=P.lns[:, 0:1], scalar1=1.0 / D),
            waits=[t_s1])
        t_s4 = P.op("vector", lambda e: e.tensor_tensor(
            out=P.lns[:, 3:4], in0=P.lns[:, 2:3], in1=P.lns[:, 2:3],
            op=ALU.mult), waits=[t_s3])
        t_s4b = P.op("vector", lambda e: e.tensor_tensor(
            out=P.lns[:, 1:2], in0=P.lns[:, 1:2], in1=P.lns[:, 7:8],
            op=ALU.add), waits=[t_s2b, t_s4])
        t_s5 = P.op("vector", lambda e: e.tensor_scalar(
            out=P.lns[:, 4:5], in0=P.lns[:, 1:2], scalar1=1.0 / D,
            scalar2=None, op0=ALU.mult), waits=[t_s4b])
        t_s6 = P.op("vector", lambda e: e.tensor_tensor(
            out=P.lns[:, 4:5], in0=P.lns[:, 4:5], in1=P.lns[:, 3:4],
            op=ALU.subtract), waits=[t_s5])
        t_s7 = P.op("scalar", lambda e: e.activation(
            out=P.lns[:, 5:6], in_=P.lns[:, 4:5], func=AF.Sqrt,
            bias=P.b_eps[:], scale=1.0), waits=[t_s6, self.t_consts]
            if t_s6 else [t_s6])
        t_s8 = P.op("vector", lambda e: e.reciprocal(
            out=P.lns[:, 6:7], in_=P.lns[:, 5:6]), waits=[t_s7])
        return t_s8

    def pt_batch(self, src_fn, dst_fn, t_in_list, nblk=8):
        P = self
        t_tr = None
        for c in range(nblk):
            t_tr = P.op("tensor",
                        lambda e, c=c: e.transpose(
                            P.PT[:, c * 128:(c + 1) * 128], src_fn(c),
                            P.iden[:]),
                        waits=(t_in_list + [self.pt_free, self.t_iden])
                        if c == 0 else [], inc=(c == nblk - 1))
        t_ev = P.op("scalar", lambda e: e.activation(
            out=dst_fn(), in_=P.PT[:, 0:nblk * 128]
            .rearrange("p (c q) -> p c q", q=128), func=AF.Copy),
            waits=[t_tr])
        self.pt_free = t_ev
        return t_ev

    def ph_a(self, r):
        P = self
        t_cos = P.op("sync", lambda e, r=r: e.dma_start(
            out=P.f32(O_COS, 12 * 32).rearrange("p (i q) -> p i q", q=32),
            in_=P.cos_in[r].rearrange("i p q -> p i q")), dma="dCOS",
            waits=[self.prev["cos_free"]] +
            (list(self.prev["store"]) if r == 1 else []))
        self.t_cos = P.op("sync", lambda e, r=r: e.dma_start(
            out=P.f32(O_SIN, 12 * 32).rearrange("p (i q) -> p i q", q=32),
            in_=P.sin_in[r].rearrange("i p q -> p i q")), dma="dCOS")
        ht_free = [None, None]
        t_lnT = None
        for i in range(NSRC):
            sl = i % 2
            htv = P.f32(O_HT + sl * 4096, 2048)
            w = [ht_free[sl]]
            if r == 0 and i < 2:
                w.append(self.T_GAMMA)
            if r == 1 and i < 2:
                w.extend(self.prev["store"])
            t_h = P.op("sync", lambda e, htv=htv, i=i, r=r: e.dma_start(
                out=htv, in_=P.h_in[r * RND + i * 128:
                                    r * RND + (i + 1) * 128, :]),
                dma=f"dH{sl}", waits=w)
            self.pump(5)
            t_st = self.ln_stats(htv, t_h, O_QT3,
                                 extra_act_w=list(self.prev["store"])
                                 if (r == 1 and i == 0) else [])
            lnt = P.bf(O_LNB + sl * 2048, 2048)
            t_ln = P.op("vector", lambda e, htv=htv, lnt=lnt: e.tensor_scalar(
                out=lnt, in0=htv, scalar1=P.lns[:, 2:3],
                scalar2=P.lns[:, 6:7], op0=ALU.subtract, op1=ALU.mult),
                waits=[t_st])
            ht_free[sl] = t_ln
            for hb in range(2):
                t_lnT = self.pt_batch(
                    lambda c, lnt=lnt, hb=hb: lnt[:, (hb * 8 + c) * 128:
                                                  (hb * 8 + c + 1) * 128],
                    lambda hb=hb, i=i: P.bf(O_LN1T, NKC * NSRC * 128)
                    .rearrange("p (k t q) -> p k t q", k=NKC, t=NSRC)
                    [:, hb * 8:(hb + 1) * 8, i, :],
                    [t_ln, self.prev["ln1_free"] if i == 0 and hb == 0
                     else None])
        self.T_LN1 = t_lnT

    def ph_b(self, r):
        P = self
        t_wk = P.op("gpsimd", lambda e: e.dma_start(
            out=P.bf(O_WK, 2048).rearrange("p (c q) -> p c q", q=128),
            in_=P.wk_in.rearrange("(c p) o -> p c o", p=128)), dma="dWK",
            waits=[self.prev["wkv_free"]])
        t_wv = P.op("gpsimd", lambda e: e.dma_start(
            out=P.bf(O_WV, 2048).rearrange("p (c q) -> p c q", q=128),
            in_=P.wv_in.rearrange("(c p) o -> p c o", p=128)), dma="dWV",
            waits=[self.prev["wkv_free"]])
        kv_free = [None, None]
        kT_done = getattr(self, "dbg_kv", [None])[0]
        t_v = None
        t_vone = P.op("vector", lambda e: e.memset(
            P.bf(O_VN, NSRC * VST).rearrange("p (i q) -> p i q", q=VST)
            [:, :, 128:129], 1.0),
            waits=[self.prev["att_free"],
                   getattr(self, "dbg_kv", [None, None])[1]])
        self.t_vone = t_vone
        for i in range(NSRC):
            self.pump(5)
            for vsel in (0, 1):
                pc = i % 2
                pcol = 512 if pc == 0 else 1024
                t_mm = None
                for kc in range(NKC):
                    t_mm = P.op("tensor", lambda e, i=i, kc=kc, pcol=pcol,
                                woff=(O_WV if vsel else O_WK): e.matmul(
                                    P.P1[:, pcol:pcol + 128],
                                    P.bf(O_LN1T + (kc * NSRC + i) * 128, 128),
                                    P.bf(woff + kc * 128, 128),
                                    start=(kc == 0), stop=(kc == NKC - 1)),
                                waits=[self.T_LN1, t_wv if vsel else t_wk,
                                       kv_free[pc]] if kc == 0 else [],
                                inc=(kc == NKC - 1))
                if vsel:
                    t_v = P.op("vector", lambda e, i=i, pcol=pcol:
                               e.tensor_copy(out=P.bf(O_VN + i * VST, 128),
                                             in_=P.P1[:, pcol:pcol + 128]),
                               waits=[t_mm, self.prev["att_free"]
                                      if i == 0 else None,
                                      getattr(self, "dbg_kv",
                                              [None, None])[1]
                                      if i == 0 else None])
                    kv_free[pc] = t_v
                else:
                    kev = P.f32(O_KSC, 128)
                    t_e = P.op("vector", lambda e, kev=kev, pcol=pcol:
                               e.tensor_copy(out=kev,
                                             in_=P.P1[:, pcol:pcol + 128]),
                               waits=[t_mm, self.prev["att_free"]
                                      if i == 0 else None])
                    kv_free[pc] = t_e
                    krf = P.f32(O_KSC + 512, 128)
                    t_r = self.rope128(kev, krf, O_KSC + 1024, i, t_e)
                    t_q1 = P.op("scalar", lambda e, krf=krf: e.activation(
                        out=P.f32(O_KSC + 1536, 128), in_=krf, func=AF.Square,
                        accum_out=P.red1[:, 2:3]), waits=[t_r])
                    t_q2 = P.op("vector", lambda e: e.tensor_scalar(
                        out=P.red1[:, 3:4], in0=P.red1[:, 2:3],
                        scalar1=1.0 / HD, scalar2=RMSEPS, op0=ALU.mult,
                        op1=ALU.add), waits=[t_q1])
                    t_q3 = P.op("scalar", lambda e: e.activation(
                        out=P.red1[:, 2:3], in_=P.red1[:, 3:4], func=AF.Sqrt),
                        waits=[t_q2])
                    t_q4 = P.op("vector", lambda e: e.reciprocal(
                        out=P.red1[:, 3:4], in_=P.red1[:, 2:3]), waits=[t_q3])
                    knb = P.bf(O_KSC + 3584, 128)
                    t_q5 = P.op("vector", lambda e, krf=krf, knb=knb:
                                e.scalar_tensor_tensor(
                                    out=knb, in0=krf, scalar=P.red1[:, 3:4],
                                    in1=P.knwb[:], op0=ALU.mult,
                                    op1=ALU.mult), waits=[t_q4, self.t_nsk])
                    t_q6 = P.op("tensor", lambda e, knb=knb: e.transpose(
                        P.PT[:, 896:1024], knb, P.iden[:]),
                        waits=[t_q5, kT_done or self.pt_free, self.t_iden])
                    kT_done = P.op("scalar", lambda e, i=i: e.activation(
                        out=P.bf(O_KT + i * 128, 128), in_=P.PT[:, 896:1024],
                        func=AF.Copy), waits=[t_q6])
        self.pt_free = kT_done
        self.T_KT = kT_done
        self.T_V = t_v

    def rope128(self, src, dst, scr, i, t_in):
        P = self
        c = P.f32(O_COS + i * 64, 32)
        s = P.f32(O_SIN + i * 64, 32)
        t1 = P.f32(scr, 32)
        t2 = P.f32(scr + 64, 32)
        ta = P.op("vector", lambda e: e.tensor_tensor(
            out=t1, in0=src[:, 0:32], in1=c, op=ALU.mult),
            waits=[t_in, self.t_cos])
        tb = P.op("vector", lambda e: e.tensor_tensor(
            out=t2, in0=src[:, 32:64], in1=s, op=ALU.mult), waits=[ta])
        tc = P.op("vector", lambda e: e.tensor_tensor(
            out=dst[:, 0:32], in0=t1, in1=t2, op=ALU.subtract), waits=[tb])
        td = P.op("vector", lambda e: e.tensor_tensor(
            out=t1, in0=src[:, 0:32], in1=s, op=ALU.mult), waits=[tc])
        te = P.op("vector", lambda e: e.tensor_tensor(
            out=t2, in0=src[:, 32:64], in1=c, op=ALU.mult), waits=[td])
        tf = P.op("vector", lambda e: e.tensor_tensor(
            out=dst[:, 32:64], in0=t1, in1=t2, op=ALU.add), waits=[te])
        tg = P.op("vector", lambda e: e.tensor_copy(
            out=dst[:, 64:128], in_=src[:, 64:128]), waits=[tf])
        return tg

    def ph_c(self, r):
        P = self
        t_wqu = P.op("gpsimd", lambda e: e.dma_start(
            out=P.bf(O_WQU, 8192).rearrange("p (c q) -> p c q", q=2048),
            in_=P.wqu_in.rearrange("(c p) o -> p c o", p=128)), dma="dWQU",
            waits=[self.prev["wqu_free"]])
        wqc_free = [None, None]
        t_q1T = None
        q1_pe = None
        for th in range(2):
            kc_tok = {}
            for kc in range(NKC):
                sl = kc % 2
                t_w = P.op("gpsimd", lambda e, kc=kc, sl=sl: e.dma_start(
                    out=P.bf(O_WQC + sl * 512, 512),
                    in_=P.wqc_in[kc * 128:(kc + 1) * 128, :]), dma=f"dQC{sl}",
                    waits=[wqc_free[sl],
                           self.T_LN1 if (th == 0 and kc < 2) else None])
                for oc in range(4):
                    w = []
                    if oc == 0:
                        w = [t_w, self.T_LN1]
                        if th == 1 and kc == 0:
                            w.append(t_q1T)
                        if th == 0 and kc == 0:
                            w.append(self.T_GAMMA)
                            w.append(self.tok("vector"))
                    tt = P.op("tensor", lambda e, th=th, kc=kc, oc=oc, sl=sl:
                              e.matmul(
                                  P.P0[:, oc * 512:(oc + 1) * 512],
                                  P.bf(O_WQC + sl * 512 + oc * 128, 128),
                                  P.bf(O_LN1T + (kc * NSRC + 4) * 128
                                       + th * 512, 512),
                                  start=(kc == 0), stop=(kc == NKC - 1)),
                              waits=w, inc=(oc == 3))
                    if oc == 3:
                        kc_tok[kc] = tt
                        wqc_free[sl] = tt
                        if kc == NKC - 1:
                            q1_pe = tt
            for oc in range(4):
                t_q1T = P.op("scalar", lambda e, th=th, oc=oc: e.activation(
                    out=P.bf(O_Q1T + oc * 1024 + th * 512, 512),
                    in_=P.P0[:, oc * 512:(oc + 1) * 512], func=AF.Copy),
                    waits=[q1_pe])
        T_Q1 = t_q1T
        qps = [(self.P0, 0), (self.P0, 1024), (self.P1, 0)]
        q_ev = [None, None, self.tok("vector")]
        rope_done = None
        t_qTn = None
        for j in range(NT):
            self.pump(10)
            evs = []
            for dh in range(2):
                s3 = (2 * j + dh) % 3
                ps, pco = qps[s3]
                t_m = None
                for kc in range(4):
                    for nn in range(2):
                        t_m = P.op("tensor", lambda e, j=j, dh=dh, kc=kc,
                                   nn=nn, ps=ps, pco=pco: e.matmul(
                                       ps[:, pco + nn * 512:
                                          pco + (nn + 1) * 512],
                                       P.bf(O_Q1T + kc * 1024 + j * 128, 128),
                                       P.bf(O_WQU + kc * 2048 + dh * 1024
                                            + nn * 512, 512),
                                       start=(kc == 0), stop=(kc == 3)),
                                   waits=[T_Q1, t_wqu, q_ev[s3]]
                                   if (kc == 0 and nn == 0) else [],
                                   inc=(kc == 3 and nn == 1))
                t_e = P.op("vector", lambda e, dh=dh, ps=ps, pco=pco:
                           e.tensor_copy(out=P.f32(O_QEV + dh * 2048, 1024),
                                         in_=ps[:, pco:pco + 1024]),
                           waits=[t_m, rope_done])
                q_ev[s3] = t_e
                evs.append(t_e)
            rope_done = self.rope_norm_q(j, evs)
            if self.dbg is not None:
                tq = P.op("sync", lambda e, j=j, r=r: e.dma_start(
                    out=self.dbg3[r, j], in_=P.bf(O_QNB, 2048)),
                    dma="dDQ", waits=[rope_done])
                self.st_final.append(tq)
                rope_done = tq
            for hb in range(2):
                t_qTn = self.pt_batch(
                    lambda c, hb=hb: P.bf(O_QNB + (hb * 8 + c) * 128, 128),
                    lambda hb=hb, j=j: P.bf(O_QTN, NT * H * 128)
                    .rearrange("p (t k q) -> p t k q", t=NT, k=H)
                    [:, j, hb * 8:(hb + 1) * 8, :],
                    [rope_done, self.prev["qtn_free"]
                     if (j == 0 and hb == 0) else None])
        self.T_QTN = t_qTn

    def rope_norm_q(self, j, evs):
        P = self
        qev3 = P.f32(O_QEV, 2048).rearrange("p (k q) -> p k q", q=128)
        qrf3 = P.f32(O_QRF, 2048).rearrange("p (k q) -> p k q", q=128)
        ci = 4 + j
        c = (P.f32(O_COS + ci * 64, 32).unsqueeze(1)
             .broadcast_to([128, H, 32]))
        s = (P.f32(O_SIN + ci * 64, 32).unsqueeze(1)
             .broadcast_to([128, H, 32]))
        t1 = P.f32(O_QT3, 512).rearrange("p (k q) -> p k q", q=32)
        t2 = P.f32(O_QT3 + 1024, 512).rearrange("p (k q) -> p k q", q=32)
        x1 = qev3[:, :, 0:32]
        x2 = qev3[:, :, 32:64]
        w0 = list(evs) + [self.t_cos]
        ta = P.op("vector", lambda e: e.tensor_tensor(out=t1, in0=x1, in1=c,
                                                      op=ALU.mult), waits=w0)
        tb = P.op("vector", lambda e: e.tensor_tensor(out=t2, in0=x2, in1=s,
                                                      op=ALU.mult), waits=[ta])
        tc = P.op("vector", lambda e: e.tensor_tensor(
            out=qrf3[:, :, 0:32], in0=t1, in1=t2, op=ALU.subtract),
            waits=[tb])
        td = P.op("vector", lambda e: e.tensor_tensor(out=t1, in0=x1, in1=s,
                                                      op=ALU.mult), waits=[tc])
        te = P.op("vector", lambda e: e.tensor_tensor(out=t2, in0=x2, in1=c,
                                                      op=ALU.mult), waits=[td])
        tf = P.op("vector", lambda e: e.tensor_tensor(
            out=qrf3[:, :, 32:64], in0=t1, in1=t2, op=ALU.add), waits=[te])
        tg = P.op("vector", lambda e: e.tensor_copy(
            out=qrf3[:, :, 64:128], in_=qev3[:, :, 64:128]), waits=[tf])
        th1 = P.op("scalar", lambda e: e.activation(
            out=P.f32(O_QT3, 1024), in_=P.f32(O_QRF, 1024), func=AF.Square),
            waits=[tg])
        th2 = P.op("scalar", lambda e: e.activation(
            out=P.f32(O_QT3 + 2048, 1024), in_=P.f32(O_QRF + 2048, 1024),
            func=AF.Square), waits=[tg, th1])
        ti1 = P.op("vector", lambda e: e.tensor_reduce(
            out=P.rmsS[:, 0:8], in_=P.f32(O_QT3, 1024)
            .rearrange("p (k q) -> p k q", q=128), axis=AX, op=ALU.add),
            waits=[th1, tg])
        ti2 = P.op("vector", lambda e: e.tensor_reduce(
            out=P.rmsS[:, 8:16], in_=P.f32(O_QT3 + 2048, 1024)
            .rearrange("p (k q) -> p k q", q=128), axis=AX, op=ALU.add),
            waits=[th2])
        tj = P.op("vector", lambda e: e.tensor_scalar(
            out=P.rmsS[:, 0:16], in0=P.rmsS[:, 0:16], scalar1=1.0 / HD,
            scalar2=RMSEPS, op0=ALU.mult, op1=ALU.add), waits=[ti1, ti2])
        tk = P.op("scalar", lambda e: e.activation(
            out=P.rmsS[:, 16:32], in_=P.rmsS[:, 0:16], func=AF.Sqrt),
            waits=[tj])
        tl = P.op("vector", lambda e: e.reciprocal(
            out=P.rmsS[:, 0:16], in_=P.rmsS[:, 16:32]), waits=[tk])
        tm = P.op("vector", lambda e: e.tensor_scalar_mul(
            out=P.rmsS[:, 0:16], in0=P.rmsS[:, 0:16], scalar1=SCALE),
            waits=[tl])
        rstdb = P.rmsS[:, 0:16].unsqueeze(2).broadcast_to([128, H, 128])
        qnwb3 = P.qnwb[:].unsqueeze(1).broadcast_to([128, H, 128])
        qt3 = P.f32(O_QT3, 2048).rearrange("p (k q) -> p k q", q=128)
        tn = P.op("vector", lambda e: e.tensor_tensor(
            out=qt3, in0=qrf3, in1=rstdb, op=ALU.mult), waits=[tm, th2])
        to = P.op("vector", lambda e: e.tensor_tensor(
            out=P.bf(O_QNB, 2048).rearrange("p (k q) -> p k q", q=128),
            in0=qt3, in1=qnwb3, op=ALU.mult), waits=[tn, self.t_qnb])
        return to

    def ph_d(self, r):
        P = self
        mk_free = list(self.prev["mask_free"])
        tv0 = self.tok("vector")
        psA_free = [tv0, tv0]
        st = {"psB_free": [tv0, tv0], "psC_free": [tv0, tv0],
              "anat_free": [self.prev["e_done"], self.prev["e_done"]],
              "cmT_rd": None, "crow_rd": None, "t_dn": [None, None],
              "att_ev": None, "t_last": None, "t_sc": {}, "t_cm": {},
              "crow_tok": {}, "p2sc": {}, "p2no": {}, "exp_tok": {}}

        def pass1_head(j, hh):
            jp = j % 2
            psA = P.P0[:, 0:640]
            P.op("tensor", lambda e, j=j, hh=hh, psA=psA: e.matmul(
                psA[:, 0:512], P.bf(O_QTN + (j * H + hh) * 128, 128),
                P.bf(O_KT + j * 128, 512), start=True, stop=True),
                waits=[self.T_QTN, self.T_KT, psA_free[0]], inc=False)
            t_s = P.op("tensor", lambda e, j=j, hh=hh, psA=psA: e.matmul(
                psA[:, 512:640], P.bf(O_QTN + (j * H + hh) * 128, 128),
                P.bf(O_KT + (j + 4) * 128, 128), start=True, stop=True))
            t_r = P.op("vector", lambda e, psA=psA: e.reduce_max(
                out=P.red1[:, 0:1], in_=psA, axis=AX, negate=True),
                waits=[t_s])
            psA_free[0] = t_r
            t_c = P.op("vector", lambda e, hh=hh, jp=jp: e.tensor_tensor(
                out=P.cmaxb[:, jp * H + hh: jp * H + hh + 1],
                in0=P.red1[:, 0:1], in1=P.nsinkb[:, hh:hh + 1],
                op=ALU.min),
                waits=[t_r] + ([st["cmT_rd"]] if hh == 0 else []))
            st["t_cm"][j] = P.op("vector", lambda e, hh=hh, jp=jp:
                                 e.tensor_copy(
                                     out=P.cmaxf[:, jp * H + hh:
                                                 jp * H + hh + 1],
                                     in_=P.cmaxb[:, jp * H + hh:
                                                 jp * H + hh + 1]),
                                 waits=[t_c])

        def crow(j):
            jp = j % 2
            t_t = P.op("tensor", lambda e, jp=jp: e.transpose(
                P.PT[0:16, 896:1024], P.cmaxb[:, jp * H:(jp + 1) * H],
                P.iden[:]),
                waits=[st["t_cm"][j], self.pt_free, st["crow_rd"]])
            t_te = P.op("scalar", lambda e: e.activation(
                out=P.AR[0:16, O_CMT:O_CMT + 128], in_=P.PT[0:16, 896:1024],
                func=AF.Copy),
                waits=[t_t] + (list(self.prev["store"])
                               if (r == 1 and j == 0) else []))
            st["cmT_rd"] = t_te
            self.pt_free = t_te
            st["crow_tok"][j] = P.op("scalar", lambda e: e.dma_start(
                out=P.AR[0:1, O_CRW:O_CRW + 16 * 640]
                .rearrange("p (c r q) -> p c r q", r=5, q=128),
                in_=P.AR[0:16, O_CMT:O_CMT + 128].unsqueeze(1).unsqueeze(1)
                .broadcast_to([16, 1, 5, 128]).squeeze(1)), dma="dCRW",
                waits=[t_te])

        def pass2_scores(j, hh):
            msl = j % 2
            t_cr = st["crow_tok"][j]
            t_mk = st["t_mk"][j]
            par = hh % 2
            psB = P.P1[:, 0:640] if par == 0 else P.P0[:, 1024:1664]
            t_sc = None
            for c in range(4):
                P.op("tensor", lambda e, j=j, hh=hh, c=c, psB=psB:
                     e.matmul(psB[:, c * 128:(c + 1) * 128],
                              P.bf(O_KT + (j + c) * 128, 128),
                              P.bf(O_QTN + (j * H + hh) * 128, 128),
                              start=(c == 0), stop=False),
                     waits=[t_cr, t_mk, st["psB_free"][par]]
                     if c == 0 else [], inc=False)
            P.op("tensor", lambda e, hh=hh, psB=psB: e.matmul(
                psB[:, 0:512], P.ones_r[:],
                P.AR[0:1, O_CRW + hh * 640: O_CRW + hh * 640 + 512],
                start=False, stop=False), inc=False)
            P.op("tensor", lambda e, msl=msl, psB=psB: e.matmul(
                psB[:, 0:512], P.iden[:],
                P.bf(O_MKS + msl * 640, 512),
                start=False, stop=True), inc=False)
            P.op("tensor", lambda e, j=j, hh=hh, psB=psB: e.matmul(
                psB[:, 512:640], P.bf(O_KT + (j + 4) * 128, 128),
                P.bf(O_QTN + (j * H + hh) * 128, 128),
                start=True, stop=False), inc=False)
            P.op("tensor", lambda e, hh=hh, psB=psB: e.matmul(
                psB[:, 512:640], P.ones_r[:],
                P.AR[0:1, O_CRW + hh * 640 + 512: O_CRW + hh * 640 + 640],
                start=False, stop=False), inc=False)
            t_sc = P.op("tensor", lambda e, msl=msl, psB=psB: e.matmul(
                psB[:, 512:640], P.iden[:],
                P.bf(O_MKS + msl * 640 + 512, 128),
                start=False, stop=True), inc=True)
            t_ex = P.op("scalar", lambda e, par=par, psB=psB:
                        e.activation(out=P.bf(O_PRB + par * 640, 640),
                                     in_=psB, func=AF.Exp),
                        waits=[t_sc] + (list(self.prev["store"])
                                        if (r == 1 and j == 0
                                            and hh < 2) else []))
            st["exp_tok"][hh] = t_ex
            st["p2sc"][j] = t_sc

        def pass2_attn(j, hh):
            jp = j % 2
            par = hh % 2
            psAt = (P.P1[:, 768:897] if par == 0 else P.P0[:, 1792:1921])
            psDen = (P.P1[:, 896:897] if par == 0 else P.P0[:, 1920:1921])
            t_ex = st["exp_tok"][hh]
            t_at = None
            for c in range(NCH):
                pch = P.bf(O_PRB + par * 640 + c * 128, 128)
                t_at = P.op("tensor", lambda e, pch=pch, psAt=psAt, j=j,
                            c=c: e.matmul(
                                psAt[:, 0:129], pch,
                                P.bf(O_VN + (j + c) * VST, 129),
                                start=(c == 0), stop=(c == NCH - 1)),
                            waits=[t_ex, self.T_V, self.t_vone,
                                   st["psC_free"][par]]
                            if c == 0 else [], inc=(c == NCH - 1))
            dso = par * 2
            t_es = P.op("scalar", lambda e, hh=hh, jp=jp, dso=dso:
                        e.activation(
                            out=P.dstat[:, dso:dso + 1],
                            in_=P.cmaxf[:, jp * H + hh: jp * H + hh + 1],
                            func=AF.Exp, bias=P.sinkb[:, hh:hh + 1],
                            scale=1.0),
                        waits=[self.t_sink, st["t_cm"][j],
                               st["t_dn"][par]])
            t_dn = P.op("vector", lambda e, psDen=psDen, dso=dso:
                        e.tensor_tensor(
                            out=P.dstat[:, dso + 1:dso + 2], in0=psDen,
                            in1=P.dstat[:, dso:dso + 1],
                            op=ALU.add), waits=[t_at, t_es])
            st["t_dn"][par] = t_dn
            t_iv = P.op("vector", lambda e, dso=dso: e.reciprocal(
                out=P.dstat[:, dso:dso + 1],
                in_=P.dstat[:, dso + 1:dso + 2]), waits=[t_dn])
            t_no = P.op("vector", lambda e, hh=hh, j=j, psAt=psAt, dso=dso:
                        e.tensor_scalar(
                            out=P.bf(O_ANAT + (j % 2) * 2048 + hh * 128,
                                     128),
                            in0=psAt[:, 0:128],
                            scalar1=P.dstat[:, dso:dso + 1],
                            scalar2=None, op0=ALU.mult),
                        waits=[t_iv] + ([st["anat_free"][j % 2]]
                                        if hh == 0 else []))
            st["psC_free"][par] = t_no
            st["psB_free"][par] = t_no
            st["p2no"][j] = t_no

        def pass2_tail(j):
            msl = j % 2
            t_no = st["p2no"][j]
            t_sc = st["p2sc"][j]
            for hb in range(2):
                st["t_last"] = self.pt_batch(
                    lambda c, hb=hb, j=j: P.bf(
                        O_ANAT + (j % 2) * 2048 + (hb * 8 + c) * 128, 128),
                    lambda hb=hb, j=j: P.bf(O_ATT, H * 1024)
                    .rearrange("p (k t q) -> p k t q", k=H, t=NT)
                    [:, hb * 8:(hb + 1) * 8, j, :],
                    [t_no, self.prev["att_free"] if (j == 0 and hb == 0)
                     else None])
            st["anat_free"][j % 2] = st["t_last"]
            mk_free[msl] = t_sc
            st["crow_rd"] = t_sc

        st["t_mk"] = {}
        for j in range(NT):
            msl = j % 2
            st["t_mk"][j] = P.op("sync", lambda e, r=r, j=j, msl=msl:
                                 e.dma_start(
                                     out=P.bf(O_MKS + msl * 640, 640)
                                     .rearrange("p (c q) -> p c q", q=128),
                                     in_=P.mask_in[r, j]
                                     .rearrange("c p q -> p c q")),
                                 dma=f"dMK{msl}",
                                 waits=[mk_free[msl], self.T_QTN] +
                                 (list(self.prev["store"])
                                  if (r == 1 and j < 2) else []))
            self.pump(12)
            for hh in range(H):
                pass1_head(j, hh)
                if j > 0:
                    pass2_scores(j - 1, hh)
                    if hh > 0:
                        pass2_attn(j - 1, hh - 1)
            if j > 0:
                pass2_attn(j - 1, H - 1)
                pass2_tail(j - 1)
            crow(j)
        for hh in range(H):
            pass2_scores(NT - 1, hh)
            if hh > 0:
                pass2_attn(NT - 1, hh - 1)
        pass2_attn(NT - 1, H - 1)
        pass2_tail(NT - 1)
        self.prev["mask_free"] = mk_free
        self.T_ATT = st["t_last"]
        self.prev["c_done"] = st["t_last"]

    def ph_e(self, r):
        P = self
        if self.dbg is not None:
            t1 = P.op("sync", lambda e, r=r: e.dma_start(
                out=self.dbg2[r], in_=P.bf(O_ATT, H * RND)), dma="dDA",
                waits=[self.T_ATT])
            t2 = P.op("sync", lambda e, r=r: e.dma_start(
                out=self.dbg4[r], in_=P.bf(O_KT, NSRC * 128)), dma="dDK",
                waits=[self.T_ATT])
            t3 = P.op("sync", lambda e, r=r: e.dma_start(
                out=self.dbg5[r].rearrange("p (i q) -> p i q", q=129),
                in_=P.bf(O_VN, NSRC * VST)
                .rearrange("p (i q) -> p i q", q=VST)[:, :, 0:129]),
                dma="dDV", waits=[self.T_ATT, t1, t2])
            self.st_final.extend([t1, t2, t3])
            self.dbg_kv = [t3, t3]
            self.T_ATT = t3
        if self.T_GAMMA is None:
            self.gamma_finalize()
        x_ld = [None] * NT
        for j in range(NT):
            x_ld[j] = P.op("sync", lambda e, j=j, r=r: e.dma_start(
                out=P.f32(O_X + j * 4096, 2048),
                in_=P.h_in[HALO + r * RND + j * 128:
                           HALO + r * RND + (j + 1) * 128, :]),
                dma=f"dX{j}",
                waits=[self.T_ATT if (j < 4 or j == 6) else
                       (self.T_QTN if j < 6 else self.T_GAMMA),
                       self.T_QTN if j == 7 else None,
                       self.prev["store"][j]])
        w2_free = [None, None, None]
        w1g_free = None
        e_add = {}
        y1ps_free = [self.T_ATT, self.T_ATT, self.T_ATT]
        t_y1_last = None

        pairs = [(g, oq) for g in range(4) for oq in range(4)]
        w2_free = [None, None, None, None]
        w2tok = {}

        def emit_w2(pi, extra=None):
            toks = []
            for hv in range(2):
                s3 = (pi * 2 + hv) % 4
                toks.append(P.op(
                    "gpsimd", lambda e, pi=pi, hv=hv, s3=s3: e.dma_start(
                        out=P.bf(O_W2QS[s3], 4096)
                        .rearrange("p (c q) -> p c q", q=512),
                        in_=P.ow2_in.rearrange("(c p) o -> c p o", p=128)
                        [pairs[pi][0] * 16 + hv * 8:
                         pairs[pi][0] * 16 + hv * 8 + 8, :,
                         pairs[pi][1] * 512:(pairs[pi][1] + 1) * 512]
                        .rearrange("c p o -> p c o")),
                    dma=f"dW2_{s3}",
                    waits=[w2_free[s3], extra,
                           self.T_ATT if (O_W2QS[s3] == 0 and
                                          w2_free[s3] is None) else None]))
            w2tok[pi] = toks

        emit_w2(0)
        for g in range(4):
            t_w1 = P.op("gpsimd", lambda e, g=g: e.dma_start(
                out=P.bf(O_W1G, 8192).rearrange("p (c q) -> p c q", q=2048),
                in_=P.ow1_in[g].rearrange("(c p) o -> p c o", p=128)),
                dma="dW1G",
                waits=[w1g_free, self.T_ATT if g == 0 else None])
            for oc in range(NKC):
                for th in range(2):
                    s3y = (2 * oc + th) % 3
                    pco = s3y * 512
                    t_m = None
                    for kc in range(4):
                        t_m = P.op("tensor", lambda e, g=g, oc=oc, th=th,
                                   kc=kc, pco=pco: e.matmul(
                                       P.P1[:, pco:pco + 512],
                                       P.bf(O_W1G + kc * 2048 + oc * 128,
                                            128),
                                       P.bf(O_ATT + (g * 4 + kc) * 1024
                                            + th * 512, 512),
                                       start=(kc == 0), stop=(kc == 3)),
                                   waits=[t_w1, y1ps_free[s3y],
                                          self.T_GAMMA if (g == 0 and oc == 0
                                                           and th == 0)
                                          else None,
                                          e_add.get(7) if (g > 0 and oc == 0
                                                           and th == 0)
                                          else None]
                                   if kc == 0 else [], inc=(kc == 3))
                    t_y1 = P.op("scalar", lambda e, oc=oc, th=th, pco=pco:
                                e.activation(
                                    out=P.bf(O_Y1 + oc * 1024 + th * 512,
                                             512),
                                    in_=P.P1[:, pco:pco + 512],
                                    func=AF.Copy), waits=[t_m])
                    y1ps_free[s3y] = t_y1
                    t_y1_last = t_y1
            w1g_free = t_y1_last
            T_Y1 = t_y1_last
            if g == 0:
                emit_w2(1, extra=T_Y1)
            for outq in range(4):
                pi = g * 4 + outq
                hv_t = w2tok[pi]
                for j in range(NT):
                    t_m = None
                    for kc2 in range(NKC):
                        hv = kc2 // 8
                        s3 = (pi * 2 + hv) % 4
                        w = []
                        if kc2 == 0:
                            w = [hv_t[0], T_Y1, x_ld[j],
                                 e_add.get((j - 4) % 8)]
                        elif kc2 == 8:
                            w = [hv_t[1]]
                        t_m = P.op("tensor", lambda e, j=j, kc2=kc2, s3=s3:
                                   e.matmul(
                                       P.P0[:, (j % 4) * 512:
                                            (j % 4) * 512 + 512],
                                       P.bf(O_Y1 + kc2 * 1024 + j * 128,
                                            128),
                                       P.bf(O_W2QS[s3] + (kc2 % 8) * 512,
                                            512),
                                       start=(kc2 == 0),
                                       stop=(kc2 == NKC - 1)),
                                   waits=w, inc=(kc2 == NKC - 1))
                    e_add[j] = P.op("vector", lambda e, j=j, outq=outq:
                                    e.tensor_tensor(
                                        out=P.f32(O_X + j * 4096
                                                  + outq * 1024, 512),
                                        in0=P.f32(O_X + j * 4096
                                                  + outq * 1024, 512),
                                        in1=P.P0[:, (j % 4) * 512:
                                                 (j % 4) * 512 + 512],
                                        op=ALU.add),
                                    waits=[t_m, e_add.get(j)])
                pe_now = self.tok("tensor")
                w2_free[(pi * 2) % 4] = pe_now
                w2_free[(pi * 2 + 1) % 4] = pe_now
                if pi + 2 < len(pairs):
                    emit_w2(pi + 2)
        self.e_add = e_add
        self.T_E = e_add[7]
        self.prev["e_done"] = self.T_E

    def ph_f(self, r):
        P = self
        self.dbg_toks = {}
        if self.dbg is not None:
            for j in range(NT):
                t = P.op("sync", lambda e, j=j, r=r: e.dma_start(
                    out=self.dbg[r * RND + j * 128:r * RND + (j + 1) * 128, :],
                    in_=P.f32(O_X + j * 4096, 2048)), dma=f"dDB{j}",
                    waits=[self.e_add[jj] for jj in range(NT)])
                self.st_final.append(t)
                self.dbg_toks[j] = t
        t_ln2 = None
        lnb2_free = None
        for j in range(NT):
            xv = P.f32(O_X + j * 4096, 2048)
            t_st = self.ln_stats(xv, self.e_add[j], O_LNB2,
                                 extra_act_w=[lnb2_free])
            lnt = P.bf(O_LNB2 + 2048, 2048)
            t_ln = P.op("vector", lambda e, xv=xv, lnt=lnt: e.tensor_scalar(
                out=lnt, in0=xv, scalar1=P.lns[:, 2:3],
                scalar2=P.lns[:, 6:7], op0=ALU.subtract, op1=ALU.mult),
                waits=[t_st, lnb2_free])
            for hb in range(2):
                t_ln2 = self.pt_batch(
                    lambda c, hb=hb, lnt=lnt: lnt[:, (hb * 8 + c) * 128:
                                                  (hb * 8 + c + 1) * 128],
                    lambda hb=hb, j=j: P.bf(O_LN2T, NKC * RND)
                    .rearrange("p (k t q) -> p k t q", k=NKC, t=NT)
                    [:, hb * 8:(hb + 1) * 8, j, :],
                    [t_ln, self.T_E if (j == 0 and hb == 0) else None])
            lnb2_free = t_ln2
        self.T_LN2 = t_ln2

    def ph_g(self, r):
        P = self
        nwin = (NW + SG - 1) // SG
        stg_free = [None, None]
        sq_free = [None, None, None]
        self._s1_rd = [None, None]
        self._s1b_rd = None
        self._s1_ping = 0
        self._qd_n = 0
        w1t_free = [None, None]
        w2t_free = [None, None]
        w3_free = [self.T_LN2] * 7
        stores = [None] * NT
        x1ps_rd = [None, None]
        x2ps_rd = None
        out_guard = [self.e_add[7]]
        hid_rd = [None] * SG
        out_ev = {}
        for j in range(NT):
            for dh in range(2):
                out_ev[(j, dh)] = self.dbg_toks.get(j, self.e_add[j])
        self.gst = getattr(self, "gst", 0)

        def quant_half(src_off, wsel, dst, t_ld, extra=()):
            if wsel == 1:
                ta = P.op("vector", lambda e: e.tensor_scalar(
                    out=P.bf(O_S1B, 1024), in0=P.f32(src_off, 1024),
                    scalar1=P.invg[:, 1:2], scalar2=0.5,
                    op0=ALU.mult, op1=ALU.is_ge),
                    waits=[t_ld, self.T_GAMMA] + list(extra) +
                    [self._s1b_rd])
                tb = P.op("vector", lambda e, dst=dst: e.tensor_scalar(
                    out=dst, in0=P.f32(src_off, 1024),
                    scalar1=P.invg[:, 3:4], scalar2=0.5,
                    op0=ALU.mult, op1=ALU.is_ge), waits=[ta])
                tc = P.op("vector", lambda e, dst=dst: e.tensor_tensor(
                    out=dst, in0=P.bf(O_S1B, 1024), in1=dst,
                    op=ALU.subtract), waits=[tb])
                self._s1b_rd = tc
                return tc, tb
            pp = self._s1_ping
            self._s1_ping ^= 1
            s1o = O_S1 if pp == 0 else O_S1C
            t1 = P.op("scalar", lambda e, s1o=s1o: e.activation(
                out=P.bf(s1o, 1024), in_=P.f32(src_off, 1024), func=AF.Sign,
                bias=P.b_m05[:], scale=P.invg[:, wsel:wsel + 1]),
                waits=[t_ld, self.T_GAMMA, self.t_consts] + list(extra) +
                [self._s1_rd[pp]])
            t2 = P.op("scalar", lambda e, dst=dst: e.activation(
                out=dst, in_=P.f32(src_off, 1024), func=AF.Sign,
                bias=P.b_p05[:], scale=P.invg[:, wsel:wsel + 1]), waits=[t1])
            t3 = P.op("vector", lambda e, dst=dst, s1o=s1o: e.tensor_tensor(
                out=dst, in0=P.bf(s1o, 1024), in1=dst,
                op=ALU.add), waits=[t2])
            self._s1_rd[pp] = t3
            return t3, t2


        def prep_w12(c, wsel, fw_in, dst_base, free_list, war):
            sl = c % 2
            dst = dst_base + sl * 2048
            lds, qts, qsls = [], [], []
            for hf in range(2):
                ssl = self.gst % 2
                self.gst += 1
                t_ld = P.op("sync", lambda e, c=c, hf=hf, ssl=ssl,
                            fw_in=fw_in: e.dma_start(
                                out=P.f32(O_FST + ssl * 2048, 1024),
                                in_=fw_in[c * 128:(c + 1) * 128,
                                          hf * 1024:(hf + 1) * 1024]),
                            dma=f"dF{ssl}", waits=[stg_free[ssl], war])
                lds.append((t_ld, ssl))
            for hf in range(2):
                t_ld, ssl = lds[hf]
                qsl = self._qd_n % NQD
                self._qd_n += 1
                qsls.append(qsl)
                t_q, t_sf = quant_half(O_FST + ssl * 2048, wsel,
                                       P.bf(O_QD + qsl * 1024, 1024), t_ld,
                                       extra=[sq_free[qsl]])
                stg_free[ssl] = t_sf
                qts.append(t_q)
            toks = []
            for hf in range(2):
                qsl = qsls[hf]
                t_tr = P.op("sync", lambda e, dst=dst, hf=hf, qsl=qsl:
                            e.dma_start(
                                out=P.bf(dst + hf * 1024, 1024)
                                .rearrange("p (cc q) -> p cc q", q=128),
                                in_=P.bf(O_QD + qsl * 1024, 1024)
                                .rearrange("p (cc q) -> p cc q", q=128),
                                transpose=True),
                            dma=f"dT{qsl}",
                            waits=[qts[hf], free_list[sl] if hf == 0
                                   else None])
                sq_free[qsl] = t_tr
                toks.append(t_tr)
            return dst, toks

        def prep_w3(c, war):
            sl = c % 7
            dst = O_W3A + sl * 2048
            lds, qts, qsls = [], [], []
            for hf in range(2):
                ssl = self.gst % 2
                self.gst += 1
                t_ld = P.op("sync", lambda e, c=c, hf=hf, ssl=ssl:
                            e.dma_start(
                                out=P.f32(O_FST + ssl * 2048, 1024)
                                .rearrange("p (m q) -> p m q", q=128),
                                in_=P.fw3_in
                                .rearrange("(m p) (cc q) -> p m cc q",
                                           p=128, q=128)
                                [:, hf * 8:(hf + 1) * 8, c, :]),
                            dma=f"dF{ssl}", waits=[stg_free[ssl], war])
                lds.append((t_ld, ssl))
            for hf in range(2):
                t_ld, ssl = lds[hf]
                qsl = self._qd_n % NQD
                self._qd_n += 1
                qsls.append(qsl)
                t_q, t_sf = quant_half(O_FST + ssl * 2048, 2,
                                       P.bf(O_QD + qsl * 1024, 1024), t_ld,
                                       extra=[sq_free[qsl]])
                stg_free[ssl] = t_sf
                qts.append(t_q)
            toks = []
            for hf in range(2):
                qsl = qsls[hf]
                t_tr = P.op("sync", lambda e, dst=dst, hf=hf, qsl=qsl:
                            e.dma_start(
                                out=P.bf(dst + hf * 1024, 1024)
                                .rearrange("p (m q) -> p m q", q=128),
                                in_=P.bf(O_QD + qsl * 1024, 1024)
                                .rearrange("p (m q) -> p m q", q=128),
                                transpose=True),
                            dma=f"dT{qsl}",
                            waits=[qts[hf], w3_free[sl] if hf == 0
                                   else None])
                sq_free[qsl] = t_tr
                toks.append(t_tr)
            return dst, toks

        cglob = 0
        preps = {}

        def prep(c):
            war0 = self.T_E if c == 0 else None
            d1, tk1 = prep_w12(c, 0, P.fw1_in, O_W1T, w1t_free, war0)
            d2, tk2 = prep_w12(c, 1, P.fw2_in, O_W2T, w2t_free, None)
            d3, tk3 = prep_w3(c, self.T_LN2 if c == 0 else None)
            preps[c] = (d1, tk1, d2, tk2, d3, tk3)

        prep(0)
        for win in range(nwin):
            ncw = min(SG, NW - win * SG)
            hid_t = [None] * ncw
            w3d = [None] * ncw
            w3tk = [None] * ncw
            for cw in range(ncw):
                c = win * SG + cw
                d1, tk1, d2, tk2, w3d[cw], tk3 = preps.pop(c)
                w3tk[cw] = tk3
                cpar = c % 2
                x1ps = P.P1[:, 0:1024] if cpar == 0 else P.P0[:, 0:1024]
                t_x1 = None
                for kc in range(NKC):
                    for nn in range(2):
                        t_x1 = P.op("tensor", lambda e, d1=d1, kc=kc, nn=nn,
                                    x1ps=x1ps: e.matmul(
                                        x1ps[:, nn * 512:(nn + 1) * 512],
                                        P.bf(d1 + kc * 128, 128),
                                        P.bf(O_LN2T + kc * 1024 + nn * 512,
                                             512),
                                        start=(kc == 0),
                                        stop=(kc == NKC - 1)),
                                    waits=[tk1[0], tk1[1], self.T_LN2,
                                           x1ps_rd[cpar], self.T_E,
                                           out_guard[0] if cpar else None]
                                    if (kc == 0 and nn == 0) else [],
                                    inc=(kc == NKC - 1 and nn == 1))
                w1t_free[c % 2] = t_x1
                if c + 1 < NW:
                    prep(c + 1)
                t_sg = P.op("scalar", lambda e, x1ps=x1ps: e.activation(
                    out=P.f32(O_SGM, 1024), in_=x1ps,
                    func=AF.Sigmoid, scale=0.5),
                    waits=[t_x1, hid_rd[cw], self.T_LN2, x2ps_rd])
                x1ps_rd[cpar] = t_sg
                t_hd = None
                for nn in range(2):
                    t_x2 = None
                    for kc in range(NKC):
                        t_x2 = P.op("tensor", lambda e, d2=d2, kc=kc, nn=nn:
                                    e.matmul(
                                        P.P1[:, 1024:1536],
                                        P.bf(d2 + kc * 128, 128),
                                        P.bf(O_LN2T + kc * 1024 + nn * 512,
                                             512),
                                        start=(kc == 0),
                                        stop=(kc == NKC - 1)),
                                    waits=[tk2[0], tk2[1], x2ps_rd]
                                    if kc == 0 else [],
                                    inc=(kc == NKC - 1))
                    t_hd = P.op("vector", lambda e, cw=cw, nn=nn:
                                e.tensor_tensor(
                                    out=P.bf(O_HID + cw * 1024 + nn * 512,
                                             512),
                                    in0=P.f32(O_SGM + nn * 1024, 512),
                                    in1=P.P1[:, 1024:1536], op=ALU.mult),
                                waits=[t_x2, t_sg] + ([hid_rd[cw]]
                                                      if nn == 0 else []))
                    x2ps_rd = t_hd
                hid_t[cw] = t_hd
                w2t_free[c % 2] = t_hd
                cglob += 1
            for j in range(NT):
                for dh in range(2):
                    pco = dh * 1024
                    t_m = None
                    for half in range(2):
                        for cw in range(ncw):
                            t_m = P.op(
                                "tensor", lambda e, j=j, cw=cw, w3d=w3d,
                                pco=pco, dh=dh, half=half, ncw=ncw: e.matmul(
                                    P.P0[:, pco + half * 512:
                                         pco + half * 512 + 512],
                                    P.bf(O_HID + cw * 1024 + j * 128, 128),
                                    P.bf(w3d[cw] + dh * 1024 + half * 512,
                                         512),
                                    start=(cw == 0), stop=(cw == ncw - 1)),
                                waits=[hid_t[cw], w3tk[cw][0], w3tk[cw][1],
                                       out_ev[(j, dh)],
                                       out_ev[((j - 1) % NT, dh)]]
                                if (cw == 0 and half == 0) else
                                ([hid_t[cw], w3tk[cw][0], w3tk[cw][1]]
                                 if cw == 0 else [w3tk[cw][0], w3tk[cw][1]]),
                                inc=(cw == ncw - 1 and half == 1))
                    t_a = P.op("vector", lambda e, j=j, dh=dh, pco=pco:
                               e.scalar_tensor_tensor(
                                   out=P.f32(O_X + j * 4096 + dh * 2048,
                                             1024),
                                   in0=P.P0[:, pco:pco + 1024], scalar=0.5,
                                   in1=P.f32(O_X + j * 4096 + dh * 2048,
                                             1024),
                                   op0=ALU.mult, op1=ALU.add),
                               waits=[t_m, out_ev[(j, dh)]])
                    out_ev[(j, dh)] = t_a
            pe_now = self.tok("tensor")
            for cw in range(ncw):
                hid_rd[cw] = pe_now
            for sl in range(7):
                w3_free[sl] = pe_now
            out_guard[0] = out_ev[(NT - 1, 0)]
            if win == nwin - 1:
                for j in range(NT):
                    stores[j] = P.op("sync", lambda e, j=j, r=r:
                                     e.dma_start(
                                         out=P.out_hbm[r * RND + j * 128:
                                                       r * RND
                                                       + (j + 1) * 128, :],
                                         in_=P.f32(O_X + j * 4096, 2048)),
                                     dma=f"dST{j}",
                                     waits=[out_ev[(j, 0)], out_ev[(j, 1)]])
        self.prev["store"] = stores
        self.st_final.extend(stores)
        pe_end = self.tok("tensor")
        self.prev["x_free"] = stores[-1]
        self.prev["att_free"] = pe_end
        self.prev["cos_free"] = pe_end
        self.prev["ln1_free"] = pe_end
        self.prev["qtn_free"] = pe_end
        self.prev["wqu_free"] = pe_end
        self.prev["wkv_free"] = pe_end

    def round(self, r):
        self.ph_a(r)
        self.ph_b(r)
        self.ph_c(r)
        self.ph_d(r)
        self.ph_e(r)
        self.ph_f(r)
        self.ph_g(r)


def _host_masks(quarter0):
    i = np.arange(128)
    m0 = np.where(i[None, :] < i[:, None], 0.0, NEG).astype(np.float32)
    m4 = np.where(i[None, :] >= i[:, None], 0.0, NEG).astype(np.float32)
    mz = np.zeros((128, 128), np.float32)
    mf = np.full((128, 128), NEG, np.float32)
    out = np.zeros((2, NT, NCH, 128, 128), np.float32)
    for r in range(2):
        for j in range(NT):
            tg = r * NT + j
            for c in range(NCH):
                if quarter0 and tg - 4 + c < 0:
                    out[r, j, c] = mf
                elif c == 0:
                    out[r, j, c] = m0
                elif c == NCH - 1:
                    out[r, j, c] = m4
                else:
                    out[r, j, c] = mz
    return out.astype(BFD)


def _host_costab(qr):
    inv = 10000.0 ** (-(np.arange(0, 64, 2, dtype=np.float64)) / 64.0)
    cost = np.zeros((2, NSRC, 128, 32), np.float32)
    sint = np.zeros((2, NSRC, 128, 32), np.float32)
    for r in range(2):
        for i in range(NSRC):
            p0 = qr * RT + r * RND + (i - 4) * 128
            pos = p0 + np.arange(128)
            valid = pos >= 0
            ang = pos[:, None].astype(np.float64) * inv[None, :]
            cost[r, i] = np.where(valid[:, None], np.cos(ang), 0.0)
            sint[r, i] = np.where(valid[:, None], np.sin(ang), 0.0)
    return cost, sint


_NC_CACHE = {}


def kernel(L0_attn_h, L0_attn_wq_comp, L0_attn_wq_up, L0_attn_wk, L0_attn_wv,
           L0_attn_q_norm_w, L0_attn_k_norm_w, L0_attn_sink_logit,
           L0_attn_out_w1, L0_attn_out_w2, L0_ffn_w1, L0_ffn_w2, L0_ffn_w3):
    asf = lambda a: np.ascontiguousarray(np.asarray(a, np.float32))
    h = asf(L0_attn_h)
    wqc, wqu, wk, wv = map(asf, (L0_attn_wq_comp, L0_attn_wq_up, L0_attn_wk,
                                 L0_attn_wv))
    qnw, knw, sink = map(asf, (L0_attn_q_norm_w, L0_attn_k_norm_w,
                               L0_attn_sink_logit))
    ow1, ow2 = asf(L0_attn_out_w1), asf(L0_attn_out_w2)
    fw1, fw2, fw3 = map(asf, (L0_ffn_w1, L0_ffn_w2, L0_ffn_w3))

    iden = np.eye(128, dtype=BFD)
    masks_g = _host_masks(False)
    masks_q0 = _host_masks(True)
    in_maps = []
    for c in range(N_CORE):
        b, qr = c // 4, c % 4
        if qr == 0:
            hs = np.concatenate([np.zeros((HALO, D), np.float32),
                                 h[b, 0:RT, :]], axis=0)
        else:
            hs = np.ascontiguousarray(h[b, qr * RT - HALO:(qr + 1) * RT, :])
        cost, sint = _host_costab(qr)
        in_maps.append({
            "h": hs, "wq_comp": wqc, "wq_up": wqu, "wk": wk, "wv": wv,
            "q_norm_w": qnw, "k_norm_w": knw, "sink": sink,
            "out_w1": ow1, "out_w2": ow2,
            "ffn_w1": fw1, "ffn_w2": fw2, "ffn_w3": fw3,
            "costab": cost, "sintab": sint, "iden": iden,
            "masks": masks_q0 if qr == 0 else masks_g,
        })

    nc = _NC_CACHE.get("nc")
    if nc is None:
        nc = Builder().build()
        _NC_CACHE["nc"] = nc
    res = run_bass_kernel_spmd(nc, in_maps, core_ids=list(range(N_CORE)))
    out = np.empty((B, T_FULL, D), np.float32)
    for c in range(N_CORE):
        b, qr = c // 4, c % 4
        out[b, qr * RT:(qr + 1) * RT, :] = res.results[c]["out"]
    return out


if __name__ == "__main__":
    Builder().build()
    print("build ok")


# version 10
# speedup vs baseline: 1.0857x; 1.0857x over previous
import math

import numpy as np
import ml_dtypes

import concourse.bass as bass
import concourse.mybir as mybir
from concourse.bass_utils import run_bass_kernel_spmd

F32 = mybir.dt.float32
BF16 = mybir.dt.bfloat16
AF = mybir.ActivationFunctionType
ALU = mybir.AluOpType
AX = mybir.AxisListType.X
BFD = ml_dtypes.bfloat16

B, T_FULL, D = 2, 8192, 2048
H, HD, QC = 16, 128, 512
DFF = 5632
RT, HALO, RND = 2048, 512, 1024
NT, NSRC, NKC, NCH, NW = 8, 12, 16, 5, 44
SG = 6
LNEPS, RMSEPS, QEPS = 1e-5, 1e-6, 1e-8
SCALE = 1.0 / math.sqrt(HD)
NEG = -1e30
N_CORE = 8

ENGINES = ("sync", "scalar", "vector", "tensor", "gpsimd")
EPROG = {"sync": "pS", "scalar": "pA", "vector": "pV", "tensor": "pP",
         "gpsimd": "pG"}

ACOL = 89088
O_GACC, O_GSTG = 0, 1024
O_LN1T = 0
O_HT, O_QEV, O_QRF, O_QT3 = 24576, 24576, 28672, 32768
O_QNB, O_Q1T, O_WQC, O_KSC, O_LNB = 36864, 38912, 43008, 44032, 46080
O_COS, O_SIN = 50176, 50944
O_WQU, O_WK, O_WV, O_KT, O_VN, O_QTN = 57344, 65536, 67584, 69632, 51712, 72704
VST = 132
O_ANAT, O_ATT = 0, 4096
O_MKS, O_PRB, O_CMT, O_CRW = 24576, 25856, 27136, 27264
O_GST2 = 53296
O_X = 24576
O_Y1, O_W1G = 57344, 73728
O_W2QS = (0, 20480, 81920, 4096)
O_LN2T, O_LNB2 = 57344, 73728
O_W3A = 73728
O_W1T, O_W2T, O_FST, O_SGM, O_S1, O_QD, O_HID = (
    0, 4096, 8192, 12288, 14336, 16384, 18432)
O_S1B = 15360
O_S1C = 88064
NQD = 2


class Builder:
    def __init__(self):
        self.nc = bass.Bass("TRN2", target_bir_lowering=False)
        self.ops = {e: [] for e in ENGINES}
        self.semh = {}
        self.semc = {}
        self.ctx = None

    def s(self, name):
        if name not in self.semh:
            self.semh[name] = self.nc.alloc_semaphore(name)
            self.semc[name] = 0
        return name

    def tok(self, eng):
        n = EPROG[eng]
        return (n, self.semc.get(n, 0))

    def op(self, eng, fn, waits=(), dma=None, inc=True):
        wl = []
        for w in waits:
            if w is None:
                continue
            n, v = w
            if v > 0:
                self.s(n)
                wl.append((n, v))
        incs = []
        if dma is not None:
            self.s(dma)
            self.semc[dma] += 16
            incs.append((dma, 16))
            ret = (dma, self.semc[dma])
        elif inc:
            n = self.s(EPROG[eng])
            self.semc[n] += 1
            incs.append((n, 1))
            ret = (n, self.semc[n])
        else:
            ret = None
        self.ops[eng].append((wl, fn, incs))
        return ret

    def emit(self):
        with self.nc.Block() as block:
            for ename in ENGINES:
                def mk(ename):
                    def body(eng):
                        hi = {}
                        for waits, fn, incs in self.ops[ename]:
                            for (n, v) in waits:
                                if hi.get(n, 0) >= v:
                                    continue
                                eng.wait_ge(self.semh[n], v)
                                hi[n] = v
                            ins = fn(eng)
                            for (n, k) in incs:
                                ins.then_inc(self.semh[n], k)
                    return body
                getattr(block, ename)(mk(ename))

    def build(self):
        nc = self.nc
        import contextlib
        self.ctx = contextlib.ExitStack()
        E = self.ctx.enter_context

        self.h_in = nc.dram_tensor("h", [RT + HALO, D], F32,
                                   kind="ExternalInput").ap()
        self.wqc_in = nc.dram_tensor("wq_comp", [D, QC], F32,
                                     kind="ExternalInput").ap()
        self.wqu_in = nc.dram_tensor("wq_up", [QC, D], F32,
                                     kind="ExternalInput").ap()
        self.wk_in = nc.dram_tensor("wk", [D, HD], F32,
                                    kind="ExternalInput").ap()
        self.wv_in = nc.dram_tensor("wv", [D, HD], F32,
                                    kind="ExternalInput").ap()
        self.qnw_ap = nc.dram_tensor("q_norm_w", [HD], F32,
                                     kind="ExternalInput").ap()
        self.knw_ap = nc.dram_tensor("k_norm_w", [HD], F32,
                                     kind="ExternalInput").ap()
        self.sink_ap = nc.dram_tensor("sink", [H], F32,
                                      kind="ExternalInput").ap()
        self.ow1_in = nc.dram_tensor("out_w1", [4, QC, 2048], F32,
                                     kind="ExternalInput").ap()
        self.ow2_in = nc.dram_tensor("out_w2", [4 * 2048, D], F32,
                                     kind="ExternalInput").ap()
        self.fw1_in = nc.dram_tensor("ffn_w1", [DFF, D], F32,
                                     kind="ExternalInput").ap()
        self.fw2_in = nc.dram_tensor("ffn_w2", [DFF, D], F32,
                                     kind="ExternalInput").ap()
        self.fw3_in = nc.dram_tensor("ffn_w3", [D, DFF], F32,
                                     kind="ExternalInput").ap()
        self.cos_in = nc.dram_tensor("costab", [2, NSRC, 128, 32], F32,
                                     kind="ExternalInput").ap()
        self.sin_in = nc.dram_tensor("sintab", [2, NSRC, 128, 32], F32,
                                     kind="ExternalInput").ap()
        self.iden_ap = nc.dram_tensor("iden", [128, 128], BF16,
                                      kind="ExternalInput").ap()
        self.mask_in = nc.dram_tensor("masks", [2, NT, NCH, 128, 128], BF16,
                                      kind="ExternalInput").ap()
        self.out_hbm = nc.dram_tensor("out", [RT, D], F32,
                                      kind="ExternalOutput").ap()
        import os
        self.dbg = None
        if os.environ.get("KDBG"):
            self.dbg = nc.dram_tensor("dbg", [RT, D], F32,
                                      kind="ExternalOutput").ap()
            self.dbg2 = nc.dram_tensor("dbg2", [2, 128, H * RND], BF16,
                                       kind="ExternalOutput").ap()
            self.dbg3 = nc.dram_tensor("dbg3", [2, NT, 128, D], BF16,
                                       kind="ExternalOutput").ap()
            self.dbg4 = nc.dram_tensor("dbg4", [2, 128, NSRC * 128], BF16,
                                       kind="ExternalOutput").ap()
            self.dbg5 = nc.dram_tensor("dbg5", [2, 128, NSRC * 129], BF16,
                                       kind="ExternalOutput").ap()

        self.iden = E(nc.sbuf_tensor("idn", [128, 128], BF16))
        self.ones_r = E(nc.sbuf_tensor("ones_r", [1, 128], BF16))
        self.ones_c = E(nc.sbuf_tensor("ones_c", [128, 1], BF16))
        self.ones_rf = E(nc.sbuf_tensor("ones_rf", [1, 128], F32))
        self.ones_cf = E(nc.sbuf_tensor("ones_cf", [128, 1], F32))
        self.b_m05 = E(nc.sbuf_tensor("b_m05", [128, 1], F32))
        self.b_eps = E(nc.sbuf_tensor("b_eps", [128, 1], F32))
        self.b_p05 = E(nc.sbuf_tensor("b_p05", [128, 1], F32))
        self.sinkb = E(nc.sbuf_tensor("sinkb", [128, H], F32))
        self.nsinkb = E(nc.sbuf_tensor("nsinkb", [128, H], F32))
        self.qnwb = E(nc.sbuf_tensor("qnwb", [128, HD], F32))
        self.knwb = E(nc.sbuf_tensor("knwb", [128, HD], F32))
        self.gacc = E(nc.sbuf_tensor("gacc", [128, 112], F32))
        self.gtot = E(nc.sbuf_tensor("gtot", [1, 4], F32))
        self.invg = E(nc.sbuf_tensor("invg", [128, 4], F32))
        self.cmaxb = E(nc.sbuf_tensor("cmaxb", [128, 2 * H], BF16))
        self.cmaxf = E(nc.sbuf_tensor("cmaxf", [128, 2 * H], F32))
        self.red1 = E(nc.sbuf_tensor("red1", [128, 4], F32))
        self.gred = E(nc.sbuf_tensor("gred", [128, 1], F32))
        self.dstat = E(nc.sbuf_tensor("dstat", [128, 4], F32))
        self.lns = E(nc.sbuf_tensor("lns", [128, 8], F32))
        self.rmsS = E(nc.sbuf_tensor("rmsS", [128, 2 * H], F32))

        self.AR = E(nc.sbuf_tensor("arena", [128, ACOL], BF16))
        self.ARF = self.AR[:].bitcast(F32)
        self.P0 = E(nc.psum_tensor("P0", [128, 2048], F32))
        self.P1 = E(nc.psum_tensor("P1", [128, 1536], F32))
        self.PT = E(nc.psum_tensor("PT", [128, 1024], BF16))

        self.st_final = []
        self.prev = {"d_done": None, "c_done": None, "e_done": None,
                     "ln1_free": None, "store": [None] * NT,
                     "cos_free": None, "wqu_free": None, "wkv_free": None,
                     "qtn_free": None, "att_free": None, "x_free": None,
                     "mask_free": [None, None]}
        self.pt_free = None

        self.consts()
        self.gamma_gen = self.gamma_steps()
        self._gamma_last = None
        self.T_GAMMA = None
        self.gamma_finalize()
        for r in range(2):
            self.round(r)
        self.op("sync", lambda e: e.nop(), waits=self.st_final, inc=False)
        self.emit()
        self.ctx.close()
        return nc

    def bf(self, off, n):
        return self.AR[:, off:off + n]

    def f32(self, off, n):
        assert off % 2 == 0
        return self.ARF[:, off // 2: off // 2 + n]

    def consts(self):
        P = self
        self.t_iden = P.op("sync", lambda e: e.dma_start(
            out=P.iden[:], in_=P.iden_ap), dma="dIDN")
        self.t_sink = P.op("gpsimd", lambda e: e.dma_start(
            out=P.sinkb[:], in_=P.sink_ap
            .unsqueeze(0).partition_broadcast(128).squeeze(1)), dma="dBCS")
        self.t_qnb = P.op("gpsimd", lambda e: e.dma_start(
            out=P.qnwb[:], in_=P.qnw_ap
            .unsqueeze(0).partition_broadcast(128).squeeze(1)), dma="dBCQ")
        t_bc = P.op("gpsimd", lambda e: e.dma_start(
            out=P.knwb[:], in_=P.knw_ap
            .unsqueeze(0).partition_broadcast(128).squeeze(1)), dma="dBCK")
        self.t_bc = t_bc
        P.op("vector", lambda e: e.memset(P.ones_r[:], 1.0))
        P.op("vector", lambda e: e.memset(P.ones_c[:], 1.0))
        P.op("vector", lambda e: e.memset(P.ones_rf[:], 1.0))
        P.op("vector", lambda e: e.memset(P.ones_cf[:], 1.0))
        P.op("vector", lambda e: e.memset(P.b_m05[:], -0.5))
        P.op("vector", lambda e: e.memset(P.b_eps[:], LNEPS))
        t_b = P.op("vector", lambda e: e.memset(P.b_p05[:], 0.5))
        self.t_nsk = P.op("vector", lambda e: e.tensor_scalar_mul(
            out=P.nsinkb[:], in0=P.sinkb[:], scalar1=-1.0),
            waits=[self.t_sink, t_bc, t_b])
        self.t_consts = self.t_nsk

    def pump(self, n):
        for _ in range(n):
            if self.gamma_gen is not None:
                try:
                    next(self.gamma_gen)
                except StopIteration:
                    self.gamma_gen = None

    def gamma_steps(self):
        P = self
        pieces = []
        for wsel, src, ncols in ([(0, P.fw1_in, D)] * 0 +
                                 [(0, P.fw1_in, D)] +
                                 []):
            pass
        chunks = ([(P.fw1_in, i, D, i) for i in range(NW)] +
                  [(P.fw2_in, i, D, NW + i) for i in range(NW)] +
                  [(P.fw3_in, i, DFF, 2 * NW + i) for i in range(NKC)])
        last = [None, None]
        prevr = None
        n = 0
        for (src, i, ncols, col) in chunks:
            npc = (ncols + 2047) // 2048
            for pc in range(npc):
                cw = min(2048, ncols - pc * 2048)
                sl = n % 2
                n += 1
                stg = P.f32(O_GSTG + sl * 4096, 2048)
                t_ld = P.op("sync", lambda e, s2=stg[:, 0:cw],
                            a=src[i * 128:(i + 1) * 128,
                                  pc * 2048:pc * 2048 + cw]:
                            e.dma_start(out=s2, in_=a), dma=f"dG{sl}",
                            waits=[last[sl]])
                dst = (P.gacc[:, col:col + 1] if pc == 0 else P.gred[:])
                t_rd = P.op("vector", lambda e, s2=stg[:, 0:cw], d=dst:
                            e.tensor_reduce(out=d, in_=s2, axis=AX,
                                            op=ALU.add,
                                            apply_absolute_value=True),
                            waits=[t_ld, prevr])
                prevr = t_rd
                last[sl] = t_rd
                if pc > 0:
                    prevr = P.op("vector", lambda e, c=col: e.tensor_tensor(
                        out=P.gacc[:, c:c + 1], in0=P.gacc[:, c:c + 1],
                        in1=P.gred[:], op=ALU.add), waits=[prevr])
                    last[sl] = prevr
                self._gamma_last = prevr
                yield

    def gamma_finalize(self):
        P = self
        while self.gamma_gen is not None:
            self.pump(8)
        t_g = self._gamma_last
        t_pe_prev = None
        for wi, (c0, cn, nel) in enumerate([(0, NW, DFF * D),
                                            (NW, NW, DFF * D),
                                            (2 * NW, NKC, D * DFF)]):
            t1 = P.op("vector", lambda e, c0=c0, cn=cn: e.reduce_sum(
                out=P.red1[:, 1:2], in_=P.gacc[:, c0:c0 + cn], axis=AX),
                waits=[t_g])
            t2 = P.op("tensor", lambda e: e.matmul(
                P.P1[0:1, 0:1], P.red1[:, 1:2], P.ones_cf[:],
                start=True, stop=True), waits=[t1, self.t_consts, t_pe_prev])
            t3 = P.op("vector", lambda e, wi=wi: e.tensor_copy(
                out=P.gtot[0:1, wi:wi + 1], in_=P.P1[0:1, 0:1]),
                waits=[t2])
            t4 = P.op("tensor", lambda e, wi=wi: e.matmul(
                P.P1[:, 0:1], P.ones_rf[:], P.gtot[0:1, wi:wi + 1],
                start=True, stop=True), waits=[t3])
            t_pe_prev = t4
            t5 = P.op("vector", lambda e, wi=wi, nel=nel: e.tensor_scalar(
                out=P.invg[:, wi:wi + 1], in0=P.P1[:, 0:1],
                scalar1=1.0 / nel, scalar2=QEPS, op0=ALU.mult, op1=ALU.add),
                waits=[t4])
            t_g = P.op("vector", lambda e, wi=wi: e.reciprocal(
                out=P.invg[:, wi:wi + 1], in_=P.invg[:, wi:wi + 1]),
                waits=[t5])
        t_g = P.op("vector", lambda e: e.tensor_scalar_mul(
            out=P.invg[:, 3:4], in0=P.invg[:, 1:2], scalar1=-1.0),
            waits=[t_g])
        self.T_GAMMA = t_g
        self.t_gamma_pe = t_pe_prev

    def ln_stats(self, xv, t_in, sq_off, extra_act_w=()):
        P = self
        t_s1 = P.op("vector", lambda e: e.reduce_sum(
            out=P.lns[:, 0:1], in_=xv, axis=AX), waits=[t_in])
        t_s2 = P.op("scalar", lambda e: e.activation(
            out=P.f32(sq_off, 1024), in_=xv[:, 0:1024], func=AF.Square,
            accum_out=P.lns[:, 1:2]), waits=[t_in] + list(extra_act_w))
        t_s2b = P.op("scalar", lambda e: e.activation(
            out=P.f32(sq_off, 1024), in_=xv[:, 1024:2048], func=AF.Square,
            accum_out=P.lns[:, 7:8]), waits=[t_s2])
        t_s3 = P.op("vector", lambda e: e.tensor_scalar_mul(
            out=P.lns[:, 2:3], in0=P.lns[:, 0:1], scalar1=1.0 / D),
            waits=[t_s1])
        t_s4 = P.op("vector", lambda e: e.tensor_tensor(
            out=P.lns[:, 3:4], in0=P.lns[:, 2:3], in1=P.lns[:, 2:3],
            op=ALU.mult), waits=[t_s3])
        t_s4b = P.op("vector", lambda e: e.tensor_tensor(
            out=P.lns[:, 1:2], in0=P.lns[:, 1:2], in1=P.lns[:, 7:8],
            op=ALU.add), waits=[t_s2b, t_s4])
        t_s5 = P.op("vector", lambda e: e.tensor_scalar(
            out=P.lns[:, 4:5], in0=P.lns[:, 1:2], scalar1=1.0 / D,
            scalar2=None, op0=ALU.mult), waits=[t_s4b])
        t_s6 = P.op("vector", lambda e: e.tensor_tensor(
            out=P.lns[:, 4:5], in0=P.lns[:, 4:5], in1=P.lns[:, 3:4],
            op=ALU.subtract), waits=[t_s5])
        t_s7 = P.op("scalar", lambda e: e.activation(
            out=P.lns[:, 5:6], in_=P.lns[:, 4:5], func=AF.Sqrt,
            bias=P.b_eps[:], scale=1.0), waits=[t_s6, self.t_consts]
            if t_s6 else [t_s6])
        t_s8 = P.op("vector", lambda e: e.reciprocal(
            out=P.lns[:, 6:7], in_=P.lns[:, 5:6]), waits=[t_s7])
        return t_s8

    def pt_batch(self, src_fn, dst_fn, t_in_list, nblk=8):
        P = self
        t_tr = None
        for c in range(nblk):
            t_tr = P.op("tensor",
                        lambda e, c=c: e.transpose(
                            P.PT[:, c * 128:(c + 1) * 128], src_fn(c),
                            P.iden[:]),
                        waits=(t_in_list + [self.pt_free, self.t_iden])
                        if c == 0 else [], inc=(c == nblk - 1))
        t_ev = P.op("scalar", lambda e: e.activation(
            out=dst_fn(), in_=P.PT[:, 0:nblk * 128]
            .rearrange("p (c q) -> p c q", q=128), func=AF.Copy),
            waits=[t_tr])
        self.pt_free = t_ev
        return t_ev

    def ph_a(self, r):
        P = self
        t_cos = P.op("sync", lambda e, r=r: e.dma_start(
            out=P.f32(O_COS, 12 * 32).rearrange("p (i q) -> p i q", q=32),
            in_=P.cos_in[r].rearrange("i p q -> p i q")), dma="dCOS",
            waits=[self.prev["cos_free"]] +
            (list(self.prev["store"]) if r == 1 else []))
        self.t_cos = P.op("sync", lambda e, r=r: e.dma_start(
            out=P.f32(O_SIN, 12 * 32).rearrange("p (i q) -> p i q", q=32),
            in_=P.sin_in[r].rearrange("i p q -> p i q")), dma="dCOS")
        ht_free = [None, None]
        t_lnT = None
        for i in range(NSRC):
            sl = i % 2
            htv = P.f32(O_HT + sl * 4096, 2048)
            w = [ht_free[sl]]
            if r == 0 and i < 2:
                w.append(self.T_GAMMA)
            if r == 1 and i < 2:
                w.extend(self.prev["store"])
            t_h = P.op("sync", lambda e, htv=htv, i=i, r=r: e.dma_start(
                out=htv, in_=P.h_in[r * RND + i * 128:
                                    r * RND + (i + 1) * 128, :]),
                dma=f"dH{sl}", waits=w)
            self.pump(5)
            t_st = self.ln_stats(htv, t_h, O_QT3,
                                 extra_act_w=list(self.prev["store"])
                                 if (r == 1 and i == 0) else [])
            lnt = P.bf(O_LNB + sl * 2048, 2048)
            t_ln = P.op("vector", lambda e, htv=htv, lnt=lnt: e.tensor_scalar(
                out=lnt, in0=htv, scalar1=P.lns[:, 2:3],
                scalar2=P.lns[:, 6:7], op0=ALU.subtract, op1=ALU.mult),
                waits=[t_st])
            ht_free[sl] = t_ln
            for hb in range(2):
                t_lnT = self.pt_batch(
                    lambda c, lnt=lnt, hb=hb: lnt[:, (hb * 8 + c) * 128:
                                                  (hb * 8 + c + 1) * 128],
                    lambda hb=hb, i=i: P.bf(O_LN1T, NKC * NSRC * 128)
                    .rearrange("p (k t q) -> p k t q", k=NKC, t=NSRC)
                    [:, hb * 8:(hb + 1) * 8, i, :],
                    [t_ln, self.prev["ln1_free"] if i == 0 and hb == 0
                     else None])
        self.T_LN1 = t_lnT

    def ph_b(self, r):
        P = self
        t_wk = P.op("gpsimd", lambda e: e.dma_start(
            out=P.bf(O_WK, 2048).rearrange("p (c q) -> p c q", q=128),
            in_=P.wk_in.rearrange("(c p) o -> p c o", p=128)), dma="dWK",
            waits=[self.prev["wkv_free"]])
        t_wv = P.op("gpsimd", lambda e: e.dma_start(
            out=P.bf(O_WV, 2048).rearrange("p (c q) -> p c q", q=128),
            in_=P.wv_in.rearrange("(c p) o -> p c o", p=128)), dma="dWV",
            waits=[self.prev["wkv_free"]])
        kv_free = [None, None]
        kT_done = getattr(self, "dbg_kv", [None])[0]
        t_v = None
        t_vone = P.op("vector", lambda e: e.memset(
            P.bf(O_VN, NSRC * VST).rearrange("p (i q) -> p i q", q=VST)
            [:, :, 128:129], 1.0),
            waits=[self.prev["att_free"],
                   getattr(self, "dbg_kv", [None, None])[1]])
        self.t_vone = t_vone
        for i in range(NSRC):
            self.pump(5)
            for vsel in (0, 1):
                pc = i % 2
                pcol = 512 if pc == 0 else 1024
                t_mm = None
                for kc in range(NKC):
                    t_mm = P.op("tensor", lambda e, i=i, kc=kc, pcol=pcol,
                                woff=(O_WV if vsel else O_WK): e.matmul(
                                    P.P1[:, pcol:pcol + 128],
                                    P.bf(O_LN1T + (kc * NSRC + i) * 128, 128),
                                    P.bf(woff + kc * 128, 128),
                                    start=(kc == 0), stop=(kc == NKC - 1)),
                                waits=[self.T_LN1, t_wv if vsel else t_wk,
                                       kv_free[pc]] if kc == 0 else [],
                                inc=(kc == NKC - 1))
                if vsel:
                    t_v = P.op("vector", lambda e, i=i, pcol=pcol:
                               e.tensor_copy(out=P.bf(O_VN + i * VST, 128),
                                             in_=P.P1[:, pcol:pcol + 128]),
                               waits=[t_mm, self.prev["att_free"]
                                      if i == 0 else None,
                                      getattr(self, "dbg_kv",
                                              [None, None])[1]
                                      if i == 0 else None])
                    kv_free[pc] = t_v
                else:
                    kev = P.f32(O_KSC, 128)
                    t_e = P.op("vector", lambda e, kev=kev, pcol=pcol:
                               e.tensor_copy(out=kev,
                                             in_=P.P1[:, pcol:pcol + 128]),
                               waits=[t_mm, self.prev["att_free"]
                                      if i == 0 else None])
                    kv_free[pc] = t_e
                    krf = P.f32(O_KSC + 512, 128)
                    t_r = self.rope128(kev, krf, O_KSC + 1024, i, t_e)
                    t_q1 = P.op("scalar", lambda e, krf=krf: e.activation(
                        out=P.f32(O_KSC + 1536, 128), in_=krf, func=AF.Square,
                        accum_out=P.red1[:, 2:3]), waits=[t_r])
                    t_q2 = P.op("vector", lambda e: e.tensor_scalar(
                        out=P.red1[:, 3:4], in0=P.red1[:, 2:3],
                        scalar1=1.0 / HD, scalar2=RMSEPS, op0=ALU.mult,
                        op1=ALU.add), waits=[t_q1])
                    t_q3 = P.op("scalar", lambda e: e.activation(
                        out=P.red1[:, 2:3], in_=P.red1[:, 3:4], func=AF.Sqrt),
                        waits=[t_q2])
                    t_q4 = P.op("vector", lambda e: e.reciprocal(
                        out=P.red1[:, 3:4], in_=P.red1[:, 2:3]), waits=[t_q3])
                    knb = P.bf(O_KSC + 3584, 128)
                    t_q5 = P.op("vector", lambda e, krf=krf, knb=knb:
                                e.scalar_tensor_tensor(
                                    out=knb, in0=krf, scalar=P.red1[:, 3:4],
                                    in1=P.knwb[:], op0=ALU.mult,
                                    op1=ALU.mult), waits=[t_q4, self.t_nsk])
                    t_q6 = P.op("tensor", lambda e, knb=knb: e.transpose(
                        P.PT[:, 896:1024], knb, P.iden[:]),
                        waits=[t_q5, kT_done or self.pt_free, self.t_iden])
                    kT_done = P.op("scalar", lambda e, i=i: e.activation(
                        out=P.bf(O_KT + i * 128, 128), in_=P.PT[:, 896:1024],
                        func=AF.Copy), waits=[t_q6])
        self.pt_free = kT_done
        self.T_KT = kT_done
        self.T_V = t_v

    def rope128(self, src, dst, scr, i, t_in):
        P = self
        c = P.f32(O_COS + i * 64, 32)
        s = P.f32(O_SIN + i * 64, 32)
        t1 = P.f32(scr, 32)
        t2 = P.f32(scr + 64, 32)
        ta = P.op("vector", lambda e: e.tensor_tensor(
            out=t1, in0=src[:, 0:32], in1=c, op=ALU.mult),
            waits=[t_in, self.t_cos])
        tb = P.op("vector", lambda e: e.tensor_tensor(
            out=t2, in0=src[:, 32:64], in1=s, op=ALU.mult), waits=[ta])
        tc = P.op("vector", lambda e: e.tensor_tensor(
            out=dst[:, 0:32], in0=t1, in1=t2, op=ALU.subtract), waits=[tb])
        td = P.op("vector", lambda e: e.tensor_tensor(
            out=t1, in0=src[:, 0:32], in1=s, op=ALU.mult), waits=[tc])
        te = P.op("vector", lambda e: e.tensor_tensor(
            out=t2, in0=src[:, 32:64], in1=c, op=ALU.mult), waits=[td])
        tf = P.op("vector", lambda e: e.tensor_tensor(
            out=dst[:, 32:64], in0=t1, in1=t2, op=ALU.add), waits=[te])
        tg = P.op("vector", lambda e: e.tensor_copy(
            out=dst[:, 64:128], in_=src[:, 64:128]), waits=[tf])
        return tg

    def ph_c(self, r):
        P = self
        t_wqu = P.op("gpsimd", lambda e: e.dma_start(
            out=P.bf(O_WQU, 8192).rearrange("p (c q) -> p c q", q=2048),
            in_=P.wqu_in.rearrange("(c p) o -> p c o", p=128)), dma="dWQU",
            waits=[self.prev["wqu_free"]])
        wqc_free = [None, None]
        t_q1T = None
        q1_pe = None
        for th in range(2):
            kc_tok = {}
            for kc in range(NKC):
                sl = kc % 2
                t_w = P.op("gpsimd", lambda e, kc=kc, sl=sl: e.dma_start(
                    out=P.bf(O_WQC + sl * 512, 512),
                    in_=P.wqc_in[kc * 128:(kc + 1) * 128, :]), dma=f"dQC{sl}",
                    waits=[wqc_free[sl],
                           self.T_LN1 if (th == 0 and kc < 2) else None])
                for oc in range(4):
                    w = []
                    if oc == 0:
                        w = [t_w, self.T_LN1]
                        if th == 1 and kc == 0:
                            w.append(t_q1T)
                        if th == 0 and kc == 0:
                            w.append(self.T_GAMMA)
                            w.append(self.tok("vector"))
                    tt = P.op("tensor", lambda e, th=th, kc=kc, oc=oc, sl=sl:
                              e.matmul(
                                  P.P0[:, oc * 512:(oc + 1) * 512],
                                  P.bf(O_WQC + sl * 512 + oc * 128, 128),
                                  P.bf(O_LN1T + (kc * NSRC + 4) * 128
                                       + th * 512, 512),
                                  start=(kc == 0), stop=(kc == NKC - 1)),
                              waits=w, inc=(oc == 3))
                    if oc == 3:
                        kc_tok[kc] = tt
                        wqc_free[sl] = tt
                        if kc == NKC - 1:
                            q1_pe = tt
            for oc in range(4):
                t_q1T = P.op("scalar", lambda e, th=th, oc=oc: e.activation(
                    out=P.bf(O_Q1T + oc * 1024 + th * 512, 512),
                    in_=P.P0[:, oc * 512:(oc + 1) * 512], func=AF.Copy),
                    waits=[q1_pe])
        T_Q1 = t_q1T
        qps = [(self.P0, 0), (self.P0, 1024), (self.P1, 0)]
        q_ev = [None, None, self.tok("vector")]
        rope_done = None
        t_qTn = None
        for j in range(NT):
            self.pump(10)
            evs = []
            for dh in range(2):
                s3 = (2 * j + dh) % 3
                ps, pco = qps[s3]
                t_m = None
                for kc in range(4):
                    for nn in range(2):
                        t_m = P.op("tensor", lambda e, j=j, dh=dh, kc=kc,
                                   nn=nn, ps=ps, pco=pco: e.matmul(
                                       ps[:, pco + nn * 512:
                                          pco + (nn + 1) * 512],
                                       P.bf(O_Q1T + kc * 1024 + j * 128, 128),
                                       P.bf(O_WQU + kc * 2048 + dh * 1024
                                            + nn * 512, 512),
                                       start=(kc == 0), stop=(kc == 3)),
                                   waits=[T_Q1, t_wqu, q_ev[s3]]
                                   if (kc == 0 and nn == 0) else [],
                                   inc=(kc == 3 and nn == 1))
                t_e = P.op("vector", lambda e, dh=dh, ps=ps, pco=pco:
                           e.tensor_copy(out=P.f32(O_QEV + dh * 2048, 1024),
                                         in_=ps[:, pco:pco + 1024]),
                           waits=[t_m, rope_done])
                q_ev[s3] = t_e
                evs.append(t_e)
            rope_done = self.rope_norm_q(j, evs)
            if self.dbg is not None:
                tq = P.op("sync", lambda e, j=j, r=r: e.dma_start(
                    out=self.dbg3[r, j], in_=P.bf(O_QNB, 2048)),
                    dma="dDQ", waits=[rope_done])
                self.st_final.append(tq)
                rope_done = tq
            for hb in range(2):
                t_qTn = self.pt_batch(
                    lambda c, hb=hb: P.bf(O_QNB + (hb * 8 + c) * 128, 128),
                    lambda hb=hb, j=j: P.bf(O_QTN, NT * H * 128)
                    .rearrange("p (t k q) -> p t k q", t=NT, k=H)
                    [:, j, hb * 8:(hb + 1) * 8, :],
                    [rope_done, self.prev["qtn_free"]
                     if (j == 0 and hb == 0) else None])
        self.T_QTN = t_qTn

    def rope_norm_q(self, j, evs):
        P = self
        qev3 = P.f32(O_QEV, 2048).rearrange("p (k q) -> p k q", q=128)
        qrf3 = P.f32(O_QRF, 2048).rearrange("p (k q) -> p k q", q=128)
        ci = 4 + j
        c = (P.f32(O_COS + ci * 64, 32).unsqueeze(1)
             .broadcast_to([128, H, 32]))
        s = (P.f32(O_SIN + ci * 64, 32).unsqueeze(1)
             .broadcast_to([128, H, 32]))
        t1 = P.f32(O_QT3, 512).rearrange("p (k q) -> p k q", q=32)
        t2 = P.f32(O_QT3 + 1024, 512).rearrange("p (k q) -> p k q", q=32)
        x1 = qev3[:, :, 0:32]
        x2 = qev3[:, :, 32:64]
        w0 = list(evs) + [self.t_cos]
        ta = P.op("vector", lambda e: e.tensor_tensor(out=t1, in0=x1, in1=c,
                                                      op=ALU.mult), waits=w0)
        tb = P.op("vector", lambda e: e.tensor_tensor(out=t2, in0=x2, in1=s,
                                                      op=ALU.mult), waits=[ta])
        tc = P.op("vector", lambda e: e.tensor_tensor(
            out=qrf3[:, :, 0:32], in0=t1, in1=t2, op=ALU.subtract),
            waits=[tb])
        td = P.op("vector", lambda e: e.tensor_tensor(out=t1, in0=x1, in1=s,
                                                      op=ALU.mult), waits=[tc])
        te = P.op("vector", lambda e: e.tensor_tensor(out=t2, in0=x2, in1=c,
                                                      op=ALU.mult), waits=[td])
        tf = P.op("vector", lambda e: e.tensor_tensor(
            out=qrf3[:, :, 32:64], in0=t1, in1=t2, op=ALU.add), waits=[te])
        tg = P.op("vector", lambda e: e.tensor_copy(
            out=qrf3[:, :, 64:128], in_=qev3[:, :, 64:128]), waits=[tf])
        th1 = P.op("scalar", lambda e: e.activation(
            out=P.f32(O_QT3, 1024), in_=P.f32(O_QRF, 1024), func=AF.Square),
            waits=[tg])
        th2 = P.op("scalar", lambda e: e.activation(
            out=P.f32(O_QT3 + 2048, 1024), in_=P.f32(O_QRF + 2048, 1024),
            func=AF.Square), waits=[tg, th1])
        ti1 = P.op("vector", lambda e: e.tensor_reduce(
            out=P.rmsS[:, 0:8], in_=P.f32(O_QT3, 1024)
            .rearrange("p (k q) -> p k q", q=128), axis=AX, op=ALU.add),
            waits=[th1, tg])
        ti2 = P.op("vector", lambda e: e.tensor_reduce(
            out=P.rmsS[:, 8:16], in_=P.f32(O_QT3 + 2048, 1024)
            .rearrange("p (k q) -> p k q", q=128), axis=AX, op=ALU.add),
            waits=[th2])
        tj = P.op("vector", lambda e: e.tensor_scalar(
            out=P.rmsS[:, 0:16], in0=P.rmsS[:, 0:16], scalar1=1.0 / HD,
            scalar2=RMSEPS, op0=ALU.mult, op1=ALU.add), waits=[ti1, ti2])
        tk = P.op("scalar", lambda e: e.activation(
            out=P.rmsS[:, 16:32], in_=P.rmsS[:, 0:16], func=AF.Sqrt),
            waits=[tj])
        tl = P.op("vector", lambda e: e.reciprocal(
            out=P.rmsS[:, 0:16], in_=P.rmsS[:, 16:32]), waits=[tk])
        tm = P.op("vector", lambda e: e.tensor_scalar_mul(
            out=P.rmsS[:, 0:16], in0=P.rmsS[:, 0:16], scalar1=SCALE),
            waits=[tl])
        rstdb = P.rmsS[:, 0:16].unsqueeze(2).broadcast_to([128, H, 128])
        qnwb3 = P.qnwb[:].unsqueeze(1).broadcast_to([128, H, 128])
        qt3 = P.f32(O_QT3, 2048).rearrange("p (k q) -> p k q", q=128)
        tn = P.op("vector", lambda e: e.tensor_tensor(
            out=qt3, in0=qrf3, in1=rstdb, op=ALU.mult), waits=[tm, th2])
        to = P.op("vector", lambda e: e.tensor_tensor(
            out=P.bf(O_QNB, 2048).rearrange("p (k q) -> p k q", q=128),
            in0=qt3, in1=qnwb3, op=ALU.mult), waits=[tn, self.t_qnb])
        return to

    def ph_d(self, r):
        P = self
        mk_free = list(self.prev["mask_free"])
        tv0 = self.tok("vector")
        psA_free = [tv0, tv0]
        st = {"psB_free": [tv0, tv0], "psC_free": [tv0, tv0],
              "anat_free": [self.prev["e_done"], self.prev["e_done"]],
              "cmT_rd": None, "crow_rd": None, "t_dn": [None, None],
              "att_ev": None, "t_last": None, "t_sc": {}, "t_cm": {},
              "crow_tok": {}, "p2sc": {}, "p2no": {}, "exp_tok": {}}

        def pass1_head(j, hh):
            jp = j % 2
            psA = P.P0[:, 0:640]
            P.op("tensor", lambda e, j=j, hh=hh, psA=psA: e.matmul(
                psA[:, 0:512], P.bf(O_QTN + (j * H + hh) * 128, 128),
                P.bf(O_KT + j * 128, 512), start=True, stop=True),
                waits=[self.T_QTN, self.T_KT, psA_free[0]], inc=False)
            t_s = P.op("tensor", lambda e, j=j, hh=hh, psA=psA: e.matmul(
                psA[:, 512:640], P.bf(O_QTN + (j * H + hh) * 128, 128),
                P.bf(O_KT + (j + 4) * 128, 128), start=True, stop=True))
            t_r = P.op("vector", lambda e, psA=psA: e.reduce_max(
                out=P.red1[:, 0:1], in_=psA, axis=AX, negate=True),
                waits=[t_s])
            psA_free[0] = t_r
            t_c = P.op("vector", lambda e, hh=hh, jp=jp: e.tensor_tensor(
                out=P.cmaxb[:, jp * H + hh: jp * H + hh + 1],
                in0=P.red1[:, 0:1], in1=P.nsinkb[:, hh:hh + 1],
                op=ALU.min),
                waits=[t_r] + ([st["cmT_rd"]] if hh == 0 else []))
            st["t_cm"][j] = P.op("vector", lambda e, hh=hh, jp=jp:
                                 e.tensor_copy(
                                     out=P.cmaxf[:, jp * H + hh:
                                                 jp * H + hh + 1],
                                     in_=P.cmaxb[:, jp * H + hh:
                                                 jp * H + hh + 1]),
                                 waits=[t_c])

        def crow(j):
            jp = j % 2
            t_t = P.op("tensor", lambda e, jp=jp: e.transpose(
                P.PT[0:16, 896:1024], P.cmaxb[:, jp * H:(jp + 1) * H],
                P.iden[:]),
                waits=[st["t_cm"][j], self.pt_free, st["crow_rd"]])
            t_te = P.op("scalar", lambda e: e.activation(
                out=P.AR[0:16, O_CMT:O_CMT + 128], in_=P.PT[0:16, 896:1024],
                func=AF.Copy),
                waits=[t_t] + (list(self.prev["store"])
                               if (r == 1 and j == 0) else []))
            st["cmT_rd"] = t_te
            self.pt_free = t_te
            st["crow_tok"][j] = P.op("scalar", lambda e: e.dma_start(
                out=P.AR[0:1, O_CRW:O_CRW + 16 * 640]
                .rearrange("p (c r q) -> p c r q", r=5, q=128),
                in_=P.AR[0:16, O_CMT:O_CMT + 128].unsqueeze(1).unsqueeze(1)
                .broadcast_to([16, 1, 5, 128]).squeeze(1)), dma="dCRW",
                waits=[t_te])

        def pass2_scores(j, hh):
            msl = j % 2
            t_cr = st["crow_tok"][j]
            t_mk = st["t_mk"][j]
            par = hh % 2
            psB = P.P1[:, 0:640] if par == 0 else P.P0[:, 1024:1664]
            t_sc = None
            for c in range(4):
                P.op("tensor", lambda e, j=j, hh=hh, c=c, psB=psB:
                     e.matmul(psB[:, c * 128:(c + 1) * 128],
                              P.bf(O_KT + (j + c) * 128, 128),
                              P.bf(O_QTN + (j * H + hh) * 128, 128),
                              start=(c == 0), stop=False),
                     waits=[t_cr, t_mk, st["psB_free"][par]]
                     if c == 0 else [], inc=False)
            P.op("tensor", lambda e, hh=hh, psB=psB: e.matmul(
                psB[:, 0:512], P.ones_r[:],
                P.AR[0:1, O_CRW + hh * 640: O_CRW + hh * 640 + 512],
                start=False, stop=False), inc=False)
            P.op("tensor", lambda e, msl=msl, psB=psB: e.matmul(
                psB[:, 0:512], P.iden[:],
                P.bf(O_MKS + msl * 640, 512),
                start=False, stop=True), inc=False)
            P.op("tensor", lambda e, j=j, hh=hh, psB=psB: e.matmul(
                psB[:, 512:640], P.bf(O_KT + (j + 4) * 128, 128),
                P.bf(O_QTN + (j * H + hh) * 128, 128),
                start=True, stop=False), inc=False)
            P.op("tensor", lambda e, hh=hh, psB=psB: e.matmul(
                psB[:, 512:640], P.ones_r[:],
                P.AR[0:1, O_CRW + hh * 640 + 512: O_CRW + hh * 640 + 640],
                start=False, stop=False), inc=False)
            t_sc = P.op("tensor", lambda e, msl=msl, psB=psB: e.matmul(
                psB[:, 512:640], P.iden[:],
                P.bf(O_MKS + msl * 640 + 512, 128),
                start=False, stop=True), inc=True)
            t_ex = P.op("scalar", lambda e, par=par, psB=psB:
                        e.activation(out=P.bf(O_PRB + par * 640, 640),
                                     in_=psB, func=AF.Exp),
                        waits=[t_sc] + (list(self.prev["store"])
                                        if (r == 1 and j == 0
                                            and hh < 2) else []))
            st["exp_tok"][hh] = t_ex
            st["p2sc"][j] = t_sc

        def pass2_attn(j, hh):
            jp = j % 2
            par = hh % 2
            psAt = (P.P1[:, 768:897] if par == 0 else P.P0[:, 1792:1921])
            psDen = (P.P1[:, 896:897] if par == 0 else P.P0[:, 1920:1921])
            t_ex = st["exp_tok"][hh]
            t_at = None
            for c in range(NCH):
                pch = P.bf(O_PRB + par * 640 + c * 128, 128)
                t_at = P.op("tensor", lambda e, pch=pch, psAt=psAt, j=j,
                            c=c: e.matmul(
                                psAt[:, 0:129], pch,
                                P.bf(O_VN + (j + c) * VST, 129),
                                start=(c == 0), stop=(c == NCH - 1)),
                            waits=[t_ex, self.T_V, self.t_vone,
                                   st["psC_free"][par]]
                            if c == 0 else [], inc=(c == NCH - 1))
            dso = par * 2
            t_es = P.op("scalar", lambda e, hh=hh, jp=jp, dso=dso:
                        e.activation(
                            out=P.dstat[:, dso:dso + 1],
                            in_=P.cmaxf[:, jp * H + hh: jp * H + hh + 1],
                            func=AF.Exp, bias=P.sinkb[:, hh:hh + 1],
                            scale=1.0),
                        waits=[self.t_sink, st["t_cm"][j],
                               st["t_dn"][par]])
            t_dn = P.op("vector", lambda e, psDen=psDen, dso=dso:
                        e.tensor_tensor(
                            out=P.dstat[:, dso + 1:dso + 2], in0=psDen,
                            in1=P.dstat[:, dso:dso + 1],
                            op=ALU.add), waits=[t_at, t_es])
            st["t_dn"][par] = t_dn
            t_iv = P.op("vector", lambda e, dso=dso: e.reciprocal(
                out=P.dstat[:, dso:dso + 1],
                in_=P.dstat[:, dso + 1:dso + 2]), waits=[t_dn])
            t_no = P.op("vector", lambda e, hh=hh, j=j, psAt=psAt, dso=dso:
                        e.tensor_scalar(
                            out=P.bf(O_ANAT + (j % 2) * 2048 + hh * 128,
                                     128),
                            in0=psAt[:, 0:128],
                            scalar1=P.dstat[:, dso:dso + 1],
                            scalar2=None, op0=ALU.mult),
                        waits=[t_iv] + ([st["anat_free"][j % 2]]
                                        if hh == 0 else []))
            st["psC_free"][par] = t_no
            st["psB_free"][par] = t_no
            st["p2no"][j] = t_no

        def pass2_tail(j):
            msl = j % 2
            t_no = st["p2no"][j]
            t_sc = st["p2sc"][j]
            for hb in range(2):
                st["t_last"] = self.pt_batch(
                    lambda c, hb=hb, j=j: P.bf(
                        O_ANAT + (j % 2) * 2048 + (hb * 8 + c) * 128, 128),
                    lambda hb=hb, j=j: P.bf(O_ATT, H * 1024)
                    .rearrange("p (k t q) -> p k t q", k=H, t=NT)
                    [:, hb * 8:(hb + 1) * 8, j, :],
                    [t_no, self.prev["att_free"] if (j == 0 and hb == 0)
                     else None])
            st["anat_free"][j % 2] = st["t_last"]
            mk_free[msl] = t_sc
            st["crow_rd"] = t_sc

        st["t_mk"] = {}
        for j in range(NT):
            msl = j % 2
            st["t_mk"][j] = P.op("sync", lambda e, r=r, j=j, msl=msl:
                                 e.dma_start(
                                     out=P.bf(O_MKS + msl * 640, 640)
                                     .rearrange("p (c q) -> p c q", q=128),
                                     in_=P.mask_in[r, j]
                                     .rearrange("c p q -> p c q")),
                                 dma=f"dMK{msl}",
                                 waits=[mk_free[msl], self.T_QTN] +
                                 (list(self.prev["store"])
                                  if (r == 1 and j < 2) else []))
            self.pump(12)
            for hh in range(H):
                pass1_head(j, hh)
                if j > 0:
                    pass2_scores(j - 1, hh)
                    if hh > 0:
                        pass2_attn(j - 1, hh - 1)
            if j > 0:
                pass2_attn(j - 1, H - 1)
                pass2_tail(j - 1)
            crow(j)
        for hh in range(H):
            pass2_scores(NT - 1, hh)
            if hh > 0:
                pass2_attn(NT - 1, hh - 1)
        pass2_attn(NT - 1, H - 1)
        pass2_tail(NT - 1)
        self.prev["mask_free"] = mk_free
        self.T_ATT = st["t_last"]
        self.prev["c_done"] = st["t_last"]

    def ph_e(self, r):
        P = self
        if self.dbg is not None:
            t1 = P.op("sync", lambda e, r=r: e.dma_start(
                out=self.dbg2[r], in_=P.bf(O_ATT, H * RND)), dma="dDA",
                waits=[self.T_ATT])
            t2 = P.op("sync", lambda e, r=r: e.dma_start(
                out=self.dbg4[r], in_=P.bf(O_KT, NSRC * 128)), dma="dDK",
                waits=[self.T_ATT])
            t3 = P.op("sync", lambda e, r=r: e.dma_start(
                out=self.dbg5[r].rearrange("p (i q) -> p i q", q=129),
                in_=P.bf(O_VN, NSRC * VST)
                .rearrange("p (i q) -> p i q", q=VST)[:, :, 0:129]),
                dma="dDV", waits=[self.T_ATT, t1, t2])
            self.st_final.extend([t1, t2, t3])
            self.dbg_kv = [t3, t3]
            self.T_ATT = t3
        if self.T_GAMMA is None:
            self.gamma_finalize()
        x_ld = [None] * NT
        for j in range(NT):
            x_ld[j] = P.op("sync", lambda e, j=j, r=r: e.dma_start(
                out=P.f32(O_X + j * 4096, 2048),
                in_=P.h_in[HALO + r * RND + j * 128:
                           HALO + r * RND + (j + 1) * 128, :]),
                dma=f"dX{j}",
                waits=[self.T_ATT if (j < 4 or j == 6) else
                       (self.T_QTN if j < 6 else self.T_GAMMA),
                       self.T_QTN if j == 7 else None,
                       self.prev["store"][j]])
        w2_free = [None, None, None]
        w1g_free = None
        e_add = {}
        y1ps_free = [self.T_ATT, self.T_ATT, self.T_ATT]
        t_y1_last = None

        pairs = [(g, oq) for g in range(4) for oq in range(4)]
        w2_free = [None, None, None, None]
        w2tok = {}

        def emit_w2(pi, extra=None):
            toks = []
            for hv in range(2):
                s3 = (pi * 2 + hv) % 4
                toks.append(P.op(
                    "gpsimd", lambda e, pi=pi, hv=hv, s3=s3: e.dma_start(
                        out=P.bf(O_W2QS[s3], 4096)
                        .rearrange("p (c q) -> p c q", q=512),
                        in_=P.ow2_in.rearrange("(c p) o -> c p o", p=128)
                        [pairs[pi][0] * 16 + hv * 8:
                         pairs[pi][0] * 16 + hv * 8 + 8, :,
                         pairs[pi][1] * 512:(pairs[pi][1] + 1) * 512]
                        .rearrange("c p o -> p c o")),
                    dma=f"dW2_{s3}",
                    waits=[w2_free[s3], extra,
                           self.T_ATT if (O_W2QS[s3] == 0 and
                                          w2_free[s3] is None) else None]))
            w2tok[pi] = toks

        emit_w2(0)
        for g in range(4):
            t_w1 = P.op("gpsimd", lambda e, g=g: e.dma_start(
                out=P.bf(O_W1G, 8192).rearrange("p (c q) -> p c q", q=2048),
                in_=P.ow1_in[g].rearrange("(c p) o -> p c o", p=128)),
                dma="dW1G",
                waits=[w1g_free, self.T_ATT if g == 0 else None])
            for oc in range(NKC):
                for th in range(2):
                    s3y = (2 * oc + th) % 3
                    pco = s3y * 512
                    t_m = None
                    for kc in range(4):
                        t_m = P.op("tensor", lambda e, g=g, oc=oc, th=th,
                                   kc=kc, pco=pco: e.matmul(
                                       P.P1[:, pco:pco + 512],
                                       P.bf(O_W1G + kc * 2048 + oc * 128,
                                            128),
                                       P.bf(O_ATT + (g * 4 + kc) * 1024
                                            + th * 512, 512),
                                       start=(kc == 0), stop=(kc == 3)),
                                   waits=[t_w1, y1ps_free[s3y],
                                          self.T_GAMMA if (g == 0 and oc == 0
                                                           and th == 0)
                                          else None,
                                          e_add.get(7) if (g > 0 and oc == 0
                                                           and th == 0)
                                          else None]
                                   if kc == 0 else [], inc=(kc == 3))
                    t_y1 = P.op("scalar", lambda e, oc=oc, th=th, pco=pco:
                                e.activation(
                                    out=P.bf(O_Y1 + oc * 1024 + th * 512,
                                             512),
                                    in_=P.P1[:, pco:pco + 512],
                                    func=AF.Copy), waits=[t_m])
                    y1ps_free[s3y] = t_y1
                    t_y1_last = t_y1
            w1g_free = t_y1_last
            T_Y1 = t_y1_last
            if g == 0:
                emit_w2(1, extra=T_Y1)
            for outq in range(4):
                pi = g * 4 + outq
                hv_t = w2tok[pi]
                for j in range(NT):
                    t_m = None
                    for kc2 in range(NKC):
                        hv = kc2 // 8
                        s3 = (pi * 2 + hv) % 4
                        w = []
                        if kc2 == 0:
                            w = [hv_t[0], T_Y1, x_ld[j],
                                 e_add.get((j - 4) % 8)]
                        elif kc2 == 8:
                            w = [hv_t[1]]
                        t_m = P.op("tensor", lambda e, j=j, kc2=kc2, s3=s3:
                                   e.matmul(
                                       P.P0[:, (j % 4) * 512:
                                            (j % 4) * 512 + 512],
                                       P.bf(O_Y1 + kc2 * 1024 + j * 128,
                                            128),
                                       P.bf(O_W2QS[s3] + (kc2 % 8) * 512,
                                            512),
                                       start=(kc2 == 0),
                                       stop=(kc2 == NKC - 1)),
                                   waits=w, inc=(kc2 == NKC - 1))
                    e_add[j] = P.op("vector", lambda e, j=j, outq=outq:
                                    e.tensor_tensor(
                                        out=P.f32(O_X + j * 4096
                                                  + outq * 1024, 512),
                                        in0=P.f32(O_X + j * 4096
                                                  + outq * 1024, 512),
                                        in1=P.P0[:, (j % 4) * 512:
                                                 (j % 4) * 512 + 512],
                                        op=ALU.add),
                                    waits=[t_m, e_add.get(j)])
                pe_now = self.tok("tensor")
                w2_free[(pi * 2) % 4] = pe_now
                w2_free[(pi * 2 + 1) % 4] = pe_now
                if pi + 2 < len(pairs):
                    emit_w2(pi + 2)
        self.e_add = e_add
        self.T_E = e_add[7]
        self.prev["e_done"] = self.T_E

    def ph_f(self, r):
        P = self
        self.dbg_toks = {}
        if self.dbg is not None:
            for j in range(NT):
                t = P.op("sync", lambda e, j=j, r=r: e.dma_start(
                    out=self.dbg[r * RND + j * 128:r * RND + (j + 1) * 128, :],
                    in_=P.f32(O_X + j * 4096, 2048)), dma=f"dDB{j}",
                    waits=[self.e_add[jj] for jj in range(NT)])
                self.st_final.append(t)
                self.dbg_toks[j] = t
        t_ln2 = None
        lnb2_free = None
        for j in range(NT):
            xv = P.f32(O_X + j * 4096, 2048)
            t_st = self.ln_stats(xv, self.e_add[j], O_LNB2,
                                 extra_act_w=[lnb2_free])
            lnt = P.bf(O_LNB2 + 2048, 2048)
            t_ln = P.op("vector", lambda e, xv=xv, lnt=lnt: e.tensor_scalar(
                out=lnt, in0=xv, scalar1=P.lns[:, 2:3],
                scalar2=P.lns[:, 6:7], op0=ALU.subtract, op1=ALU.mult),
                waits=[t_st, lnb2_free])
            for hb in range(2):
                t_ln2 = self.pt_batch(
                    lambda c, hb=hb, lnt=lnt: lnt[:, (hb * 8 + c) * 128:
                                                  (hb * 8 + c + 1) * 128],
                    lambda hb=hb, j=j: P.bf(O_LN2T, NKC * RND)
                    .rearrange("p (k t q) -> p k t q", k=NKC, t=NT)
                    [:, hb * 8:(hb + 1) * 8, j, :],
                    [t_ln, self.T_E if (j == 0 and hb == 0) else None])
            lnb2_free = t_ln2
        self.T_LN2 = t_ln2

    def ph_g(self, r):
        P = self
        nwin = (NW + SG - 1) // SG
        stg_free = [None, None]
        sq_free = [None, None, None]
        self._s1_rd = [None, None]
        self._s1b_rd = None
        self._s1_ping = 0
        self._qd_n = 0
        w1t_free = [None, None]
        w2t_free = [None, None]
        w3_free = [self.T_LN2] * 7
        stores = [None] * NT
        x1ps_rd = [None, None]
        x2ps_rd = None
        out_guard = [self.e_add[7]]
        hid_rd = [None] * SG
        out_ev = {}
        for j in range(NT):
            for dh in range(2):
                out_ev[(j, dh)] = self.dbg_toks.get(j, self.e_add[j])
        self.gst = getattr(self, "gst", 0)

        def quant_half(src_off, wsel, dst, t_ld, extra=()):
            if wsel == 1:
                ta = P.op("vector", lambda e: e.tensor_scalar(
                    out=P.bf(O_S1B, 1024), in0=P.f32(src_off, 1024),
                    scalar1=P.invg[:, 1:2], scalar2=0.5,
                    op0=ALU.mult, op1=ALU.is_ge),
                    waits=[t_ld, self.T_GAMMA] + list(extra) +
                    [self._s1b_rd])
                tb = P.op("vector", lambda e, dst=dst: e.tensor_scalar(
                    out=dst, in0=P.f32(src_off, 1024),
                    scalar1=P.invg[:, 3:4], scalar2=0.5,
                    op0=ALU.mult, op1=ALU.is_ge), waits=[ta])
                tc = P.op("vector", lambda e, dst=dst: e.tensor_tensor(
                    out=dst, in0=P.bf(O_S1B, 1024), in1=dst,
                    op=ALU.subtract), waits=[tb])
                self._s1b_rd = tc
                return tc, tb
            pp = self._s1_ping
            self._s1_ping ^= 1
            s1o = O_S1 if pp == 0 else O_S1C
            t1 = P.op("scalar", lambda e, s1o=s1o: e.activation(
                out=P.bf(s1o, 1024), in_=P.f32(src_off, 1024), func=AF.Sign,
                bias=P.b_m05[:], scale=P.invg[:, wsel:wsel + 1]),
                waits=[t_ld, self.T_GAMMA, self.t_consts] + list(extra) +
                [self._s1_rd[pp]])
            t2 = P.op("scalar", lambda e, dst=dst: e.activation(
                out=dst, in_=P.f32(src_off, 1024), func=AF.Sign,
                bias=P.b_p05[:], scale=P.invg[:, wsel:wsel + 1]), waits=[t1])
            t3 = P.op("vector", lambda e, dst=dst, s1o=s1o: e.tensor_tensor(
                out=dst, in0=P.bf(s1o, 1024), in1=dst,
                op=ALU.add), waits=[t2])
            self._s1_rd[pp] = t3
            return t3, t2


        def prep_w12(c, wsel, fw_in, dst_base, free_list, war):
            sl = c % 2
            dst = dst_base + sl * 2048
            lds, qts, qsls = [], [], []
            for hf in range(2):
                ssl = self.gst % 2
                self.gst += 1
                t_ld = P.op("sync", lambda e, c=c, hf=hf, ssl=ssl,
                            fw_in=fw_in: e.dma_start(
                                out=P.f32(O_FST + ssl * 2048, 1024),
                                in_=fw_in[c * 128:(c + 1) * 128,
                                          hf * 1024:(hf + 1) * 1024]),
                            dma=f"dF{ssl}", waits=[stg_free[ssl], war])
                lds.append((t_ld, ssl))
            for hf in range(2):
                t_ld, ssl = lds[hf]
                qsl = self._qd_n % NQD
                self._qd_n += 1
                qsls.append(qsl)
                t_q, t_sf = quant_half(O_FST + ssl * 2048, wsel,
                                       P.bf(O_QD + qsl * 1024, 1024), t_ld,
                                       extra=[sq_free[qsl]])
                stg_free[ssl] = t_sf
                qts.append(t_q)
            toks = []
            for hf in range(2):
                qsl = qsls[hf]
                t_tr = P.op("scalar", lambda e, dst=dst, hf=hf, qsl=qsl:
                            e.dma_start(
                                out=P.bf(dst + hf * 1024, 1024)
                                .rearrange("p (cc q) -> p cc q", q=128),
                                in_=P.bf(O_QD + qsl * 1024, 1024)
                                .rearrange("p (cc q) -> p cc q", q=128),
                                transpose=True),
                            dma=f"dT{qsl}",
                            waits=[qts[hf], free_list[sl] if hf == 0
                                   else None])
                sq_free[qsl] = t_tr
                toks.append(t_tr)
            return dst, toks

        def prep_w3(c, war):
            sl = c % 7
            dst = O_W3A + sl * 2048
            lds, qts, qsls = [], [], []
            for hf in range(2):
                ssl = self.gst % 2
                self.gst += 1
                t_ld = P.op("sync", lambda e, c=c, hf=hf, ssl=ssl:
                            e.dma_start(
                                out=P.f32(O_FST + ssl * 2048, 1024)
                                .rearrange("p (m q) -> p m q", q=128),
                                in_=P.fw3_in
                                .rearrange("(m p) (cc q) -> p m cc q",
                                           p=128, q=128)
                                [:, hf * 8:(hf + 1) * 8, c, :]),
                            dma=f"dF{ssl}", waits=[stg_free[ssl], war])
                lds.append((t_ld, ssl))
            for hf in range(2):
                t_ld, ssl = lds[hf]
                qsl = self._qd_n % NQD
                self._qd_n += 1
                qsls.append(qsl)
                t_q, t_sf = quant_half(O_FST + ssl * 2048, 2,
                                       P.bf(O_QD + qsl * 1024, 1024), t_ld,
                                       extra=[sq_free[qsl]])
                stg_free[ssl] = t_sf
                qts.append(t_q)
            toks = []
            for hf in range(2):
                qsl = qsls[hf]
                t_tr = P.op("scalar", lambda e, dst=dst, hf=hf, qsl=qsl:
                            e.dma_start(
                                out=P.bf(dst + hf * 1024, 1024)
                                .rearrange("p (m q) -> p m q", q=128),
                                in_=P.bf(O_QD + qsl * 1024, 1024)
                                .rearrange("p (m q) -> p m q", q=128),
                                transpose=True),
                            dma=f"dT{qsl}",
                            waits=[qts[hf], w3_free[sl] if hf == 0
                                   else None])
                sq_free[qsl] = t_tr
                toks.append(t_tr)
            return dst, toks

        cglob = 0
        preps = {}

        def prep(c):
            war0 = self.T_E if c == 0 else None
            d1, tk1 = prep_w12(c, 0, P.fw1_in, O_W1T, w1t_free, war0)
            d2, tk2 = prep_w12(c, 1, P.fw2_in, O_W2T, w2t_free, None)
            d3, tk3 = prep_w3(c, self.T_LN2 if c == 0 else None)
            preps[c] = (d1, tk1, d2, tk2, d3, tk3)

        prep(0)
        for win in range(nwin):
            ncw = min(SG, NW - win * SG)
            hid_t = [None] * ncw
            w3d = [None] * ncw
            w3tk = [None] * ncw
            for cw in range(ncw):
                c = win * SG + cw
                d1, tk1, d2, tk2, w3d[cw], tk3 = preps.pop(c)
                w3tk[cw] = tk3
                cpar = c % 2
                x1ps = P.P1[:, 0:1024] if cpar == 0 else P.P0[:, 0:1024]
                t_x1 = None
                for kc in range(NKC):
                    for nn in range(2):
                        t_x1 = P.op("tensor", lambda e, d1=d1, kc=kc, nn=nn,
                                    x1ps=x1ps: e.matmul(
                                        x1ps[:, nn * 512:(nn + 1) * 512],
                                        P.bf(d1 + kc * 128, 128),
                                        P.bf(O_LN2T + kc * 1024 + nn * 512,
                                             512),
                                        start=(kc == 0),
                                        stop=(kc == NKC - 1)),
                                    waits=[tk1[0], tk1[1], self.T_LN2,
                                           x1ps_rd[cpar], self.T_E,
                                           out_guard[0] if cpar else None]
                                    if (kc == 0 and nn == 0) else [],
                                    inc=(kc == NKC - 1 and nn == 1))
                w1t_free[c % 2] = t_x1
                if c + 1 < NW:
                    prep(c + 1)
                t_sg = P.op("scalar", lambda e, x1ps=x1ps: e.activation(
                    out=P.f32(O_SGM, 1024), in_=x1ps,
                    func=AF.Sigmoid, scale=0.5),
                    waits=[t_x1, hid_rd[cw], self.T_LN2, x2ps_rd])
                x1ps_rd[cpar] = t_sg
                t_hd = None
                for nn in range(2):
                    t_x2 = None
                    for kc in range(NKC):
                        t_x2 = P.op("tensor", lambda e, d2=d2, kc=kc, nn=nn:
                                    e.matmul(
                                        P.P1[:, 1024:1536],
                                        P.bf(d2 + kc * 128, 128),
                                        P.bf(O_LN2T + kc * 1024 + nn * 512,
                                             512),
                                        start=(kc == 0),
                                        stop=(kc == NKC - 1)),
                                    waits=[tk2[0], tk2[1], x2ps_rd]
                                    if kc == 0 else [],
                                    inc=(kc == NKC - 1))
                    t_hd = P.op("vector", lambda e, cw=cw, nn=nn:
                                e.tensor_tensor(
                                    out=P.bf(O_HID + cw * 1024 + nn * 512,
                                             512),
                                    in0=P.f32(O_SGM + nn * 1024, 512),
                                    in1=P.P1[:, 1024:1536], op=ALU.mult),
                                waits=[t_x2, t_sg] + ([hid_rd[cw]]
                                                      if nn == 0 else []))
                    x2ps_rd = t_hd
                hid_t[cw] = t_hd
                w2t_free[c % 2] = t_hd
                cglob += 1
            for j in range(NT):
                for dh in range(2):
                    pco = dh * 1024
                    t_m = None
                    for half in range(2):
                        for cw in range(ncw):
                            t_m = P.op(
                                "tensor", lambda e, j=j, cw=cw, w3d=w3d,
                                pco=pco, dh=dh, half=half, ncw=ncw: e.matmul(
                                    P.P0[:, pco + half * 512:
                                         pco + half * 512 + 512],
                                    P.bf(O_HID + cw * 1024 + j * 128, 128),
                                    P.bf(w3d[cw] + dh * 1024 + half * 512,
                                         512),
                                    start=(cw == 0), stop=(cw == ncw - 1)),
                                waits=[hid_t[cw], w3tk[cw][0], w3tk[cw][1],
                                       out_ev[(j, dh)],
                                       out_ev[((j - 1) % NT, dh)]]
                                if (cw == 0 and half == 0) else
                                ([hid_t[cw], w3tk[cw][0], w3tk[cw][1]]
                                 if cw == 0 else [w3tk[cw][0], w3tk[cw][1]]),
                                inc=(cw == ncw - 1 and half == 1))
                    t_a = P.op("vector", lambda e, j=j, dh=dh, pco=pco:
                               e.scalar_tensor_tensor(
                                   out=P.f32(O_X + j * 4096 + dh * 2048,
                                             1024),
                                   in0=P.P0[:, pco:pco + 1024], scalar=0.5,
                                   in1=P.f32(O_X + j * 4096 + dh * 2048,
                                             1024),
                                   op0=ALU.mult, op1=ALU.add),
                               waits=[t_m, out_ev[(j, dh)]])
                    out_ev[(j, dh)] = t_a
            pe_now = self.tok("tensor")
            for cw in range(ncw):
                hid_rd[cw] = pe_now
            for sl in range(7):
                w3_free[sl] = pe_now
            out_guard[0] = out_ev[(NT - 1, 0)]
            if win == nwin - 1:
                for j in range(NT):
                    stores[j] = P.op("sync", lambda e, j=j, r=r:
                                     e.dma_start(
                                         out=P.out_hbm[r * RND + j * 128:
                                                       r * RND
                                                       + (j + 1) * 128, :],
                                         in_=P.f32(O_X + j * 4096, 2048)),
                                     dma=f"dST{j}",
                                     waits=[out_ev[(j, 0)], out_ev[(j, 1)]])
        self.prev["store"] = stores
        self.st_final.extend(stores)
        pe_end = self.tok("tensor")
        self.prev["x_free"] = stores[-1]
        self.prev["att_free"] = pe_end
        self.prev["cos_free"] = pe_end
        self.prev["ln1_free"] = pe_end
        self.prev["qtn_free"] = pe_end
        self.prev["wqu_free"] = pe_end
        self.prev["wkv_free"] = pe_end

    def round(self, r):
        self.ph_a(r)
        self.ph_b(r)
        self.ph_c(r)
        self.ph_d(r)
        self.ph_e(r)
        self.ph_f(r)
        self.ph_g(r)


def _host_masks(quarter0):
    i = np.arange(128)
    m0 = np.where(i[None, :] < i[:, None], 0.0, NEG).astype(np.float32)
    m4 = np.where(i[None, :] >= i[:, None], 0.0, NEG).astype(np.float32)
    mz = np.zeros((128, 128), np.float32)
    mf = np.full((128, 128), NEG, np.float32)
    out = np.zeros((2, NT, NCH, 128, 128), np.float32)
    for r in range(2):
        for j in range(NT):
            tg = r * NT + j
            for c in range(NCH):
                if quarter0 and tg - 4 + c < 0:
                    out[r, j, c] = mf
                elif c == 0:
                    out[r, j, c] = m0
                elif c == NCH - 1:
                    out[r, j, c] = m4
                else:
                    out[r, j, c] = mz
    return out.astype(BFD)


def _host_costab(qr):
    inv = 10000.0 ** (-(np.arange(0, 64, 2, dtype=np.float64)) / 64.0)
    cost = np.zeros((2, NSRC, 128, 32), np.float32)
    sint = np.zeros((2, NSRC, 128, 32), np.float32)
    for r in range(2):
        for i in range(NSRC):
            p0 = qr * RT + r * RND + (i - 4) * 128
            pos = p0 + np.arange(128)
            valid = pos >= 0
            ang = pos[:, None].astype(np.float64) * inv[None, :]
            cost[r, i] = np.where(valid[:, None], np.cos(ang), 0.0)
            sint[r, i] = np.where(valid[:, None], np.sin(ang), 0.0)
    return cost, sint


_NC_CACHE = {}


def kernel(L0_attn_h, L0_attn_wq_comp, L0_attn_wq_up, L0_attn_wk, L0_attn_wv,
           L0_attn_q_norm_w, L0_attn_k_norm_w, L0_attn_sink_logit,
           L0_attn_out_w1, L0_attn_out_w2, L0_ffn_w1, L0_ffn_w2, L0_ffn_w3):
    asf = lambda a: np.ascontiguousarray(np.asarray(a, np.float32))
    h = asf(L0_attn_h)
    wqc, wqu, wk, wv = map(asf, (L0_attn_wq_comp, L0_attn_wq_up, L0_attn_wk,
                                 L0_attn_wv))
    qnw, knw, sink = map(asf, (L0_attn_q_norm_w, L0_attn_k_norm_w,
                               L0_attn_sink_logit))
    ow1, ow2 = asf(L0_attn_out_w1), asf(L0_attn_out_w2)
    fw1, fw2, fw3 = map(asf, (L0_ffn_w1, L0_ffn_w2, L0_ffn_w3))

    iden = np.eye(128, dtype=BFD)
    masks_g = _host_masks(False)
    masks_q0 = _host_masks(True)
    in_maps = []
    for c in range(N_CORE):
        b, qr = c // 4, c % 4
        if qr == 0:
            hs = np.concatenate([np.zeros((HALO, D), np.float32),
                                 h[b, 0:RT, :]], axis=0)
        else:
            hs = np.ascontiguousarray(h[b, qr * RT - HALO:(qr + 1) * RT, :])
        cost, sint = _host_costab(qr)
        in_maps.append({
            "h": hs, "wq_comp": wqc, "wq_up": wqu, "wk": wk, "wv": wv,
            "q_norm_w": qnw, "k_norm_w": knw, "sink": sink,
            "out_w1": ow1, "out_w2": ow2,
            "ffn_w1": fw1, "ffn_w2": fw2, "ffn_w3": fw3,
            "costab": cost, "sintab": sint, "iden": iden,
            "masks": masks_q0 if qr == 0 else masks_g,
        })

    nc = _NC_CACHE.get("nc")
    if nc is None:
        nc = Builder().build()
        _NC_CACHE["nc"] = nc
    res = run_bass_kernel_spmd(nc, in_maps, core_ids=list(range(N_CORE)))
    out = np.empty((B, T_FULL, D), np.float32)
    for c in range(N_CORE):
        b, qr = c // 4, c % 4
        out[b, qr * RT:(qr + 1) * RT, :] = res.results[c]["out"]
    return out


if __name__ == "__main__":
    Builder().build()
    print("build ok")


# version 11 (final)
# speedup vs baseline: 1.1333x; 1.0438x over previous
import math

import numpy as np
import ml_dtypes

import concourse.bass as bass
import concourse.mybir as mybir
from concourse.bass_utils import run_bass_kernel_spmd

F32 = mybir.dt.float32
BF16 = mybir.dt.bfloat16
AF = mybir.ActivationFunctionType
ALU = mybir.AluOpType
AX = mybir.AxisListType.X
BFD = ml_dtypes.bfloat16

B, T_FULL, D = 2, 8192, 2048
H, HD, QC = 16, 128, 512
DFF = 5632
RT, HALO, RND = 2048, 512, 1024
NT, NSRC, NKC, NCH, NW = 8, 12, 16, 5, 44
SG = 5
LNEPS, RMSEPS, QEPS = 1e-5, 1e-6, 1e-8
SCALE = 1.0 / math.sqrt(HD)
NEG = -1e30
N_CORE = 8

ENGINES = ("sync", "scalar", "vector", "tensor", "gpsimd")
EPROG = {"sync": "pS", "scalar": "pA", "vector": "pV", "tensor": "pP",
         "gpsimd": "pG"}

ACOL = 89088
O_GACC, O_GSTG = 0, 1024
O_LN1T = 0
O_HT, O_QEV, O_QRF, O_QT3 = 24576, 24576, 28672, 32768
O_QNB, O_Q1T, O_WQC, O_KSC, O_LNB = 36864, 38912, 43008, 44032, 46080
O_COS, O_SIN = 50176, 50944
O_WQU, O_WK, O_WV, O_KT, O_VN, O_QTN = 57344, 65536, 67584, 69632, 51712, 72704
VST = 132
O_ANAT, O_ATT = 0, 4096
O_MKS, O_PRB, O_CMT, O_CRW = 24576, 25856, 27136, 27264
O_GST2 = 53296
O_X = 24576
O_Y1, O_W1G = 57344, 73728
O_W2QS = (0, 20480, 81920, 4096)
O_LN2T, O_LNB2 = 57344, 73728
O_W3A = 73728
O_W1T, O_W2T, O_FST, O_SGM, O_S1, O_QD, O_HID = (
    0, 4096, 8192, 12288, 14336, 16384, 18432)
O_S1B = 23552
O_S1C = 88064
FSTO = (8192, 10240, 86016)
QDO = (15360, 16384, 17408)
NQD = 3


class Builder:
    def __init__(self):
        self.nc = bass.Bass("TRN2", target_bir_lowering=False)
        self.ops = {e: [] for e in ENGINES}
        self.semh = {}
        self.semc = {}
        self.ctx = None

    def s(self, name):
        if name not in self.semh:
            self.semh[name] = self.nc.alloc_semaphore(name)
            self.semc[name] = 0
        return name

    def tok(self, eng):
        n = EPROG[eng]
        return (n, self.semc.get(n, 0))

    def op(self, eng, fn, waits=(), dma=None, inc=True):
        wl = []
        for w in waits:
            if w is None:
                continue
            n, v = w
            if v > 0:
                self.s(n)
                wl.append((n, v))
        incs = []
        if dma is not None:
            self.s(dma)
            self.semc[dma] += 16
            incs.append((dma, 16))
            ret = (dma, self.semc[dma])
        elif inc:
            n = self.s(EPROG[eng])
            self.semc[n] += 1
            incs.append((n, 1))
            ret = (n, self.semc[n])
        else:
            ret = None
        self.ops[eng].append((wl, fn, incs))
        return ret

    def emit(self):
        with self.nc.Block() as block:
            for ename in ENGINES:
                def mk(ename):
                    def body(eng):
                        hi = {}
                        for waits, fn, incs in self.ops[ename]:
                            for (n, v) in waits:
                                if hi.get(n, 0) >= v:
                                    continue
                                eng.wait_ge(self.semh[n], v)
                                hi[n] = v
                            ins = fn(eng)
                            for (n, k) in incs:
                                ins.then_inc(self.semh[n], k)
                    return body
                getattr(block, ename)(mk(ename))

    def build(self):
        nc = self.nc
        import contextlib
        self.ctx = contextlib.ExitStack()
        E = self.ctx.enter_context

        self.h_in = nc.dram_tensor("h", [RT + HALO, D], F32,
                                   kind="ExternalInput").ap()
        self.wqc_in = nc.dram_tensor("wq_comp", [D, QC], F32,
                                     kind="ExternalInput").ap()
        self.wqu_in = nc.dram_tensor("wq_up", [QC, D], F32,
                                     kind="ExternalInput").ap()
        self.wk_in = nc.dram_tensor("wk", [D, HD], F32,
                                    kind="ExternalInput").ap()
        self.wv_in = nc.dram_tensor("wv", [D, HD], F32,
                                    kind="ExternalInput").ap()
        self.qnw_ap = nc.dram_tensor("q_norm_w", [HD], F32,
                                     kind="ExternalInput").ap()
        self.knw_ap = nc.dram_tensor("k_norm_w", [HD], F32,
                                     kind="ExternalInput").ap()
        self.sink_ap = nc.dram_tensor("sink", [H], F32,
                                      kind="ExternalInput").ap()
        self.ow1_in = nc.dram_tensor("out_w1", [4, QC, 2048], F32,
                                     kind="ExternalInput").ap()
        self.ow2_in = nc.dram_tensor("out_w2", [4 * 2048, D], F32,
                                     kind="ExternalInput").ap()
        self.fw1_in = nc.dram_tensor("ffn_w1", [DFF, D], F32,
                                     kind="ExternalInput").ap()
        self.fw2_in = nc.dram_tensor("ffn_w2", [DFF, D], F32,
                                     kind="ExternalInput").ap()
        self.fw3_in = nc.dram_tensor("ffn_w3", [D, DFF], F32,
                                     kind="ExternalInput").ap()
        self.cos_in = nc.dram_tensor("costab", [2, NSRC, 128, 32], F32,
                                     kind="ExternalInput").ap()
        self.sin_in = nc.dram_tensor("sintab", [2, NSRC, 128, 32], F32,
                                     kind="ExternalInput").ap()
        self.iden_ap = nc.dram_tensor("iden", [128, 128], BF16,
                                      kind="ExternalInput").ap()
        self.mask_in = nc.dram_tensor("masks", [2, NT, NCH, 128, 128], BF16,
                                      kind="ExternalInput").ap()
        self.out_hbm = nc.dram_tensor("out", [RT, D], F32,
                                      kind="ExternalOutput").ap()
        import os
        self.dbg = None
        if os.environ.get("KDBG"):
            self.dbg = nc.dram_tensor("dbg", [RT, D], F32,
                                      kind="ExternalOutput").ap()
            self.dbg2 = nc.dram_tensor("dbg2", [2, 128, H * RND], BF16,
                                       kind="ExternalOutput").ap()
            self.dbg3 = nc.dram_tensor("dbg3", [2, NT, 128, D], BF16,
                                       kind="ExternalOutput").ap()
            self.dbg4 = nc.dram_tensor("dbg4", [2, 128, NSRC * 128], BF16,
                                       kind="ExternalOutput").ap()
            self.dbg5 = nc.dram_tensor("dbg5", [2, 128, NSRC * 129], BF16,
                                       kind="ExternalOutput").ap()

        self.iden = E(nc.sbuf_tensor("idn", [128, 128], BF16))
        self.ones_r = E(nc.sbuf_tensor("ones_r", [1, 128], BF16))
        self.ones_c = E(nc.sbuf_tensor("ones_c", [128, 1], BF16))
        self.ones_rf = E(nc.sbuf_tensor("ones_rf", [1, 128], F32))
        self.ones_cf = E(nc.sbuf_tensor("ones_cf", [128, 1], F32))
        self.b_m05 = E(nc.sbuf_tensor("b_m05", [128, 1], F32))
        self.b_eps = E(nc.sbuf_tensor("b_eps", [128, 1], F32))
        self.b_p05 = E(nc.sbuf_tensor("b_p05", [128, 1], F32))
        self.sinkb = E(nc.sbuf_tensor("sinkb", [128, H], F32))
        self.nsinkb = E(nc.sbuf_tensor("nsinkb", [128, H], F32))
        self.qnwb = E(nc.sbuf_tensor("qnwb", [128, HD], F32))
        self.knwb = E(nc.sbuf_tensor("knwb", [128, HD], F32))
        self.gacc = E(nc.sbuf_tensor("gacc", [128, 112], F32))
        self.gtot = E(nc.sbuf_tensor("gtot", [1, 4], F32))
        self.invg = E(nc.sbuf_tensor("invg", [128, 4], F32))
        self.cmaxb = E(nc.sbuf_tensor("cmaxb", [128, 2 * H], BF16))
        self.cmaxf = E(nc.sbuf_tensor("cmaxf", [128, 2 * H], F32))
        self.red1 = E(nc.sbuf_tensor("red1", [128, 4], F32))
        self.gred = E(nc.sbuf_tensor("gred", [128, 1], F32))
        self.dstat = E(nc.sbuf_tensor("dstat", [128, 4], F32))
        self.lns = E(nc.sbuf_tensor("lns", [128, 8], F32))
        self.rmsS = E(nc.sbuf_tensor("rmsS", [128, 2 * H], F32))

        self.AR = E(nc.sbuf_tensor("arena", [128, ACOL], BF16))
        self.ARF = self.AR[:].bitcast(F32)
        self.P0 = E(nc.psum_tensor("P0", [128, 2048], F32))
        self.P1 = E(nc.psum_tensor("P1", [128, 1536], F32))
        self.PT = E(nc.psum_tensor("PT", [128, 1024], BF16))

        self.st_final = []
        self.prev = {"d_done": None, "c_done": None, "e_done": None,
                     "ln1_free": None, "store": [None] * NT,
                     "cos_free": None, "wqu_free": None, "wkv_free": None,
                     "qtn_free": None, "att_free": None, "x_free": None,
                     "mask_free": [None, None]}
        self.pt_free = None

        self.consts()
        self.gamma_gen = self.gamma_steps()
        self._gamma_last = None
        self.T_GAMMA = None
        self.gamma_finalize()
        for r in range(2):
            self.round(r)
        self.op("sync", lambda e: e.nop(), waits=self.st_final, inc=False)
        self.emit()
        self.ctx.close()
        return nc

    def bf(self, off, n):
        return self.AR[:, off:off + n]

    def f32(self, off, n):
        assert off % 2 == 0
        return self.ARF[:, off // 2: off // 2 + n]

    def consts(self):
        P = self
        self.t_iden = P.op("sync", lambda e: e.dma_start(
            out=P.iden[:], in_=P.iden_ap), dma="dIDN")
        self.t_sink = P.op("gpsimd", lambda e: e.dma_start(
            out=P.sinkb[:], in_=P.sink_ap
            .unsqueeze(0).partition_broadcast(128).squeeze(1)), dma="dBCS")
        self.t_qnb = P.op("gpsimd", lambda e: e.dma_start(
            out=P.qnwb[:], in_=P.qnw_ap
            .unsqueeze(0).partition_broadcast(128).squeeze(1)), dma="dBCQ")
        t_bc = P.op("gpsimd", lambda e: e.dma_start(
            out=P.knwb[:], in_=P.knw_ap
            .unsqueeze(0).partition_broadcast(128).squeeze(1)), dma="dBCK")
        self.t_bc = t_bc
        P.op("vector", lambda e: e.memset(P.ones_r[:], 1.0))
        P.op("vector", lambda e: e.memset(P.ones_c[:], 1.0))
        P.op("vector", lambda e: e.memset(P.ones_rf[:], 1.0))
        P.op("vector", lambda e: e.memset(P.ones_cf[:], 1.0))
        P.op("vector", lambda e: e.memset(P.b_m05[:], -0.5))
        P.op("vector", lambda e: e.memset(P.b_eps[:], LNEPS))
        t_b = P.op("vector", lambda e: e.memset(P.b_p05[:], 0.5))
        self.t_nsk = P.op("vector", lambda e: e.tensor_scalar_mul(
            out=P.nsinkb[:], in0=P.sinkb[:], scalar1=-1.0),
            waits=[self.t_sink, t_bc, t_b])
        self.t_consts = self.t_nsk

    def pump(self, n):
        for _ in range(n):
            if self.gamma_gen is not None:
                try:
                    next(self.gamma_gen)
                except StopIteration:
                    self.gamma_gen = None

    def gamma_steps(self):
        P = self
        pieces = []
        for wsel, src, ncols in ([(0, P.fw1_in, D)] * 0 +
                                 [(0, P.fw1_in, D)] +
                                 []):
            pass
        chunks = ([(P.fw1_in, i, D, i) for i in range(NW)] +
                  [(P.fw2_in, i, D, NW + i) for i in range(NW)] +
                  [(P.fw3_in, i, DFF, 2 * NW + i) for i in range(NKC)])
        last = [None, None]
        prevr = None
        n = 0
        for (src, i, ncols, col) in chunks:
            npc = (ncols + 2047) // 2048
            for pc in range(npc):
                cw = min(2048, ncols - pc * 2048)
                sl = n % 2
                n += 1
                stg = P.f32(O_GSTG + sl * 4096, 2048)
                t_ld = P.op("sync", lambda e, s2=stg[:, 0:cw],
                            a=src[i * 128:(i + 1) * 128,
                                  pc * 2048:pc * 2048 + cw]:
                            e.dma_start(out=s2, in_=a), dma=f"dG{sl}",
                            waits=[last[sl]])
                dst = (P.gacc[:, col:col + 1] if pc == 0 else P.gred[:])
                t_rd = P.op("vector", lambda e, s2=stg[:, 0:cw], d=dst:
                            e.tensor_reduce(out=d, in_=s2, axis=AX,
                                            op=ALU.add,
                                            apply_absolute_value=True),
                            waits=[t_ld, prevr])
                prevr = t_rd
                last[sl] = t_rd
                if pc > 0:
                    prevr = P.op("vector", lambda e, c=col: e.tensor_tensor(
                        out=P.gacc[:, c:c + 1], in0=P.gacc[:, c:c + 1],
                        in1=P.gred[:], op=ALU.add), waits=[prevr])
                    last[sl] = prevr
                self._gamma_last = prevr
                yield

    def gamma_finalize(self):
        P = self
        while self.gamma_gen is not None:
            self.pump(8)
        t_g = self._gamma_last
        t_pe_prev = None
        for wi, (c0, cn, nel) in enumerate([(0, NW, DFF * D),
                                            (NW, NW, DFF * D),
                                            (2 * NW, NKC, D * DFF)]):
            t1 = P.op("vector", lambda e, c0=c0, cn=cn: e.reduce_sum(
                out=P.red1[:, 1:2], in_=P.gacc[:, c0:c0 + cn], axis=AX),
                waits=[t_g])
            t2 = P.op("tensor", lambda e: e.matmul(
                P.P1[0:1, 0:1], P.red1[:, 1:2], P.ones_cf[:],
                start=True, stop=True), waits=[t1, self.t_consts, t_pe_prev])
            t3 = P.op("vector", lambda e, wi=wi: e.tensor_copy(
                out=P.gtot[0:1, wi:wi + 1], in_=P.P1[0:1, 0:1]),
                waits=[t2])
            t4 = P.op("tensor", lambda e, wi=wi: e.matmul(
                P.P1[:, 0:1], P.ones_rf[:], P.gtot[0:1, wi:wi + 1],
                start=True, stop=True), waits=[t3])
            t_pe_prev = t4
            t5 = P.op("vector", lambda e, wi=wi, nel=nel: e.tensor_scalar(
                out=P.invg[:, wi:wi + 1], in0=P.P1[:, 0:1],
                scalar1=1.0 / nel, scalar2=QEPS, op0=ALU.mult, op1=ALU.add),
                waits=[t4])
            t_g = P.op("vector", lambda e, wi=wi: e.reciprocal(
                out=P.invg[:, wi:wi + 1], in_=P.invg[:, wi:wi + 1]),
                waits=[t5])
        t_g = P.op("vector", lambda e: e.tensor_scalar_mul(
            out=P.invg[:, 3:4], in0=P.invg[:, 1:2], scalar1=-1.0),
            waits=[t_g])
        self.T_GAMMA = t_g
        self.t_gamma_pe = t_pe_prev

    def ln_stats(self, xv, t_in, sq_off, extra_act_w=()):
        P = self
        t_s1 = P.op("vector", lambda e: e.reduce_sum(
            out=P.lns[:, 0:1], in_=xv, axis=AX), waits=[t_in])
        t_s2 = P.op("scalar", lambda e: e.activation(
            out=P.f32(sq_off, 1024), in_=xv[:, 0:1024], func=AF.Square,
            accum_out=P.lns[:, 1:2]), waits=[t_in] + list(extra_act_w))
        t_s2b = P.op("scalar", lambda e: e.activation(
            out=P.f32(sq_off, 1024), in_=xv[:, 1024:2048], func=AF.Square,
            accum_out=P.lns[:, 7:8]), waits=[t_s2])
        t_s3 = P.op("vector", lambda e: e.tensor_scalar_mul(
            out=P.lns[:, 2:3], in0=P.lns[:, 0:1], scalar1=1.0 / D),
            waits=[t_s1])
        t_s4 = P.op("vector", lambda e: e.tensor_tensor(
            out=P.lns[:, 3:4], in0=P.lns[:, 2:3], in1=P.lns[:, 2:3],
            op=ALU.mult), waits=[t_s3])
        t_s4b = P.op("vector", lambda e: e.tensor_tensor(
            out=P.lns[:, 1:2], in0=P.lns[:, 1:2], in1=P.lns[:, 7:8],
            op=ALU.add), waits=[t_s2b, t_s4])
        t_s5 = P.op("vector", lambda e: e.tensor_scalar(
            out=P.lns[:, 4:5], in0=P.lns[:, 1:2], scalar1=1.0 / D,
            scalar2=None, op0=ALU.mult), waits=[t_s4b])
        t_s6 = P.op("vector", lambda e: e.tensor_tensor(
            out=P.lns[:, 4:5], in0=P.lns[:, 4:5], in1=P.lns[:, 3:4],
            op=ALU.subtract), waits=[t_s5])
        t_s7 = P.op("scalar", lambda e: e.activation(
            out=P.lns[:, 5:6], in_=P.lns[:, 4:5], func=AF.Sqrt,
            bias=P.b_eps[:], scale=1.0), waits=[t_s6, self.t_consts]
            if t_s6 else [t_s6])
        t_s8 = P.op("vector", lambda e: e.reciprocal(
            out=P.lns[:, 6:7], in_=P.lns[:, 5:6]), waits=[t_s7])
        return t_s8

    def pt_batch(self, src_fn, dst_fn, t_in_list, nblk=8):
        P = self
        t_tr = None
        for c in range(nblk):
            t_tr = P.op("tensor",
                        lambda e, c=c: e.transpose(
                            P.PT[:, c * 128:(c + 1) * 128], src_fn(c),
                            P.iden[:]),
                        waits=(t_in_list + [self.pt_free, self.t_iden])
                        if c == 0 else [], inc=(c == nblk - 1))
        t_ev = P.op("scalar", lambda e: e.activation(
            out=dst_fn(), in_=P.PT[:, 0:nblk * 128]
            .rearrange("p (c q) -> p c q", q=128), func=AF.Copy),
            waits=[t_tr])
        self.pt_free = t_ev
        return t_ev

    def ph_a(self, r):
        P = self
        t_cos = P.op("sync", lambda e, r=r: e.dma_start(
            out=P.f32(O_COS, 12 * 32).rearrange("p (i q) -> p i q", q=32),
            in_=P.cos_in[r].rearrange("i p q -> p i q")), dma="dCOS",
            waits=[self.prev["cos_free"]] +
            (list(self.prev["store"]) if r == 1 else []))
        self.t_cos = P.op("sync", lambda e, r=r: e.dma_start(
            out=P.f32(O_SIN, 12 * 32).rearrange("p (i q) -> p i q", q=32),
            in_=P.sin_in[r].rearrange("i p q -> p i q")), dma="dCOS")
        ht_free = [None, None]
        t_lnT = None
        for i in range(NSRC):
            sl = i % 2
            htv = P.f32(O_HT + sl * 4096, 2048)
            w = [ht_free[sl]]
            if r == 0 and i < 2:
                w.append(self.T_GAMMA)
            if r == 1 and i < 2:
                w.extend(self.prev["store"])
            t_h = P.op("sync", lambda e, htv=htv, i=i, r=r: e.dma_start(
                out=htv, in_=P.h_in[r * RND + i * 128:
                                    r * RND + (i + 1) * 128, :]),
                dma=f"dH{sl}", waits=w)
            self.pump(5)
            t_st = self.ln_stats(htv, t_h, O_QT3,
                                 extra_act_w=list(self.prev["store"])
                                 if (r == 1 and i == 0) else [])
            lnt = P.bf(O_LNB + sl * 2048, 2048)
            t_ln = P.op("vector", lambda e, htv=htv, lnt=lnt: e.tensor_scalar(
                out=lnt, in0=htv, scalar1=P.lns[:, 2:3],
                scalar2=P.lns[:, 6:7], op0=ALU.subtract, op1=ALU.mult),
                waits=[t_st])
            ht_free[sl] = t_ln
            for hb in range(2):
                t_lnT = self.pt_batch(
                    lambda c, lnt=lnt, hb=hb: lnt[:, (hb * 8 + c) * 128:
                                                  (hb * 8 + c + 1) * 128],
                    lambda hb=hb, i=i: P.bf(O_LN1T, NKC * NSRC * 128)
                    .rearrange("p (k t q) -> p k t q", k=NKC, t=NSRC)
                    [:, hb * 8:(hb + 1) * 8, i, :],
                    [t_ln, self.prev["ln1_free"] if i == 0 and hb == 0
                     else None])
        self.T_LN1 = t_lnT

    def ph_b(self, r):
        P = self
        t_wk = P.op("gpsimd", lambda e: e.dma_start(
            out=P.bf(O_WK, 2048).rearrange("p (c q) -> p c q", q=128),
            in_=P.wk_in.rearrange("(c p) o -> p c o", p=128)), dma="dWK",
            waits=[self.prev["wkv_free"]])
        t_wv = P.op("gpsimd", lambda e: e.dma_start(
            out=P.bf(O_WV, 2048).rearrange("p (c q) -> p c q", q=128),
            in_=P.wv_in.rearrange("(c p) o -> p c o", p=128)), dma="dWV",
            waits=[self.prev["wkv_free"]])
        kv_free = [None, None]
        kT_done = getattr(self, "dbg_kv", [None])[0]
        t_v = None
        t_vone = P.op("vector", lambda e: e.memset(
            P.bf(O_VN, NSRC * VST).rearrange("p (i q) -> p i q", q=VST)
            [:, :, 128:129], 1.0),
            waits=[self.prev["att_free"],
                   getattr(self, "dbg_kv", [None, None])[1]])
        self.t_vone = t_vone
        for i in range(NSRC):
            self.pump(5)
            for vsel in (0, 1):
                pc = i % 2
                pcol = 512 if pc == 0 else 1024
                t_mm = None
                for kc in range(NKC):
                    t_mm = P.op("tensor", lambda e, i=i, kc=kc, pcol=pcol,
                                woff=(O_WV if vsel else O_WK): e.matmul(
                                    P.P1[:, pcol:pcol + 128],
                                    P.bf(O_LN1T + (kc * NSRC + i) * 128, 128),
                                    P.bf(woff + kc * 128, 128),
                                    start=(kc == 0), stop=(kc == NKC - 1)),
                                waits=[self.T_LN1, t_wv if vsel else t_wk,
                                       kv_free[pc]] if kc == 0 else [],
                                inc=(kc == NKC - 1))
                if vsel:
                    t_v = P.op("vector", lambda e, i=i, pcol=pcol:
                               e.tensor_copy(out=P.bf(O_VN + i * VST, 128),
                                             in_=P.P1[:, pcol:pcol + 128]),
                               waits=[t_mm, self.prev["att_free"]
                                      if i == 0 else None,
                                      getattr(self, "dbg_kv",
                                              [None, None])[1]
                                      if i == 0 else None])
                    kv_free[pc] = t_v
                else:
                    kev = P.f32(O_KSC, 128)
                    t_e = P.op("vector", lambda e, kev=kev, pcol=pcol:
                               e.tensor_copy(out=kev,
                                             in_=P.P1[:, pcol:pcol + 128]),
                               waits=[t_mm, self.prev["att_free"]
                                      if i == 0 else None])
                    kv_free[pc] = t_e
                    krf = P.f32(O_KSC + 512, 128)
                    t_r = self.rope128(kev, krf, O_KSC + 1024, i, t_e)
                    t_q1 = P.op("scalar", lambda e, krf=krf: e.activation(
                        out=P.f32(O_KSC + 1536, 128), in_=krf, func=AF.Square,
                        accum_out=P.red1[:, 2:3]), waits=[t_r])
                    t_q2 = P.op("vector", lambda e: e.tensor_scalar(
                        out=P.red1[:, 3:4], in0=P.red1[:, 2:3],
                        scalar1=1.0 / HD, scalar2=RMSEPS, op0=ALU.mult,
                        op1=ALU.add), waits=[t_q1])
                    t_q3 = P.op("scalar", lambda e: e.activation(
                        out=P.red1[:, 2:3], in_=P.red1[:, 3:4], func=AF.Sqrt),
                        waits=[t_q2])
                    t_q4 = P.op("vector", lambda e: e.reciprocal(
                        out=P.red1[:, 3:4], in_=P.red1[:, 2:3]), waits=[t_q3])
                    knb = P.bf(O_KSC + 3584, 128)
                    t_q5 = P.op("vector", lambda e, krf=krf, knb=knb:
                                e.scalar_tensor_tensor(
                                    out=knb, in0=krf, scalar=P.red1[:, 3:4],
                                    in1=P.knwb[:], op0=ALU.mult,
                                    op1=ALU.mult), waits=[t_q4, self.t_nsk])
                    t_q6 = P.op("tensor", lambda e, knb=knb: e.transpose(
                        P.PT[:, 896:1024], knb, P.iden[:]),
                        waits=[t_q5, kT_done or self.pt_free, self.t_iden])
                    kT_done = P.op("scalar", lambda e, i=i: e.activation(
                        out=P.bf(O_KT + i * 128, 128), in_=P.PT[:, 896:1024],
                        func=AF.Copy), waits=[t_q6])
        self.pt_free = kT_done
        self.T_KT = kT_done
        self.T_V = t_v

    def rope128(self, src, dst, scr, i, t_in):
        P = self
        c = P.f32(O_COS + i * 64, 32)
        s = P.f32(O_SIN + i * 64, 32)
        t1 = P.f32(scr, 32)
        t2 = P.f32(scr + 64, 32)
        ta = P.op("vector", lambda e: e.tensor_tensor(
            out=t1, in0=src[:, 0:32], in1=c, op=ALU.mult),
            waits=[t_in, self.t_cos])
        tb = P.op("vector", lambda e: e.tensor_tensor(
            out=t2, in0=src[:, 32:64], in1=s, op=ALU.mult), waits=[ta])
        tc = P.op("vector", lambda e: e.tensor_tensor(
            out=dst[:, 0:32], in0=t1, in1=t2, op=ALU.subtract), waits=[tb])
        td = P.op("vector", lambda e: e.tensor_tensor(
            out=t1, in0=src[:, 0:32], in1=s, op=ALU.mult), waits=[tc])
        te = P.op("vector", lambda e: e.tensor_tensor(
            out=t2, in0=src[:, 32:64], in1=c, op=ALU.mult), waits=[td])
        tf = P.op("vector", lambda e: e.tensor_tensor(
            out=dst[:, 32:64], in0=t1, in1=t2, op=ALU.add), waits=[te])
        tg = P.op("vector", lambda e: e.tensor_copy(
            out=dst[:, 64:128], in_=src[:, 64:128]), waits=[tf])
        return tg

    def ph_c(self, r):
        P = self
        t_wqu = P.op("gpsimd", lambda e: e.dma_start(
            out=P.bf(O_WQU, 8192).rearrange("p (c q) -> p c q", q=2048),
            in_=P.wqu_in.rearrange("(c p) o -> p c o", p=128)), dma="dWQU",
            waits=[self.prev["wqu_free"]])
        wqc_free = [None, None]
        t_q1T = None
        q1_pe = None
        for th in range(2):
            kc_tok = {}
            for kc in range(NKC):
                sl = kc % 2
                t_w = P.op("gpsimd", lambda e, kc=kc, sl=sl: e.dma_start(
                    out=P.bf(O_WQC + sl * 512, 512),
                    in_=P.wqc_in[kc * 128:(kc + 1) * 128, :]), dma=f"dQC{sl}",
                    waits=[wqc_free[sl],
                           self.T_LN1 if (th == 0 and kc < 2) else None])
                for oc in range(4):
                    w = []
                    if oc == 0:
                        w = [t_w, self.T_LN1]
                        if th == 1 and kc == 0:
                            w.append(t_q1T)
                        if th == 0 and kc == 0:
                            w.append(self.T_GAMMA)
                            w.append(self.tok("vector"))
                    tt = P.op("tensor", lambda e, th=th, kc=kc, oc=oc, sl=sl:
                              e.matmul(
                                  P.P0[:, oc * 512:(oc + 1) * 512],
                                  P.bf(O_WQC + sl * 512 + oc * 128, 128),
                                  P.bf(O_LN1T + (kc * NSRC + 4) * 128
                                       + th * 512, 512),
                                  start=(kc == 0), stop=(kc == NKC - 1)),
                              waits=w, inc=(oc == 3))
                    if oc == 3:
                        kc_tok[kc] = tt
                        wqc_free[sl] = tt
                        if kc == NKC - 1:
                            q1_pe = tt
            for oc in range(4):
                t_q1T = P.op("scalar", lambda e, th=th, oc=oc: e.activation(
                    out=P.bf(O_Q1T + oc * 1024 + th * 512, 512),
                    in_=P.P0[:, oc * 512:(oc + 1) * 512], func=AF.Copy),
                    waits=[q1_pe])
        T_Q1 = t_q1T
        qps = [(self.P0, 0), (self.P0, 1024), (self.P1, 0)]
        q_ev = [None, None, self.tok("vector")]
        rope_done = None
        t_qTn = None
        for j in range(NT):
            self.pump(10)
            evs = []
            for dh in range(2):
                s3 = (2 * j + dh) % 3
                ps, pco = qps[s3]
                t_m = None
                for kc in range(4):
                    for nn in range(2):
                        t_m = P.op("tensor", lambda e, j=j, dh=dh, kc=kc,
                                   nn=nn, ps=ps, pco=pco: e.matmul(
                                       ps[:, pco + nn * 512:
                                          pco + (nn + 1) * 512],
                                       P.bf(O_Q1T + kc * 1024 + j * 128, 128),
                                       P.bf(O_WQU + kc * 2048 + dh * 1024
                                            + nn * 512, 512),
                                       start=(kc == 0), stop=(kc == 3)),
                                   waits=[T_Q1, t_wqu, q_ev[s3]]
                                   if (kc == 0 and nn == 0) else [],
                                   inc=(kc == 3 and nn == 1))
                t_e = P.op("vector", lambda e, dh=dh, ps=ps, pco=pco:
                           e.tensor_copy(out=P.f32(O_QEV + dh * 2048, 1024),
                                         in_=ps[:, pco:pco + 1024]),
                           waits=[t_m, rope_done])
                q_ev[s3] = t_e
                evs.append(t_e)
            rope_done = self.rope_norm_q(j, evs)
            if self.dbg is not None:
                tq = P.op("sync", lambda e, j=j, r=r: e.dma_start(
                    out=self.dbg3[r, j], in_=P.bf(O_QNB, 2048)),
                    dma="dDQ", waits=[rope_done])
                self.st_final.append(tq)
                rope_done = tq
            for hb in range(2):
                t_qTn = self.pt_batch(
                    lambda c, hb=hb: P.bf(O_QNB + (hb * 8 + c) * 128, 128),
                    lambda hb=hb, j=j: P.bf(O_QTN, NT * H * 128)
                    .rearrange("p (t k q) -> p t k q", t=NT, k=H)
                    [:, j, hb * 8:(hb + 1) * 8, :],
                    [rope_done, self.prev["qtn_free"]
                     if (j == 0 and hb == 0) else None])
        self.T_QTN = t_qTn

    def rope_norm_q(self, j, evs):
        P = self
        qev3 = P.f32(O_QEV, 2048).rearrange("p (k q) -> p k q", q=128)
        qrf3 = P.f32(O_QRF, 2048).rearrange("p (k q) -> p k q", q=128)
        ci = 4 + j
        c = (P.f32(O_COS + ci * 64, 32).unsqueeze(1)
             .broadcast_to([128, H, 32]))
        s = (P.f32(O_SIN + ci * 64, 32).unsqueeze(1)
             .broadcast_to([128, H, 32]))
        t1 = P.f32(O_QT3, 512).rearrange("p (k q) -> p k q", q=32)
        t2 = P.f32(O_QT3 + 1024, 512).rearrange("p (k q) -> p k q", q=32)
        x1 = qev3[:, :, 0:32]
        x2 = qev3[:, :, 32:64]
        w0 = list(evs) + [self.t_cos]
        ta = P.op("vector", lambda e: e.tensor_tensor(out=t1, in0=x1, in1=c,
                                                      op=ALU.mult), waits=w0)
        tb = P.op("vector", lambda e: e.tensor_tensor(out=t2, in0=x2, in1=s,
                                                      op=ALU.mult), waits=[ta])
        tc = P.op("vector", lambda e: e.tensor_tensor(
            out=qrf3[:, :, 0:32], in0=t1, in1=t2, op=ALU.subtract),
            waits=[tb])
        td = P.op("vector", lambda e: e.tensor_tensor(out=t1, in0=x1, in1=s,
                                                      op=ALU.mult), waits=[tc])
        te = P.op("vector", lambda e: e.tensor_tensor(out=t2, in0=x2, in1=c,
                                                      op=ALU.mult), waits=[td])
        tf = P.op("vector", lambda e: e.tensor_tensor(
            out=qrf3[:, :, 32:64], in0=t1, in1=t2, op=ALU.add), waits=[te])
        tg = P.op("vector", lambda e: e.tensor_copy(
            out=qrf3[:, :, 64:128], in_=qev3[:, :, 64:128]), waits=[tf])
        th1 = P.op("scalar", lambda e: e.activation(
            out=P.f32(O_QT3, 1024), in_=P.f32(O_QRF, 1024), func=AF.Square),
            waits=[tg])
        th2 = P.op("scalar", lambda e: e.activation(
            out=P.f32(O_QT3 + 2048, 1024), in_=P.f32(O_QRF + 2048, 1024),
            func=AF.Square), waits=[tg, th1])
        ti1 = P.op("vector", lambda e: e.tensor_reduce(
            out=P.rmsS[:, 0:8], in_=P.f32(O_QT3, 1024)
            .rearrange("p (k q) -> p k q", q=128), axis=AX, op=ALU.add),
            waits=[th1, tg])
        ti2 = P.op("vector", lambda e: e.tensor_reduce(
            out=P.rmsS[:, 8:16], in_=P.f32(O_QT3 + 2048, 1024)
            .rearrange("p (k q) -> p k q", q=128), axis=AX, op=ALU.add),
            waits=[th2])
        tj = P.op("vector", lambda e: e.tensor_scalar(
            out=P.rmsS[:, 0:16], in0=P.rmsS[:, 0:16], scalar1=1.0 / HD,
            scalar2=RMSEPS, op0=ALU.mult, op1=ALU.add), waits=[ti1, ti2])
        tk = P.op("scalar", lambda e: e.activation(
            out=P.rmsS[:, 16:32], in_=P.rmsS[:, 0:16], func=AF.Sqrt),
            waits=[tj])
        tl = P.op("vector", lambda e: e.reciprocal(
            out=P.rmsS[:, 0:16], in_=P.rmsS[:, 16:32]), waits=[tk])
        tm = P.op("vector", lambda e: e.tensor_scalar_mul(
            out=P.rmsS[:, 0:16], in0=P.rmsS[:, 0:16], scalar1=SCALE),
            waits=[tl])
        rstdb = P.rmsS[:, 0:16].unsqueeze(2).broadcast_to([128, H, 128])
        qnwb3 = P.qnwb[:].unsqueeze(1).broadcast_to([128, H, 128])
        qt3 = P.f32(O_QT3, 2048).rearrange("p (k q) -> p k q", q=128)
        tn = P.op("vector", lambda e: e.tensor_tensor(
            out=qt3, in0=qrf3, in1=rstdb, op=ALU.mult), waits=[tm, th2])
        to = P.op("vector", lambda e: e.tensor_tensor(
            out=P.bf(O_QNB, 2048).rearrange("p (k q) -> p k q", q=128),
            in0=qt3, in1=qnwb3, op=ALU.mult), waits=[tn, self.t_qnb])
        return to

    def ph_d(self, r):
        P = self
        mk_free = list(self.prev["mask_free"])
        tv0 = self.tok("vector")
        psA_free = [tv0, tv0]
        st = {"psB_free": [tv0, tv0], "psC_free": [tv0, tv0],
              "anat_free": [self.prev["e_done"], self.prev["e_done"]],
              "cmT_rd": None, "crow_rd": None, "t_dn": [None, None],
              "att_ev": None, "t_last": None, "t_sc": {}, "t_cm": {},
              "crow_tok": {}, "p2sc": {}, "p2no": {}, "exp_tok": {}}

        def pass1_head(j, hh):
            jp = j % 2
            psA = P.P0[:, 0:640]
            P.op("tensor", lambda e, j=j, hh=hh, psA=psA: e.matmul(
                psA[:, 0:512], P.bf(O_QTN + (j * H + hh) * 128, 128),
                P.bf(O_KT + j * 128, 512), start=True, stop=True),
                waits=[self.T_QTN, self.T_KT, psA_free[0]], inc=False)
            t_s = P.op("tensor", lambda e, j=j, hh=hh, psA=psA: e.matmul(
                psA[:, 512:640], P.bf(O_QTN + (j * H + hh) * 128, 128),
                P.bf(O_KT + (j + 4) * 128, 128), start=True, stop=True))
            t_r = P.op("vector", lambda e, psA=psA: e.reduce_max(
                out=P.red1[:, 0:1], in_=psA, axis=AX, negate=True),
                waits=[t_s])
            psA_free[0] = t_r
            t_c = P.op("vector", lambda e, hh=hh, jp=jp: e.tensor_tensor(
                out=P.cmaxb[:, jp * H + hh: jp * H + hh + 1],
                in0=P.red1[:, 0:1], in1=P.nsinkb[:, hh:hh + 1],
                op=ALU.min),
                waits=[t_r] + ([st["cmT_rd"]] if hh == 0 else []))
            st["t_cm"][j] = P.op("vector", lambda e, hh=hh, jp=jp:
                                 e.tensor_copy(
                                     out=P.cmaxf[:, jp * H + hh:
                                                 jp * H + hh + 1],
                                     in_=P.cmaxb[:, jp * H + hh:
                                                 jp * H + hh + 1]),
                                 waits=[t_c])

        def crow(j):
            jp = j % 2
            t_t = P.op("tensor", lambda e, jp=jp: e.transpose(
                P.PT[0:16, 896:1024], P.cmaxb[:, jp * H:(jp + 1) * H],
                P.iden[:]),
                waits=[st["t_cm"][j], self.pt_free, st["crow_rd"]])
            t_te = P.op("scalar", lambda e: e.activation(
                out=P.AR[0:16, O_CMT:O_CMT + 128], in_=P.PT[0:16, 896:1024],
                func=AF.Copy),
                waits=[t_t] + (list(self.prev["store"])
                               if (r == 1 and j == 0) else []))
            st["cmT_rd"] = t_te
            self.pt_free = t_te
            st["crow_tok"][j] = P.op("scalar", lambda e: e.dma_start(
                out=P.AR[0:1, O_CRW:O_CRW + 16 * 640]
                .rearrange("p (c r q) -> p c r q", r=5, q=128),
                in_=P.AR[0:16, O_CMT:O_CMT + 128].unsqueeze(1).unsqueeze(1)
                .broadcast_to([16, 1, 5, 128]).squeeze(1)), dma="dCRW",
                waits=[t_te])

        def pass2_scores(j, hh):
            msl = j % 2
            t_cr = st["crow_tok"][j]
            t_mk = st["t_mk"][j]
            par = hh % 2
            psB = P.P1[:, 0:640] if par == 0 else P.P0[:, 1024:1664]
            t_sc = None
            for c in range(4):
                P.op("tensor", lambda e, j=j, hh=hh, c=c, psB=psB:
                     e.matmul(psB[:, c * 128:(c + 1) * 128],
                              P.bf(O_KT + (j + c) * 128, 128),
                              P.bf(O_QTN + (j * H + hh) * 128, 128),
                              start=(c == 0), stop=False),
                     waits=[t_cr, t_mk, st["psB_free"][par]]
                     if c == 0 else [], inc=False)
            P.op("tensor", lambda e, hh=hh, psB=psB: e.matmul(
                psB[:, 0:512], P.ones_r[:],
                P.AR[0:1, O_CRW + hh * 640: O_CRW + hh * 640 + 512],
                start=False, stop=False), inc=False)
            P.op("tensor", lambda e, msl=msl, psB=psB: e.matmul(
                psB[:, 0:512], P.iden[:],
                P.bf(O_MKS + msl * 640, 512),
                start=False, stop=True), inc=False)
            P.op("tensor", lambda e, j=j, hh=hh, psB=psB: e.matmul(
                psB[:, 512:640], P.bf(O_KT + (j + 4) * 128, 128),
                P.bf(O_QTN + (j * H + hh) * 128, 128),
                start=True, stop=False), inc=False)
            P.op("tensor", lambda e, hh=hh, psB=psB: e.matmul(
                psB[:, 512:640], P.ones_r[:],
                P.AR[0:1, O_CRW + hh * 640 + 512: O_CRW + hh * 640 + 640],
                start=False, stop=False), inc=False)
            t_sc = P.op("tensor", lambda e, msl=msl, psB=psB: e.matmul(
                psB[:, 512:640], P.iden[:],
                P.bf(O_MKS + msl * 640 + 512, 128),
                start=False, stop=True), inc=True)
            t_ex = P.op("scalar", lambda e, par=par, psB=psB:
                        e.activation(out=P.bf(O_PRB + par * 640, 640),
                                     in_=psB, func=AF.Exp),
                        waits=[t_sc] + (list(self.prev["store"])
                                        if (r == 1 and j == 0
                                            and hh < 2) else []))
            st["exp_tok"][hh] = t_ex
            st["p2sc"][j] = t_sc

        def pass2_attn(j, hh):
            jp = j % 2
            par = hh % 2
            psAt = (P.P1[:, 768:897] if par == 0 else P.P0[:, 1792:1921])
            psDen = (P.P1[:, 896:897] if par == 0 else P.P0[:, 1920:1921])
            t_ex = st["exp_tok"][hh]
            t_at = None
            for c in range(NCH):
                pch = P.bf(O_PRB + par * 640 + c * 128, 128)
                t_at = P.op("tensor", lambda e, pch=pch, psAt=psAt, j=j,
                            c=c: e.matmul(
                                psAt[:, 0:129], pch,
                                P.bf(O_VN + (j + c) * VST, 129),
                                start=(c == 0), stop=(c == NCH - 1)),
                            waits=[t_ex, self.T_V, self.t_vone,
                                   st["psC_free"][par]]
                            if c == 0 else [], inc=(c == NCH - 1))
            dso = par * 2
            t_es = P.op("scalar", lambda e, hh=hh, jp=jp, dso=dso:
                        e.activation(
                            out=P.dstat[:, dso:dso + 1],
                            in_=P.cmaxf[:, jp * H + hh: jp * H + hh + 1],
                            func=AF.Exp, bias=P.sinkb[:, hh:hh + 1],
                            scale=1.0),
                        waits=[self.t_sink, st["t_cm"][j],
                               st["t_dn"][par]])
            t_dn = P.op("vector", lambda e, psDen=psDen, dso=dso:
                        e.tensor_tensor(
                            out=P.dstat[:, dso + 1:dso + 2], in0=psDen,
                            in1=P.dstat[:, dso:dso + 1],
                            op=ALU.add), waits=[t_at, t_es])
            st["t_dn"][par] = t_dn
            t_iv = P.op("vector", lambda e, dso=dso: e.reciprocal(
                out=P.dstat[:, dso:dso + 1],
                in_=P.dstat[:, dso + 1:dso + 2]), waits=[t_dn])
            t_no = P.op("vector", lambda e, hh=hh, j=j, psAt=psAt, dso=dso:
                        e.tensor_scalar(
                            out=P.bf(O_ANAT + (j % 2) * 2048 + hh * 128,
                                     128),
                            in0=psAt[:, 0:128],
                            scalar1=P.dstat[:, dso:dso + 1],
                            scalar2=None, op0=ALU.mult),
                        waits=[t_iv] + ([st["anat_free"][j % 2]]
                                        if hh == 0 else []))
            st["psC_free"][par] = t_no
            st["psB_free"][par] = t_no
            st["p2no"][j] = t_no

        def pass2_tail(j):
            msl = j % 2
            t_no = st["p2no"][j]
            t_sc = st["p2sc"][j]
            for hb in range(2):
                st["t_last"] = self.pt_batch(
                    lambda c, hb=hb, j=j: P.bf(
                        O_ANAT + (j % 2) * 2048 + (hb * 8 + c) * 128, 128),
                    lambda hb=hb, j=j: P.bf(O_ATT, H * 1024)
                    .rearrange("p (k t q) -> p k t q", k=H, t=NT)
                    [:, hb * 8:(hb + 1) * 8, j, :],
                    [t_no, self.prev["att_free"] if (j == 0 and hb == 0)
                     else None])
            st["anat_free"][j % 2] = st["t_last"]
            mk_free[msl] = t_sc
            st["crow_rd"] = t_sc

        st["t_mk"] = {}
        for j in range(NT):
            msl = j % 2
            st["t_mk"][j] = P.op("sync", lambda e, r=r, j=j, msl=msl:
                                 e.dma_start(
                                     out=P.bf(O_MKS + msl * 640, 640)
                                     .rearrange("p (c q) -> p c q", q=128),
                                     in_=P.mask_in[r, j]
                                     .rearrange("c p q -> p c q")),
                                 dma=f"dMK{msl}",
                                 waits=[mk_free[msl], self.T_QTN] +
                                 (list(self.prev["store"])
                                  if (r == 1 and j < 2) else []))
            self.pump(12)
            for hh in range(H):
                pass1_head(j, hh)
                if j > 0:
                    pass2_scores(j - 1, hh)
                    if hh > 0:
                        pass2_attn(j - 1, hh - 1)
            if j > 0:
                pass2_attn(j - 1, H - 1)
                pass2_tail(j - 1)
            crow(j)
        for hh in range(H):
            pass2_scores(NT - 1, hh)
            if hh > 0:
                pass2_attn(NT - 1, hh - 1)
        pass2_attn(NT - 1, H - 1)
        pass2_tail(NT - 1)
        self.prev["mask_free"] = mk_free
        self.T_ATT = st["t_last"]
        self.prev["c_done"] = st["t_last"]

    def ph_e(self, r):
        P = self
        if self.dbg is not None:
            t1 = P.op("sync", lambda e, r=r: e.dma_start(
                out=self.dbg2[r], in_=P.bf(O_ATT, H * RND)), dma="dDA",
                waits=[self.T_ATT])
            t2 = P.op("sync", lambda e, r=r: e.dma_start(
                out=self.dbg4[r], in_=P.bf(O_KT, NSRC * 128)), dma="dDK",
                waits=[self.T_ATT])
            t3 = P.op("sync", lambda e, r=r: e.dma_start(
                out=self.dbg5[r].rearrange("p (i q) -> p i q", q=129),
                in_=P.bf(O_VN, NSRC * VST)
                .rearrange("p (i q) -> p i q", q=VST)[:, :, 0:129]),
                dma="dDV", waits=[self.T_ATT, t1, t2])
            self.st_final.extend([t1, t2, t3])
            self.dbg_kv = [t3, t3]
            self.T_ATT = t3
        if self.T_GAMMA is None:
            self.gamma_finalize()
        x_ld = [None] * NT
        for j in range(NT):
            x_ld[j] = P.op("sync", lambda e, j=j, r=r: e.dma_start(
                out=P.f32(O_X + j * 4096, 2048),
                in_=P.h_in[HALO + r * RND + j * 128:
                           HALO + r * RND + (j + 1) * 128, :]),
                dma=f"dX{j}",
                waits=[self.T_ATT if (j < 4 or j == 6) else
                       (self.T_QTN if j < 6 else self.T_GAMMA),
                       self.T_QTN if j == 7 else None,
                       self.prev["store"][j]])
        w2_free = [None, None, None]
        w1g_free = None
        e_add = {}
        y1ps_free = [self.T_ATT, self.T_ATT, self.T_ATT]
        t_y1_last = None

        pairs = [(g, oq) for g in range(4) for oq in range(4)]
        w2_free = [None, None, None, None]
        w2tok = {}

        def emit_w2(pi, extra=None):
            toks = []
            for hv in range(2):
                s3 = (pi * 2 + hv) % 4
                toks.append(P.op(
                    "gpsimd", lambda e, pi=pi, hv=hv, s3=s3: e.dma_start(
                        out=P.bf(O_W2QS[s3], 4096)
                        .rearrange("p (c q) -> p c q", q=512),
                        in_=P.ow2_in.rearrange("(c p) o -> c p o", p=128)
                        [pairs[pi][0] * 16 + hv * 8:
                         pairs[pi][0] * 16 + hv * 8 + 8, :,
                         pairs[pi][1] * 512:(pairs[pi][1] + 1) * 512]
                        .rearrange("c p o -> p c o")),
                    dma=f"dW2_{s3}",
                    waits=[w2_free[s3], extra,
                           self.T_ATT if (O_W2QS[s3] == 0 and
                                          w2_free[s3] is None) else None]))
            w2tok[pi] = toks

        emit_w2(0)
        for g in range(4):
            t_w1 = P.op("gpsimd", lambda e, g=g: e.dma_start(
                out=P.bf(O_W1G, 8192).rearrange("p (c q) -> p c q", q=2048),
                in_=P.ow1_in[g].rearrange("(c p) o -> p c o", p=128)),
                dma="dW1G",
                waits=[w1g_free, self.T_ATT if g == 0 else None])
            for oc in range(NKC):
                for th in range(2):
                    s3y = (2 * oc + th) % 3
                    pco = s3y * 512
                    t_m = None
                    for kc in range(4):
                        t_m = P.op("tensor", lambda e, g=g, oc=oc, th=th,
                                   kc=kc, pco=pco: e.matmul(
                                       P.P1[:, pco:pco + 512],
                                       P.bf(O_W1G + kc * 2048 + oc * 128,
                                            128),
                                       P.bf(O_ATT + (g * 4 + kc) * 1024
                                            + th * 512, 512),
                                       start=(kc == 0), stop=(kc == 3)),
                                   waits=[t_w1, y1ps_free[s3y],
                                          self.T_GAMMA if (g == 0 and oc == 0
                                                           and th == 0)
                                          else None,
                                          e_add.get(7) if (g > 0 and oc == 0
                                                           and th == 0)
                                          else None]
                                   if kc == 0 else [], inc=(kc == 3))
                    t_y1 = P.op("scalar", lambda e, oc=oc, th=th, pco=pco:
                                e.activation(
                                    out=P.bf(O_Y1 + oc * 1024 + th * 512,
                                             512),
                                    in_=P.P1[:, pco:pco + 512],
                                    func=AF.Copy), waits=[t_m])
                    y1ps_free[s3y] = t_y1
                    t_y1_last = t_y1
            w1g_free = t_y1_last
            T_Y1 = t_y1_last
            if g == 0:
                emit_w2(1, extra=T_Y1)
            for outq in range(4):
                pi = g * 4 + outq
                hv_t = w2tok[pi]
                for j in range(NT):
                    t_m = None
                    for kc2 in range(NKC):
                        hv = kc2 // 8
                        s3 = (pi * 2 + hv) % 4
                        w = []
                        if kc2 == 0:
                            w = [hv_t[0], T_Y1, x_ld[j],
                                 e_add.get((j - 4) % 8)]
                        elif kc2 == 8:
                            w = [hv_t[1]]
                        t_m = P.op("tensor", lambda e, j=j, kc2=kc2, s3=s3:
                                   e.matmul(
                                       P.P0[:, (j % 4) * 512:
                                            (j % 4) * 512 + 512],
                                       P.bf(O_Y1 + kc2 * 1024 + j * 128,
                                            128),
                                       P.bf(O_W2QS[s3] + (kc2 % 8) * 512,
                                            512),
                                       start=(kc2 == 0),
                                       stop=(kc2 == NKC - 1)),
                                   waits=w, inc=(kc2 == NKC - 1))
                    e_add[j] = P.op("vector", lambda e, j=j, outq=outq:
                                    e.tensor_tensor(
                                        out=P.f32(O_X + j * 4096
                                                  + outq * 1024, 512),
                                        in0=P.f32(O_X + j * 4096
                                                  + outq * 1024, 512),
                                        in1=P.P0[:, (j % 4) * 512:
                                                 (j % 4) * 512 + 512],
                                        op=ALU.add),
                                    waits=[t_m, e_add.get(j)])
                pe_now = self.tok("tensor")
                w2_free[(pi * 2) % 4] = pe_now
                w2_free[(pi * 2 + 1) % 4] = pe_now
                if pi + 2 < len(pairs):
                    emit_w2(pi + 2)
        self.e_add = e_add
        self.T_E = e_add[7]
        self.prev["e_done"] = self.T_E

    def ph_f(self, r):
        P = self
        self.dbg_toks = {}
        if self.dbg is not None:
            for j in range(NT):
                t = P.op("sync", lambda e, j=j, r=r: e.dma_start(
                    out=self.dbg[r * RND + j * 128:r * RND + (j + 1) * 128, :],
                    in_=P.f32(O_X + j * 4096, 2048)), dma=f"dDB{j}",
                    waits=[self.e_add[jj] for jj in range(NT)])
                self.st_final.append(t)
                self.dbg_toks[j] = t
        t_ln2 = None
        lnb2_free = None
        for j in range(NT):
            xv = P.f32(O_X + j * 4096, 2048)
            t_st = self.ln_stats(xv, self.e_add[j], O_LNB2,
                                 extra_act_w=[lnb2_free])
            lnt = P.bf(O_LNB2 + 2048, 2048)
            t_ln = P.op("vector", lambda e, xv=xv, lnt=lnt: e.tensor_scalar(
                out=lnt, in0=xv, scalar1=P.lns[:, 2:3],
                scalar2=P.lns[:, 6:7], op0=ALU.subtract, op1=ALU.mult),
                waits=[t_st, lnb2_free])
            for hb in range(2):
                t_ln2 = self.pt_batch(
                    lambda c, hb=hb, lnt=lnt: lnt[:, (hb * 8 + c) * 128:
                                                  (hb * 8 + c + 1) * 128],
                    lambda hb=hb, j=j: P.bf(O_LN2T, NKC * RND)
                    .rearrange("p (k t q) -> p k t q", k=NKC, t=NT)
                    [:, hb * 8:(hb + 1) * 8, j, :],
                    [t_ln, self.T_E if (j == 0 and hb == 0) else None])
            lnb2_free = t_ln2
        self.T_LN2 = t_ln2

    def ph_g(self, r):
        P = self
        nwin = (NW + SG - 1) // SG
        stg_free = [None, None, None]
        sq_free = [None, None, None]
        self._s1_rd = [None, None]
        self._s1b_rd = None
        self._s1_ping = 0
        self._qd_n = 0
        w1t_free = [None, None]
        w2t_free = [None, None]
        w3_free = [self.T_LN2] * 6
        stores = [None] * NT
        x1ps_rd = [None, None]
        x2ps_rd = None
        out_guard = [self.e_add[7]]
        hid_rd = [None] * SG
        out_ev = {}
        for j in range(NT):
            for dh in range(2):
                out_ev[(j, dh)] = self.dbg_toks.get(j, self.e_add[j])
        self.gst = getattr(self, "gst", 0)

        def quant_half(src_off, wsel, dst, t_ld, extra=()):
            if wsel == 1:
                ta = P.op("vector", lambda e: e.tensor_scalar(
                    out=P.bf(O_S1B, 1024), in0=P.f32(src_off, 1024),
                    scalar1=P.invg[:, 1:2], scalar2=0.5,
                    op0=ALU.mult, op1=ALU.is_ge),
                    waits=[t_ld, self.T_GAMMA] + list(extra) +
                    [self._s1b_rd])
                tb = P.op("vector", lambda e, dst=dst: e.tensor_scalar(
                    out=dst, in0=P.f32(src_off, 1024),
                    scalar1=P.invg[:, 3:4], scalar2=0.5,
                    op0=ALU.mult, op1=ALU.is_ge), waits=[ta])
                tc = P.op("vector", lambda e, dst=dst: e.tensor_tensor(
                    out=dst, in0=P.bf(O_S1B, 1024), in1=dst,
                    op=ALU.subtract), waits=[tb])
                self._s1b_rd = tc
                return tc, tb
            pp = self._s1_ping
            self._s1_ping ^= 1
            s1o = O_S1 if pp == 0 else O_S1C
            t1 = P.op("scalar", lambda e, s1o=s1o: e.activation(
                out=P.bf(s1o, 1024), in_=P.f32(src_off, 1024), func=AF.Sign,
                bias=P.b_m05[:], scale=P.invg[:, wsel:wsel + 1]),
                waits=[t_ld, self.T_GAMMA, self.t_consts] + list(extra) +
                [self._s1_rd[pp]])
            t2 = P.op("scalar", lambda e, dst=dst: e.activation(
                out=dst, in_=P.f32(src_off, 1024), func=AF.Sign,
                bias=P.b_p05[:], scale=P.invg[:, wsel:wsel + 1]), waits=[t1])
            t3 = P.op("vector", lambda e, dst=dst, s1o=s1o: e.tensor_tensor(
                out=dst, in0=P.bf(s1o, 1024), in1=dst,
                op=ALU.add), waits=[t2])
            self._s1_rd[pp] = t3
            return t3, t2


        def prep_w12(c, wsel, fw_in, dst_base, free_list, war):
            sl = c % 2
            dst = dst_base + sl * 2048
            lds, qts, qsls = [], [], []
            for hf in range(2):
                ssl = self.gst % 3
                self.gst += 1
                t_ld = P.op("sync", lambda e, c=c, hf=hf, ssl=ssl,
                            fw_in=fw_in: e.dma_start(
                                out=P.f32(FSTO[ssl], 1024),
                                in_=fw_in[c * 128:(c + 1) * 128,
                                          hf * 1024:(hf + 1) * 1024]),
                            dma=f"dF{ssl}", waits=[stg_free[ssl], war])
                lds.append((t_ld, ssl))
            for hf in range(2):
                t_ld, ssl = lds[hf]
                qsl = self._qd_n % NQD
                self._qd_n += 1
                qsls.append(qsl)
                t_q, t_sf = quant_half(FSTO[ssl], wsel,
                                       P.bf(QDO[qsl], 1024), t_ld,
                                       extra=[sq_free[qsl]])
                stg_free[ssl] = t_sf
                qts.append(t_q)
            toks = []
            for hf in range(2):
                qsl = qsls[hf]
                t_tr = P.op("scalar", lambda e, dst=dst, hf=hf, qsl=qsl:
                            e.dma_start(
                                out=P.bf(dst + hf * 1024, 1024)
                                .rearrange("p (cc q) -> p cc q", q=128),
                                in_=P.bf(QDO[qsl], 1024)
                                .rearrange("p (cc q) -> p cc q", q=128),
                                transpose=True),
                            dma=f"dT{qsl}",
                            waits=[qts[hf], free_list[sl] if hf == 0
                                   else None])
                sq_free[qsl] = t_tr
                toks.append(t_tr)
            return dst, toks

        def prep_w3(c, war):
            sl = c % 6
            dst = O_W3A + sl * 2048
            lds, qts, qsls = [], [], []
            for hf in range(2):
                ssl = self.gst % 3
                self.gst += 1
                t_ld = P.op("sync", lambda e, c=c, hf=hf, ssl=ssl:
                            e.dma_start(
                                out=P.f32(FSTO[ssl], 1024)
                                .rearrange("p (m q) -> p m q", q=128),
                                in_=P.fw3_in
                                .rearrange("(m p) (cc q) -> p m cc q",
                                           p=128, q=128)
                                [:, hf * 8:(hf + 1) * 8, c, :]),
                            dma=f"dF{ssl}", waits=[stg_free[ssl], war])
                lds.append((t_ld, ssl))
            for hf in range(2):
                t_ld, ssl = lds[hf]
                qsl = self._qd_n % NQD
                self._qd_n += 1
                qsls.append(qsl)
                t_q, t_sf = quant_half(FSTO[ssl], 2,
                                       P.bf(QDO[qsl], 1024), t_ld,
                                       extra=[sq_free[qsl]])
                stg_free[ssl] = t_sf
                qts.append(t_q)
            toks = []
            for hf in range(2):
                qsl = qsls[hf]
                t_tr = P.op("scalar", lambda e, dst=dst, hf=hf, qsl=qsl:
                            e.dma_start(
                                out=P.bf(dst + hf * 1024, 1024)
                                .rearrange("p (m q) -> p m q", q=128),
                                in_=P.bf(QDO[qsl], 1024)
                                .rearrange("p (m q) -> p m q", q=128),
                                transpose=True),
                            dma=f"dT{qsl}",
                            waits=[qts[hf], w3_free[sl] if hf == 0
                                   else None])
                sq_free[qsl] = t_tr
                toks.append(t_tr)
            return dst, toks

        cglob = 0
        preps = {}

        def prep(c):
            war0 = self.T_E if c == 0 else None
            d1, tk1 = prep_w12(c, 0, P.fw1_in, O_W1T, w1t_free, war0)
            d2, tk2 = prep_w12(c, 1, P.fw2_in, O_W2T, w2t_free, war0)
            d3, tk3 = prep_w3(c, self.T_LN2 if c == 0 else None)
            preps[c] = (d1, tk1, d2, tk2, d3, tk3)

        prep(0)
        for win in range(nwin):
            ncw = min(SG, NW - win * SG)
            hid_t = [None] * ncw
            w3d = [None] * ncw
            w3tk = [None] * ncw
            for cw in range(ncw):
                c = win * SG + cw
                d1, tk1, d2, tk2, w3d[cw], tk3 = preps.pop(c)
                w3tk[cw] = tk3
                cpar = c % 2
                x1ps = P.P1[:, 0:1024] if cpar == 0 else P.P0[:, 0:1024]
                t_x1 = None
                for kc in range(NKC):
                    for nn in range(2):
                        t_x1 = P.op("tensor", lambda e, d1=d1, kc=kc, nn=nn,
                                    x1ps=x1ps: e.matmul(
                                        x1ps[:, nn * 512:(nn + 1) * 512],
                                        P.bf(d1 + kc * 128, 128),
                                        P.bf(O_LN2T + kc * 1024 + nn * 512,
                                             512),
                                        start=(kc == 0),
                                        stop=(kc == NKC - 1)),
                                    waits=[tk1[0], tk1[1], self.T_LN2,
                                           x1ps_rd[cpar], self.T_E,
                                           out_guard[0] if cpar else None]
                                    if (kc == 0 and nn == 0) else [],
                                    inc=(kc == NKC - 1 and nn == 1))
                w1t_free[c % 2] = t_x1
                if c + 1 < NW:
                    prep(c + 1)
                t_sg = P.op("scalar", lambda e, x1ps=x1ps: e.activation(
                    out=P.f32(O_SGM, 1024), in_=x1ps,
                    func=AF.Sigmoid, scale=0.5),
                    waits=[t_x1, hid_rd[cw], self.T_LN2, x2ps_rd])
                x1ps_rd[cpar] = t_sg
                t_hd = None
                for nn in range(2):
                    t_x2 = None
                    for kc in range(NKC):
                        t_x2 = P.op("tensor", lambda e, d2=d2, kc=kc, nn=nn:
                                    e.matmul(
                                        P.P1[:, 1024:1536],
                                        P.bf(d2 + kc * 128, 128),
                                        P.bf(O_LN2T + kc * 1024 + nn * 512,
                                             512),
                                        start=(kc == 0),
                                        stop=(kc == NKC - 1)),
                                    waits=[tk2[0], tk2[1], x2ps_rd]
                                    if kc == 0 else [],
                                    inc=(kc == NKC - 1))
                    t_hd = P.op("vector", lambda e, cw=cw, nn=nn:
                                e.tensor_tensor(
                                    out=P.bf(O_HID + cw * 1024 + nn * 512,
                                             512),
                                    in0=P.f32(O_SGM + nn * 1024, 512),
                                    in1=P.P1[:, 1024:1536], op=ALU.mult),
                                waits=[t_x2, t_sg] + ([hid_rd[cw]]
                                                      if nn == 0 else []))
                    x2ps_rd = t_hd
                hid_t[cw] = t_hd
                w2t_free[c % 2] = t_hd
                cglob += 1
            for j in range(NT):
                for dh in range(2):
                    pco = dh * 1024
                    t_m = None
                    for half in range(2):
                        for cw in range(ncw):
                            t_m = P.op(
                                "tensor", lambda e, j=j, cw=cw, w3d=w3d,
                                pco=pco, dh=dh, half=half, ncw=ncw: e.matmul(
                                    P.P0[:, pco + half * 512:
                                         pco + half * 512 + 512],
                                    P.bf(O_HID + cw * 1024 + j * 128, 128),
                                    P.bf(w3d[cw] + dh * 1024 + half * 512,
                                         512),
                                    start=(cw == 0), stop=(cw == ncw - 1)),
                                waits=[hid_t[cw], w3tk[cw][0], w3tk[cw][1],
                                       out_ev[(j, dh)],
                                       out_ev[((j - 1) % NT, dh)]]
                                if (cw == 0 and half == 0) else
                                ([hid_t[cw], w3tk[cw][0], w3tk[cw][1]]
                                 if cw == 0 else [w3tk[cw][0], w3tk[cw][1]]),
                                inc=(cw == ncw - 1 and half == 1))
                    t_a = P.op("vector", lambda e, j=j, dh=dh, pco=pco:
                               e.scalar_tensor_tensor(
                                   out=P.f32(O_X + j * 4096 + dh * 2048,
                                             1024),
                                   in0=P.P0[:, pco:pco + 1024], scalar=0.5,
                                   in1=P.f32(O_X + j * 4096 + dh * 2048,
                                             1024),
                                   op0=ALU.mult, op1=ALU.add),
                               waits=[t_m, out_ev[(j, dh)]])
                    out_ev[(j, dh)] = t_a
            pe_now = self.tok("tensor")
            for cw in range(ncw):
                hid_rd[cw] = pe_now
            for sl in range(6):
                w3_free[sl] = pe_now
            out_guard[0] = out_ev[(NT - 1, 0)]
            if win == nwin - 1:
                for j in range(NT):
                    stores[j] = P.op("sync", lambda e, j=j, r=r:
                                     e.dma_start(
                                         out=P.out_hbm[r * RND + j * 128:
                                                       r * RND
                                                       + (j + 1) * 128, :],
                                         in_=P.f32(O_X + j * 4096, 2048)),
                                     dma=f"dST{j}",
                                     waits=[out_ev[(j, 0)], out_ev[(j, 1)]])
        self.prev["store"] = stores
        self.st_final.extend(stores)
        pe_end = self.tok("tensor")
        self.prev["x_free"] = stores[-1]
        self.prev["att_free"] = pe_end
        self.prev["cos_free"] = pe_end
        self.prev["ln1_free"] = pe_end
        self.prev["qtn_free"] = pe_end
        self.prev["wqu_free"] = pe_end
        self.prev["wkv_free"] = pe_end

    def round(self, r):
        self.ph_a(r)
        self.ph_b(r)
        self.ph_c(r)
        self.ph_d(r)
        self.ph_e(r)
        self.ph_f(r)
        self.ph_g(r)


def _host_masks(quarter0):
    i = np.arange(128)
    m0 = np.where(i[None, :] < i[:, None], 0.0, NEG).astype(np.float32)
    m4 = np.where(i[None, :] >= i[:, None], 0.0, NEG).astype(np.float32)
    mz = np.zeros((128, 128), np.float32)
    mf = np.full((128, 128), NEG, np.float32)
    out = np.zeros((2, NT, NCH, 128, 128), np.float32)
    for r in range(2):
        for j in range(NT):
            tg = r * NT + j
            for c in range(NCH):
                if quarter0 and tg - 4 + c < 0:
                    out[r, j, c] = mf
                elif c == 0:
                    out[r, j, c] = m0
                elif c == NCH - 1:
                    out[r, j, c] = m4
                else:
                    out[r, j, c] = mz
    return out.astype(BFD)


def _host_costab(qr):
    inv = 10000.0 ** (-(np.arange(0, 64, 2, dtype=np.float64)) / 64.0)
    cost = np.zeros((2, NSRC, 128, 32), np.float32)
    sint = np.zeros((2, NSRC, 128, 32), np.float32)
    for r in range(2):
        for i in range(NSRC):
            p0 = qr * RT + r * RND + (i - 4) * 128
            pos = p0 + np.arange(128)
            valid = pos >= 0
            ang = pos[:, None].astype(np.float64) * inv[None, :]
            cost[r, i] = np.where(valid[:, None], np.cos(ang), 0.0)
            sint[r, i] = np.where(valid[:, None], np.sin(ang), 0.0)
    return cost, sint


_NC_CACHE = {}


def kernel(L0_attn_h, L0_attn_wq_comp, L0_attn_wq_up, L0_attn_wk, L0_attn_wv,
           L0_attn_q_norm_w, L0_attn_k_norm_w, L0_attn_sink_logit,
           L0_attn_out_w1, L0_attn_out_w2, L0_ffn_w1, L0_ffn_w2, L0_ffn_w3):
    asf = lambda a: np.ascontiguousarray(np.asarray(a, np.float32))
    h = asf(L0_attn_h)
    wqc, wqu, wk, wv = map(asf, (L0_attn_wq_comp, L0_attn_wq_up, L0_attn_wk,
                                 L0_attn_wv))
    qnw, knw, sink = map(asf, (L0_attn_q_norm_w, L0_attn_k_norm_w,
                               L0_attn_sink_logit))
    ow1, ow2 = asf(L0_attn_out_w1), asf(L0_attn_out_w2)
    fw1, fw2, fw3 = map(asf, (L0_ffn_w1, L0_ffn_w2, L0_ffn_w3))

    iden = np.eye(128, dtype=BFD)
    masks_g = _host_masks(False)
    masks_q0 = _host_masks(True)
    in_maps = []
    for c in range(N_CORE):
        b, qr = c // 4, c % 4
        if qr == 0:
            hs = np.concatenate([np.zeros((HALO, D), np.float32),
                                 h[b, 0:RT, :]], axis=0)
        else:
            hs = np.ascontiguousarray(h[b, qr * RT - HALO:(qr + 1) * RT, :])
        cost, sint = _host_costab(qr)
        in_maps.append({
            "h": hs, "wq_comp": wqc, "wq_up": wqu, "wk": wk, "wv": wv,
            "q_norm_w": qnw, "k_norm_w": knw, "sink": sink,
            "out_w1": ow1, "out_w2": ow2,
            "ffn_w1": fw1, "ffn_w2": fw2, "ffn_w3": fw3,
            "costab": cost, "sintab": sint, "iden": iden,
            "masks": masks_q0 if qr == 0 else masks_g,
        })

    nc = _NC_CACHE.get("nc")
    if nc is None:
        nc = Builder().build()
        _NC_CACHE["nc"] = nc
    res = run_bass_kernel_spmd(nc, in_maps, core_ids=list(range(N_CORE)))
    out = np.empty((B, T_FULL, D), np.float32)
    for c in range(N_CORE):
        b, qr = c // 4, c % 4
        out[b, qr * RT:(qr + 1) * RT, :] = res.results[c]["out"]
    return out


if __name__ == "__main__":
    Builder().build()
    print("build ok")
